# version 03
# baseline (speedup 1.0000x reference)
import contextlib
import math

import numpy as np
import ml_dtypes

import concourse.bass as bass
import concourse.mybir as mybir
from concourse.bass_utils import run_bass_kernel_spmd

F32 = mybir.dt.float32
BF16 = mybir.dt.bfloat16
AF = mybir.ActivationFunctionType
ALU = mybir.AluOpType
AX = mybir.AxisListType

NCORES = 8
T = 4096
D = 1024
PT = 128
KD = D // PT

BF = ml_dtypes.bfloat16


ENG_SECTION = {"t": "tensor", "v": "vector", "s": "scalar", "g": "gpsimd", "y": "sync"}
ENGS = tuple(ENG_SECTION)


SEMSEG = 16384


class Prog:
    def __init__(self, nc):
        self.nc = nc
        self.q = {e: [] for e in ENGS}
        self.cnt = {e: 0 for e in ENGS}
        self.waited = {}
        self.wmap = {}
        self.rmap = {}
        self.sems = {}
        self._qtok = {}
        self.nuniq = 0
        self.drams = {}

    def uname(self, name):
        self.nuniq += 1
        return f"{name}_u{self.nuniq}"

    def gd(self, name, shape, dtype, kind="ExternalInput"):
        if name not in self.drams:
            self.drams[name] = self.nc.dram_tensor(name, list(shape), dtype,
                                                   kind=kind).ap()
        return self.drams[name]

    def _wait(self, engobj, te, tv):
        seg = (tv - 1) // SEMSEG
        engobj.wait_ge(self.sems[te][seg], tv - seg * SEMSEG)

    def _key(self, h):
        return id(h)

    def _deps(self, e, ins, outs):
        need = {}

        def add(te, tv):
            if te == e and e == "t":
                return
            if need.get(te, -1) < tv:
                need[te] = tv

        for h in ins:
            k = self._key(h)
            if k in self.wmap:
                add(*self.wmap[k])
        for h in outs:
            k = self._key(h)
            if k in self.wmap:
                add(*self.wmap[k])
            for te, tv in self.rmap.get(k, {}).items():
                add(te, tv)
        waits = []
        for te, tv in sorted(need.items()):
            if tv > self.waited.get((e, te), 0):
                waits.append((te, tv))
                self.waited[(e, te)] = tv
        return waits

    def op(self, e, fn, ins=(), outs=(), inc=1):
        waits = self._deps(e, ins, outs)
        self.cnt[e] += inc
        myval = self.cnt[e]

        def emit(engobj, waits=waits, fn=fn, e=e, inc=inc, myval=myval):
            for te, tv in waits:
                self._wait(engobj, te, tv)
            instr = fn(engobj)
            seg = (myval - 1) // SEMSEG
            instr.then_inc(self.sems[e][seg], inc)

        self.q[e].append(emit)
        for h in outs:
            k = self._key(h)
            self.wmap[k] = (e, myval)
            self.rmap[k] = {}
        for h in ins:
            self.rmap.setdefault(self._key(h), {})[e] = myval
        return myval

    def dma(self, e, out_ap, in_ap, ins=(), outs=(), slow=False):
        tok = self._qtok.setdefault(e, object())
        return self.op(e, lambda eng: eng.dma_start(
            out=out_ap, in_=in_ap, allow_slow_non_contiguous=slow),
            ins=ins, outs=list(outs) + [tok], inc=16)

    def barrier(self):
        snap = dict(self.cnt)
        for e in ENGS:
            waits = []
            for te, tv in sorted(snap.items()):
                if te != e and tv > 0 and tv > self.waited.get((e, te), 0):
                    waits.append((te, tv))
                    self.waited[(e, te)] = tv
            if waits:
                def emit(engobj, waits=waits):
                    for te, tv in waits:
                        self._wait(engobj, te, tv)
                self.q[e].append(emit)

    def fence_outputs(self):
        snap = dict(self.cnt)

        def emit(engobj, snap=snap):
            for te, tv in snap.items():
                nseg = (tv - 1) // SEMSEG + 1 if tv > 0 else 0
                for seg in range(nseg):
                    top = min(tv, (seg + 1) * SEMSEG)
                    self._wait(engobj, te, top)

        self.q["y"].append(emit)

    def emit(self):
        nc = self.nc
        with contextlib.ExitStack() as st:
            for e in ENGS:
                nseg = self.cnt[e] // SEMSEG + 1
                self.sems[e] = [st.enter_context(nc.semaphore(f"sem_{e}{i}"))
                                for i in range(nseg)]
            blk = st.enter_context(nc.Block())
            for e in ENGS:
                fns = self.q[e]
                if not fns:
                    continue
                section = getattr(blk, ENG_SECTION[e])

                def body(engobj, fns=fns):
                    for f in fns:
                        f(engobj)

                section(body)



def sb(P, st, name, shape, dtype):
    h = st.enter_context(P.nc.sbuf_tensor(P.uname(name), list(shape), dtype))
    return h


def ps(P, st, name, shape, dtype=F32):
    h = st.enter_context(P.nc.psum_tensor(P.uname(name), list(shape), dtype))
    return h


class CPool:

    def __init__(self, P, st, name="cpool", n=24):
        self.P = P
        self.tile = sb(P, st, name, [128, n], F32)
        self.map = {}
        self.n = n

    def col(self, val):
        val = float(val)
        if val not in self.map:
            i = len(self.map)
            assert i < self.n, "CPool full"
            self.map[val] = i
            self.P.op("v", lambda e, i=i, val=val: e.memset(self.tile[:, i:i + 1], val),
                      outs=[self.tile])
        return self.tile[:, self.map[val]:self.map[val] + 1]


def ln_tile(P, cp, x_t, h_t, tmp_col, tmp_sq, eps=1e-5, n=D):
    mean = tmp_col[:, 0:1]
    var = tmp_col[:, 1:2]
    rstd = tmp_col[:, 2:3]
    P.op("v", lambda e: e.reduce_sum(mean, x_t[:, :], axis=AX.X), ins=[x_t], outs=[tmp_col])
    P.op("v", lambda e: e.tensor_scalar_mul(mean, mean, 1.0 / n), ins=[tmp_col], outs=[tmp_col])
    P.op("v", lambda e: e.tensor_scalar(h_t[:, :], x_t[:, :], mean, None, ALU.subtract),
         ins=[x_t, tmp_col], outs=[h_t])
    P.op("s", lambda e: e.activation(tmp_sq[:, :], h_t[:, :], AF.Square),
         ins=[h_t], outs=[tmp_sq])
    P.op("v", lambda e: e.reduce_sum(var, tmp_sq[:, :], axis=AX.X), ins=[tmp_sq], outs=[tmp_col])
    c_eps, c_invn, c_half = cp.col(eps), cp.col(1.0 / n), cp.col(-0.5)
    P.op("s", lambda e: e.activation(var, var, AF.Ln, bias=c_eps, scale=c_invn),
         ins=[tmp_col, cp.tile], outs=[tmp_col])
    P.op("s", lambda e: e.activation(rstd, var, AF.Exp, scale=c_half),
         ins=[tmp_col, cp.tile], outs=[tmp_col])
    P.op("v", lambda e: e.tensor_scalar_mul(h_t[:, :], h_t[:, :], rstd),
         ins=[h_t, tmp_col], outs=[h_t])


def softplus_tile(P, cp, out, in_, tmp1, tmp2):
    P.op("s", lambda e: e.activation(tmp1[:, :], in_[:, :], AF.Abs),
         ins=[in_], outs=[tmp1])
    c_m1, c_1 = cp.col(-1.0), cp.col(1.0)
    P.op("s", lambda e: e.activation(tmp1[:, :], tmp1[:, :], AF.Exp, scale=c_m1),
         ins=[tmp1, cp.tile], outs=[tmp1])
    P.op("s", lambda e: e.activation(tmp1[:, :], tmp1[:, :], AF.Ln, bias=c_1),
         ins=[tmp1, cp.tile], outs=[tmp1])
    P.op("v", lambda e: e.tensor_scalar_max(tmp2[:, :], in_[:, :], 0.0),
         ins=[in_], outs=[tmp2])
    P.op("v", lambda e: e.tensor_add(out[:, :], tmp1[:, :], tmp2[:, :]),
         ins=[tmp1, tmp2], outs=[out])




def load_tiles(P, st, name, dram_ap, rows, cols, dtype, q="y"):
    nt = rows // PT
    tiles = []
    for i in range(nt):
        t = sb(P, st, f"{name}{i}", [PT, cols], dtype)
        P.dma(q, t[:, :], dram_ap[i * PT:(i + 1) * PT, :], outs=[t])
        tiles.append(t)
    return tiles


def load_cols(P, st, name, dram_1d, n, q="y"):
    t = sb(P, st, name, [PT, n], F32)
    v = dram_1d.rearrange("(a p) -> a p", p=PT)
    for j in range(n):
        P.dma(q, t[:, j:j + 1], v[j:j + 1].rearrange("a p -> p a"), outs=[t])
    return t


def bcast_row(P, st, name, row_ap, n, ones_f, psum_f, row_h=None):
    stage = sb(P, st, name + "_r", [1, n], F32)
    out = sb(P, st, name, [PT, n], F32)
    if row_h is None:
        P.dma("y", stage[:, :], row_ap, outs=[stage])
        src = stage
    else:
        src = row_ap
    srch = stage if row_h is None else row_h
    for j in range(0, n, 512):
        w = min(512, n - j)
        P.op("t", lambda e, j=j, w=w: e.matmul(psum_f[:, 0:w], ones_f[0:1, :],
                                               src[:, j:j + w], start=True, stop=True),
             ins=[ones_f, srch], outs=[psum_f])
        P.op("v", lambda e, j=j, w=w: e.tensor_copy(out[:, j:j + w], psum_f[:, 0:w]),
             ins=[psum_f], outs=[out])
    return out


class Common:

    def __init__(self, P, st):
        nc = P.nc
        self.cp = CPool(P, st)
        self.ident = sb(P, st, "c_ident_sb", [PT, PT], BF16)
        self.ones_f = sb(P, st, "c_ones_sb", [PT, PT], F32)
        self.ones_bf = sb(P, st, "c_onesbf_sb", [PT, PT], BF16)
        i_d = P.gd("c_ident", [PT, PT], BF16)
        P.dma("y", self.ident[:, :], i_d[:, :], outs=[self.ident])
        P.op("v", lambda e: e.memset(self.ones_f[:, :], 1.0), outs=[self.ones_f])
        P.op("v", lambda e: e.memset(self.ones_bf[:, :], 1.0), outs=[self.ones_bf])


def make_hT(P, st, cm, x_d, sink_dram=None, sink_tiles=None, mode="plain",
            gw_bc=None, gb_bc=None, tag="", router=None):
    cp = cm.cp
    ntt = T // PT
    st = contextlib.ExitStack()
    xt = sb(P, st, f"ht_x{tag}", [PT, D], F32)
    hf = sb(P, st, f"ht_h{tag}", [PT, D], F32)
    h2 = sb(P, st, f"ht_h2{tag}", [PT, D], F32) if mode == "double_affine" else None
    hbf = sb(P, st, f"ht_hbf{tag}", [PT, D], BF16)
    tsq = xt
    tcol = sb(P, st, f"ht_tcol{tag}", [PT, 8], F32)
    stage = sb(P, st, f"ht_stage{tag}", [PT, PT], BF16)
    ptr = [ps(P, st, f"ht_ps{tag}{i}", [PT, PT], BF16) for i in range(2)]
    for tt in range(ntt):
        P.dma("y", xt[:, :], x_d[tt * PT:(tt + 1) * PT, :], outs=[xt])
        ln_tile(P, cp, xt, hf, tcol, tsq)
        if mode == "double_affine":
            ln_tile(P, cp, hf, h2, tcol, tsq)
            P.op("v", lambda e: e.tensor_mul(h2[:, :], h2[:, :], gw_bc[:, :]),
                 ins=[h2, gw_bc], outs=[h2])
            P.op("v", lambda e: e.tensor_add(h2[:, :], h2[:, :], gb_bc[:, :]),
                 ins=[h2, gb_bc], outs=[h2])
            src = h2
        else:
            src = hf
        P.op("v", lambda e, src=src: e.tensor_copy(hbf[:, :], src[:, :]), ins=[src], outs=[hbf])
        for dp in range(KD):
            pp = ptr[dp % 2]
            P.op("t", lambda e, dp=dp, pp=pp: e.transpose(
                pp[:, :], hbf[:, dp * PT:(dp + 1) * PT], cm.ident[:, :]),
                ins=[hbf, cm.ident], outs=[pp])
            if sink_tiles is not None:
                dst = sink_tiles[dp]
                P.op("v", lambda e, dp=dp, pp=pp, dst=dst, tt=tt: e.tensor_copy(
                    dst[:, tt * PT:(tt + 1) * PT], pp[:, :]), ins=[pp], outs=[dst])
            else:
                P.op("v", lambda e, pp=pp: e.tensor_copy(stage[:, :], pp[:, :]),
                     ins=[pp], outs=[stage])
                P.dma("g", sink_dram[tt, dp * PT:(dp + 1) * PT, :], stage[:, :], ins=[stage])
                if router is not None:
                    wg_tiles, w_out, ps_r = router
                    P.op("t", lambda e, dp=dp, wgt=wg_tiles[dp], ps_r=ps_r: e.matmul(
                        ps_r[:, :], stage[:, :], wgt[:, :],
                        start=(dp == 0), stop=(dp == KD - 1)),
                        ins=[stage, wg_tiles[dp]], outs=[ps_r])
        if router is not None:
            _top2(P, cp, router[2], router[1], tt)
    P.barrier()
    st.close()


def _top2(P, cp, ps_r, w_out, tt):
    st_ = P._top2_scratch
    r, m1, m2, r2, eq1, eq2, e1, den = st_
    P.op("v", lambda e: e.tensor_copy(r[:, :], ps_r[:, 0:8]), ins=[ps_r], outs=[r])
    P.op("v", lambda e: e.reduce_max(m1[:, 0:1], r[:, :], axis=AX.X), ins=[r], outs=[m1])
    P.op("v", lambda e: e.tensor_scalar(eq1[:, :], r[:, :], m1[:, 0:1], None, ALU.is_equal),
         ins=[r, m1], outs=[eq1])
    P.op("v", lambda e: e.scalar_tensor_tensor(r2[:, :], eq1[:, :], -1e30, r[:, :],
                                               ALU.mult, ALU.add), ins=[eq1, r], outs=[r2])
    P.op("v", lambda e: e.reduce_max(m2[:, 0:1], r2[:, :], axis=AX.X), ins=[r2], outs=[m2])
    P.op("v", lambda e: e.tensor_scalar(eq2[:, :], r2[:, :], m2[:, 0:1], None, ALU.is_equal),
         ins=[r2, m2], outs=[eq2])
    negm1 = m1[:, 1:2]
    P.op("v", lambda e: e.tensor_scalar_mul(negm1, m1[:, 0:1], -1.0), ins=[m1], outs=[m1])
    P.op("s", lambda e: e.activation(e1[:, :], r[:, :], AF.Exp, bias=negm1),
         ins=[r, m1], outs=[e1])
    P.op("v", lambda e: e.tensor_scalar(den[:, 0:1], m2[:, 0:1], m1[:, 0:1], None, ALU.subtract),
         ins=[m2, m1], outs=[den])
    P.op("s", lambda e: e.activation(den[:, 0:1], den[:, 0:1], AF.Exp), ins=[den], outs=[den])
    P.op("v", lambda e: e.tensor_scalar_add(den[:, 0:1], den[:, 0:1], 1.0), ins=[den], outs=[den])
    P.op("v", lambda e: e.reciprocal(den[:, 1:2], den[:, 0:1]), ins=[den], outs=[den])
    P.op("v", lambda e: e.tensor_add(eq1[:, :], eq1[:, :], eq2[:, :]), ins=[eq1, eq2], outs=[eq1])
    P.op("v", lambda e: e.tensor_mul(e1[:, :], e1[:, :], eq1[:, :]), ins=[e1, eq1], outs=[e1])
    P.op("v", lambda e, tt=tt: e.tensor_scalar(w_out[:, tt * 8:tt * 8 + 8], e1[:, :],
                                               den[:, 1:2], None, ALU.mult),
         ins=[e1, den], outs=[w_out])


CAST_SPECS = [
    ("wot", 1024, 1024), ("wg", 1024, 8),
    ("w1", 8192, 1408), ("w2", 8192, 1408), ("w3", 11264, 1024),
    ("ws1", 1024, 2816), ("ws2", 1024, 2816), ("ws3", 2816, 1024),
    ("wint", 1024, 2048), ("wdtt", 1024, 1024), ("wxt", 1024, 32),
    ("woutt", 1024, 1024),
    ("wqkv", 1024, 3072), ("wout1a", 1024, 1024),
    ("qwt", 1024, 1024), ("kwt", 1024, 1024), ("vwt", 1024, 1024),
    ("owt", 1024, 1024),
    ("qwt2", 1024, 1024), ("kwt2", 1024, 1024), ("vwt2", 1024, 1024),
    ("owt2", 1024, 1024),
]




def _ctx(ctx):
    if ctx is None:
        nc = bass.Bass("TRN2", target_bir_lowering=False)
        P = Prog(nc)
        x_d = nc.dram_tensor("x", [T, D], F32, kind="ExternalInput").ap()
        xo_d = nc.dram_tensor("xo", [T, D], F32, kind="ExternalOutput").ap()
        return nc, P, x_d, xo_d, "", True
    nc, P, x_d, xo_d, uid = ctx
    return nc, P, x_d, xo_d, uid, False

def build_prep(ctx=None):
    nc, P, _, _, uid, standalone = _ctx(ctx)
    okind = "ExternalOutput" if standalone else "Internal"
    with contextlib.ExitStack() as st:
        bufs = [sb(P, st, f"cb{i}", [PT, 3072], F32) for i in range(2)]
        obufs = [sb(P, st, f"co{i}", [PT, 3072], BF16) for i in range(2)]
        for name, rows, cols in CAST_SPECS:
            i_ap = P.gd("p_" + name, [rows, cols], F32)
            o_ap = P.gd("t_" + name, [rows, cols], BF16, kind=okind)
            for i in range(rows // PT):
                b, o = bufs[i % 2], obufs[i % 2]
                P.dma("y", b[:, 0:cols], i_ap[i * PT:(i + 1) * PT, :], outs=[b])
                P.op("v", lambda e, b=b, o=o, cols=cols: e.tensor_copy(o[:, 0:cols], b[:, 0:cols]),
                     ins=[b], outs=[o])
                P.dma("g", o_ap[i * PT:(i + 1) * PT, :], o[:, 0:cols], ins=[o])
        NH, HD, RK = 16, 64, 32
        v_d = P.gd("p_v", [HD, RK], F32)
        ut_d = P.gd("p_u_t", [RK, HD], F32)
        ut_o = P.gd("t_ut", [RK, HD], BF16, kind=okind)
        v_sb = sb(P, st, "v_sb", [PT, RK], F32)
        vs_sb = sb(P, st, "vs_sb", [PT, RK], F32)
        ut_sb = sb(P, st, "ut_sbf", [RK, HD], F32)
        ut_bf = sb(P, st, "ut_bf", [RK, HD], BF16)
        P.dma("y", v_sb[0:HD, :], v_d[:, :], outs=[v_sb])
        P.dma("y", v_sb[HD:PT, :], v_d[:, :], outs=[v_sb])
        P.dma("y", ut_sb[:, :], ut_d[:, :], outs=[ut_sb])
        P.op("v", lambda e: e.tensor_scalar_mul(vs_sb[:, :], v_sb[:, :], 1.0 / math.sqrt(RK)),
             ins=[v_sb], outs=[vs_sb])
        P.op("v", lambda e: e.tensor_copy(ut_bf[:, :], ut_sb[:, :]), ins=[ut_sb], outs=[ut_bf])
        P.dma("g", ut_o[:, :], ut_bf[:, :], ins=[ut_bf])
        ps_f = ps(P, st, "ps_f", [PT, 512], F32)
        ps_b = ps(P, st, "ps_b", [RK, 16], F32)
        for wname, vv, bname, scaled, aug in [
            ("wq", vs_sb, "bq", True, False), ("wk", v_sb, "bk", False, False),
            ("wv", v_sb, "bv", False, True),
        ]:
            w_d = P.gd("p_" + wname, [D, D], F32)
            w_tiles = load_tiles(P, st, wname + "_t", w_d, D, D, F32)
            b_cols = load_cols(P, st, wname + "_b", P.gd("p_" + bname, [D], F32), KD)
            cw = 33 if aug else 32
            o_ap = P.gd("t_" + ("wvv" if aug else ("wqv" if wname == "wq" else "wkv")),
                        [D, NH * cw], BF16, kind=okind)
            stage = sb(P, st, wname + "_st", [PT, NH * cw], BF16)
            for ds_ in range(KD):
                P.op("v", lambda e, stage=stage: e.memset(stage[:, :], 0.0), outs=[stage])
                for h in range(NH):
                    base = (h % 2) * HD
                    lhs = w_tiles[h // 2][base:base + HD, ds_ * PT:(ds_ + 1) * PT]
                    P.op("t", lambda e, lhs=lhs, vv=vv, base=base: e.matmul(
                        ps_f[:, 0:RK], lhs, vv[base:base + HD, :], start=True, stop=True),
                        ins=[w_tiles[h // 2], vv], outs=[ps_f])
                    P.op("v", lambda e, h=h, cw=cw, stage=stage: e.tensor_copy(
                        stage[:, h * cw:h * cw + RK], ps_f[:, 0:RK]), ins=[ps_f], outs=[stage])
                P.dma("g", o_ap[ds_ * PT:(ds_ + 1) * PT, :], stage[:, :], ins=[stage])
            if aug:
                o9 = P.gd("t_wvv9", [1, NH * cw], BF16, kind=okind)
                r9 = sb(P, st, "wvv9_st", [1, NH * cw], BF16)
                for h in range(NH):
                    base = (h % 2) * HD
                    bcol = b_cols[base:base + HD, (h // 2):(h // 2) + 1]
                    P.op("t", lambda e, bcol=bcol, vv=vv, base=base: e.matmul(
                        ps_f[0:1, 0:RK],
                        bcol, vv[base:base + HD, :], start=True, stop=True),
                        ins=[b_cols, vv], outs=[ps_f])
                    P.op("v", lambda e, h=h, r9=r9, cw=cw: e.tensor_copy(r9[0:1, h * cw:h * cw + RK],
                                                           ps_f[0:1, 0:RK]),
                         ins=[ps_f], outs=[r9])
                    P.op("v", lambda e, h=h, r9=r9, cw=cw: e.memset(
                        r9[0:1, h * cw + RK:h * cw + RK + 1], 1.0),
                         outs=[r9])
                P.dma("g", o9[:, :], r9[:, :], ins=[r9])
            else:
                o_b = P.gd("t_b" + wname[1] + "v", [RK, NH], F32, kind=okind)
                bstage = sb(P, st, wname + "_bst", [RK, NH], F32)
                for h in range(NH):
                    base = (h % 2) * HD
                    bcol = b_cols[base:base + HD, (h // 2):(h // 2) + 1]
                    P.op("t", lambda e, bcol=bcol, vv=vv, base=base: e.matmul(
                        ps_b[:, 0:1], vv[base:base + HD, :], bcol, start=True, stop=True),
                        ins=[b_cols, vv], outs=[ps_b])
                    P.op("v", lambda e, h=h, bstage=bstage: e.tensor_copy(
                        bstage[:, h:h + 1], ps_b[:, 0:1]),
                         ins=[ps_b], outs=[bstage])
                P.dma("g", o_b[:, :], bstage[:, :], ins=[bstage])
        if standalone:
            P.fence_outputs()
            P.emit()
    return nc


def build_l0aa(ctx=None):
    nc, P, x_d, xo_d, uid, standalone = _ctx(ctx)
    NH, HD, RK = 16, 64, 32
    CW = RK + 1
    wqv_d = P.gd("t_wqv", [D, NH * RK], BF16)
    wkv_d = P.gd("t_wkv", [D, NH * RK], BF16)
    wvv_d = P.gd("t_wvv", [D, NH * CW], BF16)
    wvv9_d = P.gd("t_wvv9", [1, NH * CW], BF16)
    bqv_d = P.gd("t_bqv", [RK, NH], F32)
    bkv_d = P.gd("t_bkv", [RK, NH], F32)
    ut_d = P.gd("t_ut", [RK, HD], BF16)
    wot_d = P.gd("t_wot", [D, D], BF16)
    bo_d = P.gd("bo", [D], F32)
    with contextlib.ExitStack() as st:
        cm = Common(P, st)
        bqv = sb(P, st, "bqv", [RK, NH], F32)
        bkv = sb(P, st, "bkv", [RK, NH], F32)
        P.dma("y", bqv[:, :], bqv_d[:, :], outs=[bqv])
        P.dma("y", bkv[:, :], bkv_d[:, :], outs=[bkv])
        zq = [sb(P, st, f"zq{i}", [PT, T], BF16) for i in range(6)]
        zk = [sb(P, st, f"zk{i}", [PT, T], BF16) for i in range(6)]
        zv = [sb(P, st, f"zv{h}", [PT, 32 * CW], BF16) for h in range(NH)]
        NTT = T // PT
        NCH = T // 512
        zst = contextlib.ExitStack()
        hT = [sb(P, zst, f"hT{i}", [PT, T], BF16) for i in range(KD)]
        make_hT(P, zst, cm, x_d, sink_tiles=hT)
        ps_z = ps(P, st, "ps_z", [RK, 512], F32)
        ps_v = ps(P, st, "ps_v", [PT, CW], F32)
        for wname, w_dram, ztarget, b_sb in [
            ("wqv", wqv_d, zq, None), ("wkv", wkv_d, zk, None), ("wvv", wvv_d, zv, None),
        ]:
            wst = contextlib.ExitStack()
            if wname == "wvv":
                wtiles = load_tiles(P, wst, "wvv", wvv_d, D, NH * CW, BF16)
                wvv9 = sb(P, wst, "wvv9", [1, NH * CW], BF16)
                P.dma("y", wvv9[:, :], wvv9_d[:, :], outs=[wvv9])
                for h in range(NH):
                    for tt in range(NTT):
                        for kt in range(KD):
                            P.op("t", lambda e, h=h, tt=tt, kt=kt, wtiles=wtiles: e.matmul(
                                ps_v[:, :], hT[kt][:, tt * PT:(tt + 1) * PT],
                                wtiles[kt][:, h * CW:(h + 1) * CW],
                                start=(kt == 0), stop=False),
                                ins=[hT[kt], wtiles[kt]], outs=[ps_v])
                        P.op("t", lambda e, h=h, tt=tt, wvv9=wvv9: e.matmul(
                            ps_v[:, :], cm.ones_bf[0:1, :], wvv9[0:1, h * CW:(h + 1) * CW],
                            start=False, stop=True),
                            ins=[cm.ones_bf, wvv9], outs=[ps_v])
                        P.op("v", lambda e, h=h, tt=tt: e.tensor_copy(
                            zv[h][:, tt * CW:(tt + 1) * CW], ps_v[:, :]),
                            ins=[ps_v], outs=[zv[h]])
            else:
                wtiles = load_tiles(P, wst, wname, w_dram, D, NH * RK, BF16)
                bias = bqv if wname == "wqv" else bkv
                for h in range(NH):
                    pk, r0 = h // 3, (h % 3) * RK
                    for nch in range(NCH):
                        for kt in range(KD):
                            P.op("t", lambda e, h=h, nch=nch, kt=kt, wtiles=wtiles: e.matmul(
                                ps_z[:, :], wtiles[kt][:, h * RK:(h + 1) * RK],
                                hT[kt][:, nch * 512:(nch + 1) * 512],
                                start=(kt == 0), stop=(kt == KD - 1)),
                                ins=[wtiles[kt], hT[kt]], outs=[ps_z])
                        P.op("v", lambda e, nch=nch, pk=pk, r0=r0, ztarget=ztarget,
                             h=h, bias=bias: e.tensor_scalar(
                            ztarget[pk][r0:r0 + RK, nch * 512:(nch + 1) * 512], ps_z[:, :],
                            bias[:, h:h + 1], None, ALU.add),
                            ins=[ps_z, bias], outs=[ztarget[pk]])
            P.barrier()
            wst.close()
        P.barrier()
        zst.close()
        ut = sb(P, st, "ut", [RK, HD], BF16)
        P.dma("y", ut[:, :], ut_d[:, :], outs=[ut])
        wot = load_tiles(P, st, "wot", wot_d, D, D, BF16)
        ps_b = ps(P, st, "ps_b", [PT, 512], F32)
        bo_bc = bcast_row(P, st, "bo_bc", bo_d.rearrange("(a d) -> a d", a=1), D,
                          cm.ones_f, ps_b)
        ps_s = ps(P, st, "ps_s", [PT, 512], F32)
        ps_y = ps(P, st, "ps_y", [CW, 512], F32)
        ps_lb = ps(P, st, "ps_lb", [RK, 512], F32)
        ps_u = ps(P, st, "ps_u", [HD, 512], F32)
        ps_o = ps(P, st, "ps_o", [PT, 512], F32)
        slab = sb(P, st, "slab", [PT, 512], BF16)
        rl_sb = sb(P, st, "rl_sb", [1, 512], F32)
        yr_f = sb(P, st, "yr_f", [RK, 512], F32)
        yrn = sb(P, st, "yrn", [RK, 512], BF16)
        yT = [sb(P, st, f"yT{i}", [PT, 512], BF16) for i in range(KD)]
        x_res = sb(P, st, "x_res", [PT, D], F32)
        o_f = sb(P, st, "o_f", [PT, D], F32)
        for qc in range(NCH):
            for h in range(NH):
                pk, r0 = h // 3, (h % 3) * RK
                for kt in range(NTT):
                    P.op("t", lambda e, h=h, qc=qc, kt=kt, pk=pk, r0=r0: e.matmul(
                        ps_s[:, :], zk[pk][r0:r0 + RK, kt * PT:(kt + 1) * PT],
                        zq[pk][r0:r0 + RK, qc * 512:(qc + 1) * 512],
                        start=True, stop=True), ins=[zk[pk], zq[pk]], outs=[ps_s])
                    P.op("s", lambda e: e.activation(slab[:, :], ps_s[:, :], AF.Exp),
                         ins=[ps_s], outs=[slab])
                    P.op("t", lambda e, h=h, kt=kt: e.matmul(
                        ps_y[:, :], zv[h][:, kt * CW:(kt + 1) * CW], slab[:, :],
                        start=(kt == 0), stop=(kt == NTT - 1)),
                        ins=[zv[h], slab], outs=[ps_y])
                P.op("v", lambda e: e.reciprocal(rl_sb[:, :], ps_y[RK:CW, :]),
                     ins=[ps_y], outs=[rl_sb])
                P.op("t", lambda e: e.matmul(ps_lb[:, :], cm.ones_f[0:1, 0:RK],
                                             rl_sb[:, :], start=True, stop=True),
                     ins=[cm.ones_f, rl_sb], outs=[ps_lb])
                P.op("s", lambda e: e.copy(yr_f[:, :], ps_y[0:RK, :]), ins=[ps_y], outs=[yr_f])
                P.op("v", lambda e: e.tensor_tensor(yrn[:, :], yr_f[:, :], ps_lb[:, :],
                                                    ALU.mult), ins=[yr_f, ps_lb], outs=[yrn])
                P.op("t", lambda e: e.matmul(ps_u[:, :], ut[:, :], yrn[:, :],
                                             start=True, stop=True), ins=[ut, yrn], outs=[ps_u])
                dp, rr = h // 2, (h % 2) * HD
                P.op("v", lambda e, dp=dp, rr=rr: e.tensor_copy(
                    yT[dp][rr:rr + HD, :], ps_u[:, :]), ins=[ps_u], outs=[yT[dp]])
            for qt in range(4):
                row = qc * 512 + qt * PT
                P.dma("y", x_res[:, :], x_d[row:row + PT, :], outs=[x_res])
                for n2 in range(2):
                    for kt in range(KD):
                        P.op("t", lambda e, qt=qt, n2=n2, kt=kt: e.matmul(
                            ps_o[:, :], yT[kt][:, qt * PT:(qt + 1) * PT],
                            wot[kt][:, n2 * 512:(n2 + 1) * 512],
                            start=(kt == 0), stop=(kt == KD - 1)),
                            ins=[yT[kt], wot[kt]], outs=[ps_o])
                    P.op("v", lambda e, n2=n2: e.tensor_tensor(
                        o_f[:, n2 * 512:(n2 + 1) * 512], ps_o[:, :],
                        bo_bc[:, n2 * 512:(n2 + 1) * 512], ALU.add),
                        ins=[ps_o, bo_bc], outs=[o_f])
                P.op("v", lambda e: e.tensor_add(o_f[:, :], o_f[:, :], x_res[:, :]),
                     ins=[o_f, x_res], outs=[o_f])
                P.dma("g", xo_d[row:row + PT, :], o_f[:, :], ins=[o_f])
        if standalone:
            P.fence_outputs()
            P.emit()
    return nc



def host_consts():
    ident = np.eye(PT, dtype=BF)
    half = 32
    theta = 1.0 / 10000 ** (np.arange(half, dtype=np.float64) / half)
    pos = np.arange(T, dtype=np.float64)
    ang = pos[None, :] * theta[:, None]
    cos32 = np.cos(ang).astype(np.float32)
    sin32 = np.sin(ang).astype(np.float32)
    cosp = np.zeros((PT, T), np.float32)
    sinp = np.zeros((PT, T), np.float32)
    for r in range(PT):
        f = r % half
        cosp[r] = cos32[f]
        sinp[r] = -sin32[f] if (r % 64) < half else sin32[f]
    swap = np.array([i + half if (i % 64) < half else i - half for i in range(PT)])
    psw = np.zeros((PT, PT), np.float32)
    for i in range(PT):
        psw[swap[i], i] = 1.0
    hsel = np.zeros((16, 1024), np.float32)
    for h in range(16):
        hsel[h, h * 64:(h + 1) * 64] = 1.0
    return {"c_ident": ident, "c_cosp": cosp.astype(BF), "c_sinp": sinp.astype(BF),
            "c_pswap": psw.astype(BF), "c_hsel": hsel.astype(BF)}


def prep_inputs(inp):
    tr = lambda a: np.ascontiguousarray(np.asarray(a).T)
    d = {
        "p_wot": tr(inp["L0_attn_a_Wo"]),
        "p_wg": np.asarray(inp["L0_ffn_Wg"]),
        "p_w1": np.asarray(inp["L0_ffn_W1"]).reshape(8192, 1408),
        "p_w2": np.asarray(inp["L0_ffn_W2"]).reshape(8192, 1408),
        "p_w3": np.asarray(inp["L0_ffn_W3"]).reshape(11264, 1024),
        "p_ws1": np.asarray(inp["L0_ffn_Ws1"]),
        "p_ws2": np.asarray(inp["L0_ffn_Ws2"]),
        "p_ws3": np.asarray(inp["L0_ffn_Ws3"]),
        "p_wint": tr(inp["L0_attn_b_W_in"]),
        "p_wdtt": tr(inp["L0_attn_b_W_dt"]),
        "p_wxt": tr(inp["L0_attn_b_W_x"]),
        "p_woutt": tr(inp["L0_attn_b_W_out"]),
        "p_wout1a": np.asarray(inp["L1_attn_a_w_out"]),
        "p_qwt": tr(inp["L1_ffn_q_w"]), "p_kwt": tr(inp["L1_ffn_k_w"]),
        "p_vwt": tr(inp["L1_ffn_v_w"]), "p_owt": tr(inp["L1_ffn_o_w"]),
        "p_qwt2": tr(inp["L1_attn_b_q_w"]), "p_kwt2": tr(inp["L1_attn_b_k_w"]),
        "p_vwt2": tr(inp["L1_attn_b_v_w"]), "p_owt2": tr(inp["L1_attn_b_out_w"]),
        "p_wq": np.asarray(inp["L0_attn_a_Wq"]), "p_wk": np.asarray(inp["L0_attn_a_Wk"]),
        "p_wv": np.asarray(inp["L0_attn_a_Wv"]),
        "p_bq": np.asarray(inp["L0_attn_a_bq"]), "p_bk": np.asarray(inp["L0_attn_a_bk"]),
        "p_bv": np.asarray(inp["L0_attn_a_bv"]),
        "p_v": np.asarray(inp["L0_attn_a_V"]),
        "p_u_t": tr(inp["L0_attn_a_U"]),
    }
    wqkv = np.asarray(inp["L1_attn_a_w_qkv"]).copy()
    bqkv = np.asarray(inp["L1_attn_a_b_qkv"]).copy()
    perm = np.arange(3072)
    for blk in range(2):
        for h in range(16):
            base = blk * 1024 + h * 64
            perm[base:base + 32] = blk * 1024 + h * 64 + np.arange(0, 64, 2)
            perm[base + 32:base + 64] = blk * 1024 + h * 64 + np.arange(1, 64, 2)
    d["p_wqkv"] = np.ascontiguousarray(wqkv[:, perm])
    d["_b_qkv_perm"] = bqkv[perm]
    return d


def l0aa_inputs(inp, pr, cst, x):
    return {
        "x": x, "c_ident": cst["c_ident"],
        "t_wqv": pr["t_wqv"], "t_wkv": pr["t_wkv"], "t_wvv": pr["t_wvv"],
        "t_wvv9": pr["t_wvv9"], "t_bqv": pr["t_bqv"], "t_bkv": pr["t_bkv"],
        "t_ut": pr["t_ut"], "t_wot": pr["t_wot"],
        "bo": np.asarray(inp["L0_attn_a_bo"]),
    }


def build_fused():
    nc = bass.Bass("TRN2", target_bir_lowering=False)
    P = Prog(nc)
    x_in = nc.dram_tensor("x", [T, D], F32, kind="ExternalInput").ap()
    xo_out = nc.dram_tensor("xo", [T, D], F32, kind="ExternalOutput").ap()
    build_prep(ctx=(nc, P, None, None, ""))
    P.barrier()
    seq = [build_l0aa, build_l0f, build_l0ab, build_l0f,
           build_l1aa, build_l1f, build_l1ab, build_l1f]
    cur = x_in
    for i, b in enumerate(seq):
        nxt = xo_out if i == len(seq) - 1 else P.gd(f"xi{i}", [T, D], F32,
                                                    kind="Internal")
        b(ctx=(nc, P, cur, nxt, f"_{i}"))
        P.barrier()
        cur = nxt
    P.fence_outputs()
    P.emit()
    return nc, P


def fused_inputs(inp, cst, pin, x):
    d = {k: v for k, v in pin.items() if not k.startswith("_")}
    d.update(cst)
    d["bqkvp"] = pin["_b_qkv_perm"]
    d["x"] = x
    for nm, key in [("bo", "L0_attn_a_bo"), ("gamma", "L0_attn_b_gamma"),
                    ("beta", "L0_attn_b_beta"), ("b_dt", "L0_attn_b_b_dt"),
                    ("a_log", "L0_attn_b_A_log"), ("d_par", "L0_attn_b_D_param"),
                    ("b_out", "L1_attn_a_b_out"), ("q_b", "L1_ffn_q_b"),
                    ("k_b", "L1_ffn_k_b"), ("v_b", "L1_ffn_v_b"),
                    ("o_b", "L1_ffn_o_b"), ("norm_w", "L1_ffn_norm_w"),
                    ("norm_b", "L1_ffn_norm_b"), ("out_b", "L1_attn_b_out_b")]:
        d[nm] = np.asarray(inp[key])
    d["conv_w"] = np.ascontiguousarray(np.asarray(inp["L0_attn_b_conv_w"])[:, 0, :])
    return d


def _launches():
    return [(build_l0aa, l0aa_inputs), (build_l0f, l0f_inputs),
            (build_l0ab, l0ab_inputs), (build_l0f, l0f_inputs),
            (build_l1aa, l1aa_inputs), (build_l1f, l1f_inputs),
            (build_l1ab, l1ab_inputs), (build_l1f, l1f_inputs)]


def kernel(**inp):
    return kernel_fused(**inp)


def kernel_multilaunch(**inp):
    assert np.asarray(inp["L1_ffn_router_prob"]).shape[-1] == 4
    cores = list(range(NCORES))
    x_full = np.ascontiguousarray(np.asarray(inp["L0_attn_a_x"], np.float32))
    xs = [np.ascontiguousarray(x_full[c]) for c in range(NCORES)]
    cst = host_consts()
    pin = prep_inputs(inp)
    prep_nc = build_prep()
    pr = run_bass_kernel_spmd(prep_nc, [{k: v for k, v in pin.items()
                                         if not k.startswith("_")}],
                              core_ids=[0]).results[0]
    pr = dict(pr)
    pr["_b_qkv_perm"] = pin["_b_qkv_perm"]
    nc_cache = {}
    for builder, mkin in _launches():
        if builder not in nc_cache:
            nc_cache[builder] = builder()
        nc = nc_cache[builder]
        in_maps = [mkin(inp, pr, cst, xs[c]) for c in range(NCORES)]
        rr = run_bass_kernel_spmd(nc, in_maps, core_ids=cores)
        xs = [np.ascontiguousarray(np.asarray(rr.results[c]["xo"])) for c in range(NCORES)]
    return np.stack(xs).astype(np.float32)


def kernel_fused(**inp):
    assert np.asarray(inp["L1_ffn_router_prob"]).shape[-1] == 4
    cores = list(range(NCORES))
    x_full = np.ascontiguousarray(np.asarray(inp["L0_attn_a_x"], np.float32))
    xs = [np.ascontiguousarray(x_full[c]) for c in range(NCORES)]
    cst = host_consts()
    pin = prep_inputs(inp)
    nc, _ = build_fused()
    in_maps = [fused_inputs(inp, cst, pin, xs[c]) for c in range(NCORES)]
    rr = run_bass_kernel_spmd(nc, in_maps, core_ids=cores)
    return np.stack([np.ascontiguousarray(np.asarray(rr.results[c]["xo"]))
                     for c in range(NCORES)]).astype(np.float32)


def build_l0f(ctx=None):
    nc, P, x_d, xo_d, uid, standalone = _ctx(ctx)
    E, HH, HS = 8, 1408, 2816
    NHH = HH // PT
    wg_d = P.gd("t_wg", [D, E], BF16)
    w1_d = P.gd("t_w1", [E * D, HH], BF16)
    w2_d = P.gd("t_w2", [E * D, HH], BF16)
    w3_d = P.gd("t_w3", [E * HH, D], BF16)
    ws1_d = P.gd("t_ws1", [D, HS], BF16)
    ws2_d = P.gd("t_ws2", [D, HS], BF16)
    ws3_d = P.gd("t_ws3", [HS, D], BF16)
    sc_ht = P.gd("sc_ht" + uid, [T // PT, D, PT], BF16, kind="Internal")
    NTT, NCH = T // PT, T // 512
    with contextlib.ExitStack() as st:
        cm = Common(P, st)
        cp = cm.cp
        wg = load_tiles(P, st, "wg", wg_d, D, E, BF16)
        w_sb = sb(P, st, "w_sb", [PT, NTT * E], F32)
        P._top2_scratch = (
            sb(P, st, "t2_r", [PT, E], F32), sb(P, st, "t2_m1", [PT, 2], F32),
            sb(P, st, "t2_m2", [PT, 1], F32), sb(P, st, "t2_r2", [PT, E], F32),
            sb(P, st, "t2_eq1", [PT, E], F32), sb(P, st, "t2_eq2", [PT, E], F32),
            sb(P, st, "t2_e1", [PT, E], F32), sb(P, st, "t2_den", [PT, 2], F32),
        )
        ps_r = ps(P, st, "ps_r", [PT, E], F32)
        make_hT(P, st, cm, x_d, sink_dram=sc_ht, router=(wg, w_sb, ps_r))
        yacc = [sb(P, st, f"yacc{i}", [PT, D], BF16) for i in range(NTT)]
        hTc = [sb(P, st, f"hTc{i}", [PT, 512], BF16) for i in range(KD)]
        het = [sb(P, st, f"het{i}", [PT, 512], BF16) for i in range(NHH)]
        g1s = sb(P, st, "g1s", [PT, 512], BF16)
        x_res = sb(P, st, "x_res", [PT, D], F32)
        o_f = sb(P, st, "o_f", [PT, D], F32)
        ps_g1 = ps(P, st, "ps_g1", [PT, 512], F32)
        ps_g2 = ps(P, st, "ps_g2", [PT, 512], F32)
        ps_y = ps(P, st, "ps_y", [PT, 512], F32)
        units = [("e", e) for e in range(E)] + [("s", 0), ("s", 1)]
        for ui, (kind, idx) in enumerate(units):
            wst = contextlib.ExitStack()
            if kind == "e":
                w1u = load_tiles(P, wst, f"w1u{ui}", w1_d[idx * D:(idx + 1) * D, :],
                                 D, HH, BF16)
                w2u = load_tiles(P, wst, f"w2u{ui}", w2_d[idx * D:(idx + 1) * D, :],
                                 D, HH, BF16)
                w3u = load_tiles(P, wst, f"w3u{ui}", w3_d[idx * HH:(idx + 1) * HH, :],
                                 HH, D, BF16)
            else:
                c0 = idx * HH
                w1u = load_tiles(P, wst, f"w1u{ui}", ws1_d[:, c0:c0 + HH], D, HH, BF16)
                w2u = load_tiles(P, wst, f"w2u{ui}", ws2_d[:, c0:c0 + HH], D, HH, BF16)
                w3u = load_tiles(P, wst, f"w3u{ui}", ws3_d[c0:c0 + HH, :], HH, D, BF16)
            for tc in range(NCH):
                for dp in range(KD):
                    for j in range(4):
                        tt = tc * 4 + j
                        P.dma("y", hTc[dp][:, j * PT:(j + 1) * PT],
                              sc_ht[tt, dp * PT:(dp + 1) * PT, :], outs=[hTc[dp]])
                for hh in range(NHH):
                    for kt in range(KD):
                        P.op("t", lambda e, hh=hh, kt=kt, w1u=w1u: e.matmul(
                            ps_g1[:, :], w1u[kt][:, hh * PT:(hh + 1) * PT], hTc[kt][:, :],
                            start=(kt == 0), stop=(kt == KD - 1)),
                            ins=[w1u[kt], hTc[kt]], outs=[ps_g1])
                    P.op("s", lambda e: e.activation(g1s[:, :], ps_g1[:, :], AF.Silu),
                         ins=[ps_g1], outs=[g1s])
                    for kt in range(KD):
                        P.op("t", lambda e, hh=hh, kt=kt, w2u=w2u: e.matmul(
                            ps_g2[:, :], w2u[kt][:, hh * PT:(hh + 1) * PT], hTc[kt][:, :],
                            start=(kt == 0), stop=(kt == KD - 1)),
                            ins=[w2u[kt], hTc[kt]], outs=[ps_g2])
                    P.op("v", lambda e, hh=hh: e.tensor_tensor(
                        het[hh][:, :], g1s[:, :], ps_g2[:, :], ALU.mult),
                        ins=[g1s, ps_g2], outs=[het[hh]])
                for j in range(4):
                    tt = tc * 4 + j
                    if ui == len(units) - 1:
                        P.dma("y", x_res[:, :], x_d[tt * PT:(tt + 1) * PT, :], outs=[x_res])
                    for n2 in range(2):
                        for hh in range(NHH):
                            P.op("t", lambda e, j=j, n2=n2, hh=hh, w3u=w3u: e.matmul(
                                ps_y[:, :], het[hh][:, j * PT:(j + 1) * PT],
                                w3u[hh][:, n2 * 512:(n2 + 1) * 512],
                                start=(hh == 0), stop=(hh == NHH - 1)),
                                ins=[het[hh], w3u[hh]], outs=[ps_y])
                        ysl = yacc[tt][:, n2 * 512:(n2 + 1) * 512]
                        if kind == "e":
                            wcol = w_sb[:, tt * E + idx:tt * E + idx + 1]
                            if ui == 0:
                                P.op("v", lambda e, ysl=ysl, wcol=wcol: e.tensor_scalar(
                                    ysl, ps_y[:, :], wcol, None, ALU.mult),
                                    ins=[ps_y, w_sb], outs=[yacc[tt]])
                            else:
                                P.op("v", lambda e, ysl=ysl, wcol=wcol: e.scalar_tensor_tensor(
                                    ysl, ps_y[:, :], wcol, ysl, ALU.mult, ALU.add),
                                    ins=[ps_y, w_sb, yacc[tt]], outs=[yacc[tt]])
                        elif ui == len(units) - 2:
                            P.op("v", lambda e, ysl=ysl: e.scalar_tensor_tensor(
                                ysl, ps_y[:, :], 1.0, ysl, ALU.mult, ALU.add),
                                ins=[ps_y, yacc[tt]], outs=[yacc[tt]])
                        else:
                            osl = o_f[:, n2 * 512:(n2 + 1) * 512]
                            P.op("v", lambda e, ysl=ysl, osl=osl: e.scalar_tensor_tensor(
                                osl, ps_y[:, :], 1.0, ysl, ALU.mult, ALU.add),
                                ins=[ps_y, yacc[tt]], outs=[o_f])
                    if ui == len(units) - 1:
                        P.op("v", lambda e: e.tensor_add(o_f[:, :], o_f[:, :], x_res[:, :]),
                             ins=[o_f, x_res], outs=[o_f])
                        P.dma("g", xo_d[tt * PT:(tt + 1) * PT, :], o_f[:, :], ins=[o_f])
            P.barrier()
            wst.close()
        if standalone:
            P.fence_outputs()
            P.emit()
    return nc


def l0f_inputs(inp, pr, cst, x):
    return {"x": x, "c_ident": cst["c_ident"],
            "t_wg": pr["t_wg"], "t_w1": pr["t_w1"], "t_w2": pr["t_w2"],
            "t_w3": pr["t_w3"], "t_ws1": pr["t_ws1"], "t_ws2": pr["t_ws2"],
            "t_ws3": pr["t_ws3"]}


def build_l0ab(ctx=None):
    nc, P, x_d, xo_d, uid, standalone = _ctx(ctx)
    S = 16
    wint_d = P.gd("t_wint", [D, 2 * D], BF16)
    wdtt_d = P.gd("t_wdtt", [D, D], BF16)
    wxt_d = P.gd("t_wxt", [D, 2 * S], BF16)
    woutt_d = P.gd("t_woutt", [D, D], BF16)
    gam_d = P.gd("gamma", [D], F32)
    bet_d = P.gd("beta", [D], F32)
    bdt_d = P.gd("b_dt", [D], F32)
    alog_d = P.gd("a_log", [D, S], F32)
    dpar_d = P.gd("d_par", [D], F32)
    conv_d = P.gd("conv_w", [D, 3], F32)
    sc_ht = P.gd("sc_ht" + uid, [T // PT, D, PT], BF16, kind="Internal")
    sc_xx = P.gd("sc_xx" + uid, [D, T + 2], BF16, kind="Internal")
    sc_zs = P.gd("sc_zs" + uid, [D, T], BF16, kind="Internal")
    NCH = T // 512
    with contextlib.ExitStack() as st:
        cm = Common(P, st)
        cp = cm.cp
        ps_b = ps(P, st, "ps_bx", [PT, 512], F32)
        gw_bc = bcast_row(P, st, "gw_bc", gam_d.rearrange("(a d) -> a d", a=1), D,
                          cm.ones_f, ps_b)
        gb_bc = bcast_row(P, st, "gb_bc", bet_d.rearrange("(a d) -> a d", a=1), D,
                          cm.ones_f, ps_b)
        make_hT(P, st, cm, x_d, sink_dram=sc_ht, mode="double_affine",
                gw_bc=gw_bc, gb_bc=gb_bc)
        wdtt = load_tiles(P, st, "wdtt", wdtt_d, D, D, BF16)
        wxt = load_tiles(P, st, "wxt", wxt_d, D, 2 * S, BF16)
        woutt = load_tiles(P, st, "woutt", woutt_d, D, D, BF16)
        a_sbt = load_tiles(P, st, "a_sbt", alog_d, D, S, F32)
        for dp in range(KD):
            P.op("s", lambda e, dp=dp: e.activation(a_sbt[dp][:, :], a_sbt[dp][:, :], AF.Exp),
                 ins=[a_sbt[dp]], outs=[a_sbt[dp]])
            P.op("v", lambda e, dp=dp: e.tensor_scalar_mul(a_sbt[dp][:, :], a_sbt[dp][:, :],
                                                           -1.0), ins=[a_sbt[dp]], outs=[a_sbt[dp]])
        conv_c = load_tiles(P, st, "conv_c", conv_d, D, 3, F32)
        bdt_c = load_cols(P, st, "bdt_c", bdt_d, KD)
        dpar_c = load_cols(P, st, "dpar_c", dpar_d, KD)
        xzst = contextlib.ExitStack()
        wint = load_tiles(P, xzst, "wint", wint_d, D, 2 * D, BF16)
        hTc = [sb(P, xzst, f"hTc{i}", [PT, 512], BF16) for i in range(KD)]
        stage = sb(P, xzst, "stg", [PT, 512], BF16)
        zcol = sb(P, xzst, "zcol", [PT, 1], BF16)
        P.op("v", lambda e: e.memset(zcol[:, :], 0.0), outs=[zcol])
        ps_m = ps(P, st, "ps_m", [PT, 512], F32)
        for dp in range(KD):
            P.dma("g", sc_xx[dp * PT:(dp + 1) * PT, 0:1], zcol[:, :], ins=[zcol], slow=True)
            P.dma("g", sc_xx[dp * PT:(dp + 1) * PT, T + 1:T + 2], zcol[:, :], ins=[zcol],
                  slow=True)
        for nch in range(NCH):
            c0 = nch * 512
            for dp in range(KD):
                for j in range(4):
                    tt = nch * 4 + j
                    P.dma("y", hTc[dp][:, j * PT:(j + 1) * PT],
                          sc_ht[tt, dp * PT:(dp + 1) * PT, :], outs=[hTc[dp]])
            for ms in range(16):
                for kt in range(KD):
                    P.op("t", lambda e, ms=ms, kt=kt: e.matmul(
                        ps_m[:, :], wint[kt][:, ms * PT:(ms + 1) * PT], hTc[kt][:, :],
                        start=(kt == 0), stop=(kt == KD - 1)),
                        ins=[wint[kt], hTc[kt]], outs=[ps_m])
                if ms < 8:
                    P.op("v", lambda e: e.tensor_copy(stage[:, :], ps_m[:, :]),
                         ins=[ps_m], outs=[stage])
                    P.dma("g", sc_xx[ms * PT:(ms + 1) * PT, 1 + c0:1 + c0 + 512],
                          stage[:, :], ins=[stage])
                else:
                    P.op("s", lambda e: e.activation(stage[:, :], ps_m[:, :], AF.Silu),
                         ins=[ps_m], outs=[stage])
                    P.dma("g", sc_zs[(ms - 8) * PT:(ms - 7) * PT, c0:c0 + 512],
                          stage[:, :], ins=[stage])
        P.barrier()
        xzst.close()
        xxc = [sb(P, st, f"xxc{i}", [PT, 514], BF16) for i in range(KD)]
        xc_f = [sb(P, st, f"xcf{i}", [PT, 512], F32) for i in range(KD)]
        xc_bf = [sb(P, st, f"xcb{i}", [PT, 512], BF16) for i in range(KD)]
        dt_f = [sb(P, st, f"dtf{i}", [PT, 512], F32) for i in range(KD)]
        zs_bf = [sb(P, st, f"zsb{i}", [PT, 512], BF16) for i in range(KD)]
        y2_bf = [sb(P, st, f"y2b{i}", [PT, 512], BF16) for i in range(KD)]
        y_acc = [sb(P, st, f"yac{i}", [PT, 512], F32) for i in range(KD)]
        sp1 = sb(P, st, "sp1", [PT, 512], F32)
        sp2 = sb(P, st, "sp2", [PT, 512], F32)
        bc_sb = sb(P, st, "bc_sb", [32, 512], F32)
        brow = sb(P, st, "brow", [1, 512], F32)
        crow = sb(P, st, "crow", [1, 512], F32)
        a_t = sb(P, st, "a_t", [PT, 512], F32)
        b_t = sb(P, st, "b_t", [PT, 512], F32)
        h_t = sb(P, st, "h_t2", [PT, 512], F32)
        hc_t = sb(P, st, "hc_t", [PT, 512], F32)
        carry = sb(P, st, "carry", [PT, PT], F32)
        oT_sb = sb(P, st, "oT_sb", [PT, 512], BF16)
        out_f = [sb(P, st, f"outf{i}", [PT, D], F32) for i in range(4)]
        hof = sb(P, st, "hof", [PT, D], F32)
        tqo = sb(P, st, "tqo", [PT, D], F32)
        tco = sb(P, st, "tco", [PT, 8], F32)
        P.op("v", lambda e: e.memset(carry[:, :], 0.0), outs=[carry])
        ps_bc = ps(P, st, "ps_bc", [32, 512], F32)
        ps_bb = ps(P, st, "ps_bb", [PT, 512], F32)
        ps_cb = ps(P, st, "ps_cb", [PT, 512], F32)
        ps_tr = ps(P, st, "ps_tr", [PT, PT], BF16)
        for tc in range(NCH):
            c0 = tc * 512
            for dp in range(KD):
                P.dma("y", xxc[dp][:, :], sc_xx[dp * PT:(dp + 1) * PT, c0:c0 + 514],
                      outs=[xxc[dp]])
                P.dma("y", zs_bf[dp][:, :], sc_zs[dp * PT:(dp + 1) * PT, c0:c0 + 512],
                      outs=[zs_bf[dp]])
            for dp in range(KD):
                w0 = conv_c[dp][:, 0:1]
                w1 = conv_c[dp][:, 1:2]
                w2 = conv_c[dp][:, 2:3]
                P.op("v", lambda e, dp=dp, w1=w1: e.tensor_scalar(
                    xc_f[dp][:, :], xxc[dp][:, 1:513], w1, None, ALU.mult),
                    ins=[xxc[dp], conv_c[dp]], outs=[xc_f[dp]])
                P.op("v", lambda e, dp=dp, w0=w0: e.scalar_tensor_tensor(
                    xc_f[dp][:, :], xxc[dp][:, 0:512], w0, xc_f[dp][:, :],
                    ALU.mult, ALU.add), ins=[xxc[dp], conv_c[dp], xc_f[dp]], outs=[xc_f[dp]])
                P.op("v", lambda e, dp=dp, w2=w2: e.scalar_tensor_tensor(
                    xc_f[dp][:, :], xxc[dp][:, 2:514], w2, xc_f[dp][:, :],
                    ALU.mult, ALU.add), ins=[xxc[dp], conv_c[dp], xc_f[dp]], outs=[xc_f[dp]])
                P.op("s", lambda e, dp=dp: e.activation(xc_f[dp][:, :], xc_f[dp][:, :],
                                                        AF.Silu), ins=[xc_f[dp]], outs=[xc_f[dp]])
                P.op("v", lambda e, dp=dp: e.tensor_copy(xc_bf[dp][:, :], xc_f[dp][:, :]),
                     ins=[xc_f[dp]], outs=[xc_bf[dp]])
            for dp in range(KD):
                for kt in range(KD):
                    P.op("t", lambda e, dp=dp, kt=kt: e.matmul(
                        ps_m[:, :], wdtt[kt][:, dp * PT:(dp + 1) * PT], xc_bf[kt][:, :],
                        start=(kt == 0), stop=(kt == KD - 1)),
                        ins=[wdtt[kt], xc_bf[kt]], outs=[ps_m])
                P.op("v", lambda e, dp=dp: e.tensor_scalar(
                    dt_f[dp][:, :], ps_m[:, :], bdt_c[:, dp:dp + 1], None, ALU.add),
                    ins=[ps_m, bdt_c], outs=[dt_f[dp]])
                softplus_tile(P, cp, dt_f[dp], dt_f[dp], sp1, sp2)
            for kt in range(KD):
                P.op("t", lambda e, kt=kt: e.matmul(
                    ps_bc[:, :], wxt[kt][:, 0:32], xc_bf[kt][:, :],
                    start=(kt == 0), stop=(kt == KD - 1)),
                    ins=[wxt[kt], xc_bf[kt]], outs=[ps_bc])
            P.op("v", lambda e: e.tensor_copy(bc_sb[:, :], ps_bc[:, :]), ins=[ps_bc],
                 outs=[bc_sb])
            for s_ in range(S):
                P.dma("y", brow[:, :], bc_sb[s_:s_ + 1, :], ins=[bc_sb], outs=[brow])
                P.dma("y", crow[:, :], bc_sb[S + s_:S + s_ + 1, :], ins=[bc_sb], outs=[crow])
                P.op("t", lambda e: e.matmul(ps_bb[:, :], cm.ones_f[0:1, :], brow[:, :],
                                             start=True, stop=True),
                     ins=[cm.ones_f, brow], outs=[ps_bb])
                P.op("t", lambda e: e.matmul(ps_cb[:, :], cm.ones_f[0:1, :], crow[:, :],
                                             start=True, stop=True),
                     ins=[cm.ones_f, crow], outs=[ps_cb])
                for dp in range(KD):
                    acol = a_sbt[dp][:, s_:s_ + 1]
                    P.op("s", lambda e, dp=dp, acol=acol: e.activation(
                        a_t[:, :], dt_f[dp][:, :], AF.Exp, scale=acol),
                        ins=[dt_f[dp], a_sbt[dp]], outs=[a_t])
                    P.op("v", lambda e, dp=dp: e.tensor_tensor(
                        b_t[:, :], dt_f[dp][:, :], ps_bb[:, :], ALU.mult),
                        ins=[dt_f[dp], ps_bb], outs=[b_t])
                    ccol = carry[:, dp * S + s_:dp * S + s_ + 1]
                    P.op("v", lambda e, ccol=ccol: e.tensor_tensor_scan(
                        h_t[:, :], a_t[:, :], b_t[:, :], ccol, ALU.mult, ALU.add),
                        ins=[a_t, b_t, carry], outs=[h_t])
                    P.op("v", lambda e, ccol=ccol: e.tensor_copy(ccol, h_t[:, 511:512]),
                         ins=[h_t], outs=[carry])
                    P.op("v", lambda e: e.tensor_tensor(hc_t[:, :], h_t[:, :], ps_cb[:, :],
                                                        ALU.mult), ins=[h_t, ps_cb], outs=[hc_t])
                    if s_ == 0:
                        P.op("v", lambda e, dp=dp: e.tensor_copy(y_acc[dp][:, :], hc_t[:, :]),
                             ins=[hc_t], outs=[y_acc[dp]])
                    else:
                        P.op("v", lambda e, dp=dp: e.tensor_add(
                            y_acc[dp][:, :], y_acc[dp][:, :], hc_t[:, :]),
                            ins=[hc_t, y_acc[dp]], outs=[y_acc[dp]])
            for dp in range(KD):
                P.op("v", lambda e, dp=dp: e.scalar_tensor_tensor(
                    y_acc[dp][:, :], xc_f[dp][:, :], dpar_c[:, dp:dp + 1], y_acc[dp][:, :],
                    ALU.mult, ALU.add), ins=[xc_f[dp], dpar_c, y_acc[dp]], outs=[y_acc[dp]])
                P.op("v", lambda e, dp=dp: e.tensor_tensor(
                    y2_bf[dp][:, :], y_acc[dp][:, :], zs_bf[dp][:, :], ALU.mult),
                    ins=[y_acc[dp], zs_bf[dp]], outs=[y2_bf[dp]])
            for j in range(4):
                tt = tc * 4 + j
                P.dma("y", out_f[j][:, :], x_d[tt * PT:(tt + 1) * PT, :], outs=[out_f[j]])
                ln_tile(P, cp, out_f[j], hof, tco, tqo)
                P.op("v", lambda e, j=j: e.tensor_add(out_f[j][:, :], out_f[j][:, :],
                                                      hof[:, :]),
                     ins=[hof, out_f[j]], outs=[out_f[j]])
            for dpo in range(KD):
                for kt in range(KD):
                    P.op("t", lambda e, dpo=dpo, kt=kt: e.matmul(
                        ps_m[:, :], woutt[kt][:, dpo * PT:(dpo + 1) * PT], y2_bf[kt][:, :],
                        start=(kt == 0), stop=(kt == KD - 1)),
                        ins=[woutt[kt], y2_bf[kt]], outs=[ps_m])
                P.op("v", lambda e: e.tensor_copy(oT_sb[:, :], ps_m[:, :]),
                     ins=[ps_m], outs=[oT_sb])
                for j in range(4):
                    P.op("t", lambda e, j=j: e.transpose(
                        ps_tr[:, :], oT_sb[:, j * PT:(j + 1) * PT], cm.ident[:, :]),
                        ins=[oT_sb, cm.ident], outs=[ps_tr])
                    osl = out_f[j][:, dpo * PT:(dpo + 1) * PT]
                    P.op("v", lambda e, osl=osl, j=j: e.scalar_tensor_tensor(
                        osl, ps_tr[:, :], 1.0, osl, ALU.mult, ALU.add),
                        ins=[ps_tr, out_f[j]], outs=[out_f[j]])
            for j in range(4):
                tt = tc * 4 + j
                P.dma("g", xo_d[tt * PT:(tt + 1) * PT, :], out_f[j][:, :], ins=[out_f[j]])
        if standalone:
            P.fence_outputs()
            P.emit()
    return nc


def l0ab_inputs(inp, pr, cst, x):
    return {"x": x, "c_ident": cst["c_ident"],
            "t_wint": pr["t_wint"], "t_wdtt": pr["t_wdtt"], "t_wxt": pr["t_wxt"],
            "t_woutt": pr["t_woutt"],
            "gamma": np.asarray(inp["L0_attn_b_gamma"]),
            "beta": np.asarray(inp["L0_attn_b_beta"]),
            "b_dt": np.asarray(inp["L0_attn_b_b_dt"]),
            "a_log": np.asarray(inp["L0_attn_b_A_log"]),
            "d_par": np.asarray(inp["L0_attn_b_D_param"]),
            "conv_w": np.ascontiguousarray(np.asarray(inp["L0_attn_b_conv_w"])[:, 0, :])}


def build_l1aa(ctx=None):
    nc, P, x_d, xo_d, uid, standalone = _ctx(ctx)
    NH, HD = 16, 64
    wqkv_d = P.gd("t_wqkv", [D, 3 * D], BF16)
    wout_d = P.gd("t_wout1a", [D, D], BF16)
    bqkv_d = P.gd("bqkvp", [3 * D], F32)
    bout_d = P.gd("b_out", [D], F32)
    cosp_d = P.gd("c_cosp", [PT, T], BF16)
    sinp_d = P.gd("c_sinp", [PT, T], BF16)
    psw_d = P.gd("c_pswap", [PT, PT], BF16)
    sc_ht = P.gd("sc_ht" + uid, [T // PT, D, PT], BF16, kind="Internal")
    sc_qte = P.gd("sc_qte" + uid, [D, T], BF16, kind="Internal")
    sc_rq = P.gd("sc_rq" + uid, [D, T], BF16, kind="Internal")
    sc_yt = P.gd("sc_yt" + uid, [D, T], BF16, kind="Internal")
    NCH, NTT = T // 512, T // PT
    with contextlib.ExitStack() as st:
        cm = Common(P, st)
        make_hT(P, st, cm, x_d, sink_dram=sc_ht)
        cosp = sb(P, st, "cosp", [PT, T], BF16)
        sinp = sb(P, st, "sinp", [PT, T], BF16)
        P.dma("y", cosp[:, :], cosp_d[:, :], outs=[cosp])
        P.dma("y", sinp[:, :], sinp_d[:, :], outs=[sinp])
        psw = sb(P, st, "psw", [PT, PT], BF16)
        P.dma("y", psw[:, :], psw_d[:, :], outs=[psw])
        bq_c = load_cols(P, st, "bq_c", bqkv_d[0:D], KD)
        bk_c = load_cols(P, st, "bk_c", bqkv_d[D:2 * D], KD)
        bv_c = load_cols(P, st, "bv_c", bqkv_d[2 * D:3 * D], KD)
        hTc = [sb(P, st, f"hTc{i}", [PT, 512], BF16) for i in range(KD)]
        ps_m = ps(P, st, "ps_m", [PT, 512], F32)
        ps_sw = ps(P, st, "ps_sw", [PT, 512], F32)
        qraw = sb(P, st, "qraw", [PT, 512], F32)
        e1t = sb(P, st, "e1t", [PT, 512], F32)
        t2t = sb(P, st, "t2t", [PT, 512], F32)
        r1t = sb(P, st, "r1t", [PT, 512], F32)
        r2t = sb(P, st, "r2t", [PT, 512], F32)

        def elu1(dst_ap, dst_h, bias_col, bias_h):
            P.op("v", lambda e: e.tensor_scalar(qraw[:, :], ps_m[:, :], bias_col,
                                                None, ALU.add),
                 ins=[ps_m, bias_h], outs=[qraw])
            P.op("v", lambda e: e.tensor_scalar_min(e1t[:, :], qraw[:, :], 0.0),
                 ins=[qraw], outs=[e1t])
            P.op("s", lambda e: e.activation(e1t[:, :], e1t[:, :], AF.Exp),
                 ins=[e1t], outs=[e1t])
            P.op("v", lambda e: e.tensor_scalar_max(t2t[:, :], qraw[:, :], 0.0),
                 ins=[qraw], outs=[t2t])
            P.op("v", lambda e: e.tensor_add(dst_ap, t2t[:, :], e1t[:, :]),
                 ins=[t2t, e1t], outs=[dst_h])

        def rope(dst_ap, dst_h, src_ap, src_h, c0):
            P.op("t", lambda e: e.matmul(ps_sw[:, :], psw[:, :], src_ap,
                                         start=True, stop=True),
                 ins=[psw, src_h], outs=[ps_sw])
            P.op("v", lambda e, c0=c0: e.tensor_tensor(r1t[:, :], src_ap,
                                                       cosp[:, c0:c0 + 512], ALU.mult),
                 ins=[src_h, cosp], outs=[r1t])
            P.op("v", lambda e, c0=c0: e.tensor_tensor(r2t[:, :], ps_sw[:, :],
                                                       sinp[:, c0:c0 + 512], ALU.mult),
                 ins=[ps_sw, sinp], outs=[r2t])
            P.op("v", lambda e: e.tensor_add(dst_ap, r1t[:, :], r2t[:, :]),
                 ins=[r1t, r2t], outs=[dst_h])

        def load_htc(nch):
            for dp in range(KD):
                for j in range(4):
                    tt = nch * 4 + j
                    P.dma("y", hTc[dp][:, j * PT:(j + 1) * PT],
                          sc_ht[tt, dp * PT:(dp + 1) * PT, :], outs=[hTc[dp]])

        qst = contextlib.ExitStack()
        wq = load_tiles(P, qst, "wq", wqkv_d[:, 0:D], D, D, BF16)
        qte_c = [sb(P, qst, f"qtec{i}", [PT, 512], BF16) for i in range(KD)]
        rq_c = [sb(P, qst, f"rqc{i}", [PT, 512], BF16) for i in range(KD)]
        for nch in range(NCH):
            c0 = nch * 512
            load_htc(nch)
            for dp in range(KD):
                for kt in range(KD):
                    P.op("t", lambda e, dp=dp, kt=kt, wq=wq: e.matmul(
                        ps_m[:, :], wq[kt][:, dp * PT:(dp + 1) * PT], hTc[kt][:, :],
                        start=(kt == 0), stop=(kt == KD - 1)),
                        ins=[wq[kt], hTc[kt]], outs=[ps_m])
                elu1(qte_c[dp][:, :], qte_c[dp], bq_c[:, dp:dp + 1], bq_c)
                rope(rq_c[dp][:, :], rq_c[dp], qte_c[dp][:, :], qte_c[dp], c0)
                P.dma("g", sc_qte[dp * PT:(dp + 1) * PT, c0:c0 + 512], qte_c[dp][:, :],
                      ins=[qte_c[dp]])
                P.dma("g", sc_rq[dp * PT:(dp + 1) * PT, c0:c0 + 512], rq_c[dp][:, :],
                      ins=[rq_c[dp]])
        P.barrier()
        qst.close()
        den_f = sb(P, st, "den_f", [PT, 512], F32)
        slab = sb(P, st, "slab", [PT, 512], BF16)
        ps_n = ps(P, st, "ps_n", [PT, 512], F32)
        ps_d = ps(P, st, "ps_d", [PT, 512], F32)
        ps_y = ps(P, st, "ps_y", [HD, 512], F32)
        ps_tv = ps(P, st, "ps_tv", [PT, HD], BF16)
        for hg in range(2):
            gst = contextlib.ExitStack()
            kte = [sb(P, gst, f"kte{hg}{i}", [PT, T], BF16) for i in range(4)]
            rkT = [sb(P, gst, f"rkT{hg}{i}", [PT, T], BF16) for i in range(4)]
            vh_all = [sb(P, gst, f"vh{hg}_{i}", [PT, NTT * HD], BF16) for i in range(8)]
            qe_c = [sb(P, gst, f"qec{hg}_{i}", [PT, 512], BF16) for i in range(4)]
            rqc2 = [sb(P, gst, f"rqc2{hg}_{i}", [PT, 512], BF16) for i in range(4)]
            yt_sb = [sb(P, gst, f"yt{hg}_{i}", [PT, 512], BF16) for i in range(4)]
            vst = contextlib.ExitStack()
            wk = load_tiles(P, vst, f"wk{hg}", wqkv_d[:, D + hg * 512:D + hg * 512 + 512],
                            D, 512, BF16)
            wv = load_tiles(P, vst, f"wv{hg}", wqkv_d[:, 2 * D + hg * 512:2 * D + hg * 512 + 512],
                            D, 512, BF16)
            vT = [sb(P, vst, f"vT{hg}{i}", [PT, T], BF16) for i in range(4)]
            for nch in range(NCH):
                c0 = nch * 512
                load_htc(nch)
                for dl in range(4):
                    dg = hg * 4 + dl
                    for kt in range(KD):
                        P.op("t", lambda e, dl=dl, kt=kt, wk=wk: e.matmul(
                            ps_m[:, :], wk[kt][:, dl * PT:(dl + 1) * PT], hTc[kt][:, :],
                            start=(kt == 0), stop=(kt == KD - 1)),
                            ins=[wk[kt], hTc[kt]], outs=[ps_m])
                    elu1(kte[dl][:, c0:c0 + 512], kte[dl], bk_c[:, dg:dg + 1], bk_c)
                    for kt in range(KD):
                        P.op("t", lambda e, dl=dl, kt=kt, wv=wv: e.matmul(
                            ps_m[:, :], wv[kt][:, dl * PT:(dl + 1) * PT], hTc[kt][:, :],
                            start=(kt == 0), stop=(kt == KD - 1)),
                            ins=[wv[kt], hTc[kt]], outs=[ps_m])
                    P.op("v", lambda e, dl=dl, c0=c0, dg=dg, vT=vT: e.tensor_scalar(
                        vT[dl][:, c0:c0 + 512], ps_m[:, :], bv_c[:, dg:dg + 1],
                        None, ALU.add), ins=[ps_m, bv_c], outs=[vT[dl]])
            for dl in range(4):
                for nch in range(NCH):
                    c0 = nch * 512
                    rope(rkT[dl][:, c0:c0 + 512], rkT[dl], kte[dl][:, c0:c0 + 512],
                         kte[dl], c0)
            for hl in range(8):
                b = (hl % 2) * HD
                vh_sb = vh_all[hl]
                for tt in range(NTT):
                    P.op("t", lambda e, hl=hl, tt=tt, b=b, vT=vT: e.transpose(
                        ps_tv[:, :], vT[hl // 2][b:b + HD, tt * PT:(tt + 1) * PT],
                        cm.ident[b:b + HD, b:b + HD]),
                        ins=[vT[hl // 2], cm.ident], outs=[ps_tv])
                    P.op("v", lambda e, tt=tt, vh_sb=vh_sb: e.tensor_copy(
                        vh_sb[:, tt * HD:(tt + 1) * HD], ps_tv[:, :]),
                        ins=[ps_tv], outs=[vh_sb])
            P.barrier()
            vst.close()
            for qc in range(NCH):
                c0 = qc * 512
                for dl in range(4):
                    dq = hg * 4 + dl
                    P.dma("y", qe_c[dl][:, :],
                          sc_qte[dq * PT:(dq + 1) * PT, c0:c0 + 512], outs=[qe_c[dl]])
                    P.dma("y", rqc2[dl][:, :],
                          sc_rq[dq * PT:(dq + 1) * PT, c0:c0 + 512], outs=[rqc2[dl]])
                for hl in range(8):
                    b = (hl % 2) * HD
                    dl = hl // 2
                    vh_sb = vh_all[hl]
                    for kt in range(NTT):
                        P.op("t", lambda e, dl=dl, kt=kt, b=b, rkT=rkT,
                             rqc2=rqc2: e.matmul(
                            ps_n[:, :], rkT[dl][b:b + HD, kt * PT:(kt + 1) * PT],
                            rqc2[dl][b:b + HD, :], start=True, stop=True),
                            ins=[rkT[dl], rqc2[dl]], outs=[ps_n])
                        P.op("t", lambda e, dl=dl, kt=kt, b=b, kte=kte,
                             qe_c=qe_c: e.matmul(
                            ps_d[:, :], kte[dl][b:b + HD, kt * PT:(kt + 1) * PT],
                            qe_c[dl][b:b + HD, :], start=True, stop=True),
                            ins=[kte[dl], qe_c[dl]], outs=[ps_d])
                        P.op("v", lambda e: e.reciprocal(den_f[:, :], ps_d[:, :]),
                             ins=[ps_d], outs=[den_f])
                        P.op("v", lambda e: e.tensor_tensor(slab[:, :], ps_n[:, :],
                                                            den_f[:, :], ALU.mult),
                             ins=[ps_n, den_f], outs=[slab])
                        P.op("t", lambda e, kt=kt, vh_sb=vh_sb: e.matmul(
                            ps_y[:, :], vh_sb[:, kt * HD:(kt + 1) * HD], slab[:, :],
                            start=(kt == 0), stop=(kt == NTT - 1)),
                            ins=[vh_sb, slab], outs=[ps_y])
                    P.op("v", lambda e, dl=dl, b=b, yt_sb=yt_sb: e.tensor_copy(
                        yt_sb[dl][b:b + HD, :], ps_y[:, :]),
                        ins=[ps_y], outs=[yt_sb[dl]])
                for dl in range(4):
                    P.dma("g", sc_yt[hg * 512 + dl * PT:hg * 512 + (dl + 1) * PT,
                                     c0:c0 + 512], yt_sb[dl][:, :], ins=[yt_sb[dl]])
            P.barrier()
            gst.close()
        wout = load_tiles(P, st, "wout", wout_d, D, D, BF16)
        ps_b2 = ps(P, st, "ps_b2", [PT, 512], F32)
        bo_bc = bcast_row(P, st, "bo_bc", bout_d.rearrange("(a d) -> a d", a=1), D,
                          cm.ones_f, ps_b2)
        ytq = [sb(P, st, f"ytq{i}", [PT, PT], BF16) for i in range(KD)]
        x_res = sb(P, st, "x_res", [PT, D], F32)
        o_f = sb(P, st, "o_f", [PT, D], F32)
        ps_o = ps(P, st, "ps_o", [PT, 512], F32)
        for qt in range(NTT):
            for kt in range(KD):
                P.dma("y", ytq[kt][:, :], sc_yt[kt * PT:(kt + 1) * PT,
                                                qt * PT:(qt + 1) * PT], outs=[ytq[kt]])
            P.dma("y", x_res[:, :], x_d[qt * PT:(qt + 1) * PT, :], outs=[x_res])
            for n2 in range(2):
                for kt in range(KD):
                    P.op("t", lambda e, n2=n2, kt=kt: e.matmul(
                        ps_o[:, :], ytq[kt][:, :], wout[kt][:, n2 * 512:(n2 + 1) * 512],
                        start=(kt == 0), stop=(kt == KD - 1)),
                        ins=[ytq[kt], wout[kt]], outs=[ps_o])
                P.op("v", lambda e, n2=n2: e.tensor_tensor(
                    o_f[:, n2 * 512:(n2 + 1) * 512], ps_o[:, :],
                    bo_bc[:, n2 * 512:(n2 + 1) * 512], ALU.add),
                    ins=[ps_o, bo_bc], outs=[o_f])
            P.op("v", lambda e: e.tensor_add(o_f[:, :], o_f[:, :], x_res[:, :]),
                 ins=[o_f, x_res], outs=[o_f])
            P.dma("g", xo_d[qt * PT:(qt + 1) * PT, :], o_f[:, :], ins=[o_f])
        if standalone:
            P.fence_outputs()
            P.emit()
    return nc


def l1aa_inputs(inp, pr, cst, x):
    return {"x": x, "c_ident": cst["c_ident"], "c_cosp": cst["c_cosp"],
            "c_sinp": cst["c_sinp"], "c_pswap": cst["c_pswap"],
            "t_wqkv": pr["t_wqkv"], "t_wout1a": pr["t_wout1a"],
            "bqkvp": pr["_b_qkv_perm"], "b_out": np.asarray(inp["L1_attn_a_b_out"])}


def build_l1f(ctx=None):
    nc, P, x_d, xo_d, uid, standalone = _ctx(ctx)
    NHD = 8
    qwt_d = P.gd("t_qwt", [D, D], BF16)
    kwt_d = P.gd("t_kwt", [D, D], BF16)
    vwt_d = P.gd("t_vwt", [D, D], BF16)
    owt_d = P.gd("t_owt", [D, D], BF16)
    qb_d = P.gd("q_b", [D], F32)
    kb_d = P.gd("k_b", [D], F32)
    vb_d = P.gd("v_b", [D], F32)
    ob_d = P.gd("o_b", [D], F32)
    nw_d = P.gd("norm_w", [D], F32)
    nb_d = P.gd("norm_b", [D], F32)
    sc_ht = P.gd("sc_ht" + uid, [T // PT, D, PT], BF16, kind="Internal")
    sc_q = P.gd("sc_q" + uid, [D, T], BF16, kind="Internal")
    NCH, NTT = T // 512, T // PT
    with contextlib.ExitStack() as st:
        cm = Common(P, st)
        cp = cm.cp
        ps_bx = ps(P, st, "ps_bx", [PT, 512], F32)
        ob_bc = bcast_row(P, st, "ob_bc", ob_d.rearrange("(a d) -> a d", a=1), D,
                          cm.ones_f, ps_bx)
        nst = contextlib.ExitStack()
        gw_bc = bcast_row(P, nst, "gw_bc", nw_d.rearrange("(a d) -> a d", a=1), D,
                          cm.ones_f, ps_bx)
        gb_bc = bcast_row(P, nst, "gb_bc", nb_d.rearrange("(a d) -> a d", a=1), D,
                          cm.ones_f, ps_bx)
        make_hT(P, st, cm, x_d, sink_dram=sc_ht, mode="double_affine",
                gw_bc=gw_bc, gb_bc=gb_bc)
        P.barrier()
        nst.close()
        qb_c = load_cols(P, st, "qb_c", qb_d, KD)
        kb_c = load_cols(P, st, "kb_c", kb_d, KD)
        vb_row = sb(P, st, "vb_row", [1, D], F32)
        P.dma("y", vb_row[:, :], vb_d.rearrange("(a d) -> a d", a=1), outs=[vb_row])
        vb_bf = sb(P, st, "vb_bf", [1, D], BF16)
        P.op("v", lambda e: e.tensor_copy(vb_bf[:, :], vb_row[:, :]),
             ins=[vb_row], outs=[vb_bf])
        kT = [sb(P, st, f"kT{i}", [PT, T], BF16) for i in range(KD)]
        v_sb = [sb(P, st, f"v{i}", [PT, D], BF16) for i in range(NTT)]
        hTt = [sb(P, st, f"hTt{i}", [PT, PT], BF16) for i in range(KD)]
        ps_m = ps(P, st, "ps_m", [PT, 512], F32)
        for wpass in range(3):
            bst = contextlib.ExitStack()
            wname = ["kwt", "qwt", "vwt"][wpass]
            wd = [kwt_d, qwt_d, vwt_d][wpass]
            wt = load_tiles(P, bst, wname, wd, D, D, BF16)
            qstage = sb(P, bst, f"qstage{wpass}", [PT, PT], BF16)
            for tt in range(NTT):
                seg = tt // 8
                mk = 8 >> seg
                for kt in range(mk):
                    P.dma("y", hTt[kt][:, :], sc_ht[tt, kt * PT:(kt + 1) * PT, :],
                          outs=[hTt[kt]])
                if wpass == 0:
                    for dpo in range(KD):
                        for kt in range(mk):
                            P.op("t", lambda e, dpo=dpo, kt=kt, mk=mk, wt=wt: e.matmul(
                                ps_m[:, 0:PT], wt[kt][:, dpo * PT:(dpo + 1) * PT],
                                hTt[kt][:, :],
                                start=(kt == 0), stop=(kt == mk - 1)),
                                ins=[wt[kt], hTt[kt]], outs=[ps_m])
                        P.op("v", lambda e, dpo=dpo, tt=tt: e.tensor_scalar(
                            kT[dpo][:, tt * PT:(tt + 1) * PT], ps_m[:, 0:PT],
                            kb_c[:, dpo:dpo + 1], None, ALU.add),
                            ins=[ps_m, kb_c], outs=[kT[dpo]])
                elif wpass == 1:
                    for dpo in range(KD):
                        for kt in range(mk):
                            P.op("t", lambda e, dpo=dpo, kt=kt, mk=mk, wt=wt: e.matmul(
                                ps_m[:, 0:PT], wt[kt][:, dpo * PT:(dpo + 1) * PT],
                                hTt[kt][:, :],
                                start=(kt == 0), stop=(kt == mk - 1)),
                                ins=[wt[kt], hTt[kt]], outs=[ps_m])
                        P.op("v", lambda e, dpo=dpo, qstage=qstage: e.tensor_scalar(
                            qstage[:, :], ps_m[:, 0:PT],
                            qb_c[:, dpo:dpo + 1], None, ALU.add),
                            ins=[ps_m, qb_c], outs=[qstage])
                        P.dma("g", sc_q[dpo * PT:(dpo + 1) * PT, tt * PT:(tt + 1) * PT],
                              qstage[:, :], ins=[qstage])
                else:
                    for n2 in range(2):
                        for kt in range(mk):
                            P.op("t", lambda e, n2=n2, kt=kt, mk=mk, wt=wt: e.matmul(
                                ps_m[:, :], hTt[kt][:, :],
                                wt[kt][:, n2 * 512:(n2 + 1) * 512],
                                start=(kt == 0), stop=False),
                                ins=[hTt[kt], wt[kt]], outs=[ps_m])
                        P.op("t", lambda e, n2=n2: e.matmul(
                            ps_m[:, :], cm.ones_bf[0:1, :],
                            vb_bf[0:1, n2 * 512:(n2 + 1) * 512],
                            start=False, stop=True), ins=[cm.ones_bf, vb_bf], outs=[ps_m])
                        P.op("v", lambda e, n2=n2, tt=tt: e.tensor_copy(
                            v_sb[tt][:, n2 * 512:(n2 + 1) * 512], ps_m[:, :]),
                            ins=[ps_m], outs=[v_sb[tt]])
            P.barrier()
            bst.close()
        owt = load_tiles(P, st, "owt", owt_d, D, D, BF16)
        qc_sb = [sb(P, st, f"qc{i}", [PT, 512], BF16) for i in range(KD)]
        slab = sb(P, st, "slab", [PT, 512], BF16)
        linv = sb(P, st, "linv", [1, 512], F32)
        ytf = sb(P, st, "ytf", [PT, 512], F32)
        yt_sb = [sb(P, st, f"yts{i}", [PT, 512], BF16) for i in range(KD)]
        x_res = sb(P, st, "x_res", [PT, D], F32)
        ho_f = sb(P, st, "ho_f", [PT, D], F32)
        tqo = sb(P, st, "tqo", [PT, D], F32)
        tco = sb(P, st, "tco", [PT, 8], F32)
        ps_s = ps(P, st, "ps_s", [PT, 512], F32)
        ps_l = ps(P, st, "ps_l", [1, 512], F32)
        ps_lb = ps(P, st, "ps_lb", [PT, 512], F32)
        ps_yT = ps(P, st, "ps_yT", [PT, 512], F32)
        ps_o = ps(P, st, "ps_o", [PT, 512], F32)
        c_sc = cp.col(1.0 / 32.0)
        for qc in range(NCH):
            c0 = qc * 512
            for dp in range(KD):
                P.dma("y", qc_sb[dp][:, :], sc_q[dp * PT:(dp + 1) * PT, c0:c0 + 512],
                      outs=[qc_sb[dp]])
            for h in range(NHD):
                for kt in range(NTT):
                    P.op("t", lambda e, h=h, kt=kt: e.matmul(
                        ps_s[:, :], kT[h][:, kt * PT:(kt + 1) * PT], qc_sb[h][:, :],
                        start=True, stop=True), ins=[kT[h], qc_sb[h]], outs=[ps_s])
                    P.op("s", lambda e: e.activation(slab[:, :], ps_s[:, :], AF.Exp,
                                                     scale=c_sc),
                         ins=[ps_s, cp.tile], outs=[slab])
                    P.op("t", lambda e, kt=kt: e.matmul(
                        ps_l[:, :], cm.ones_bf[:, 0:1], slab[:, :],
                        start=(kt == 0), stop=(kt == NTT - 1)),
                        ins=[cm.ones_bf, slab], outs=[ps_l])
                    P.op("t", lambda e, h=h, kt=kt: e.matmul(
                        ps_yT[:, :], v_sb[kt][:, h * PT:(h + 1) * PT], slab[:, :],
                        start=(kt == 0), stop=(kt == NTT - 1)),
                        ins=[v_sb[kt], slab], outs=[ps_yT])
                P.op("v", lambda e: e.reciprocal(linv[:, :], ps_l[:, :]),
                     ins=[ps_l], outs=[linv])
                P.op("t", lambda e: e.matmul(ps_lb[:, :], cm.ones_f[0:1, :], linv[:, :],
                                             start=True, stop=True),
                     ins=[cm.ones_f, linv], outs=[ps_lb])
                P.op("s", lambda e: e.copy(ytf[:, :], ps_yT[:, :]), ins=[ps_yT], outs=[ytf])
                P.op("v", lambda e, h=h: e.tensor_tensor(yt_sb[h][:, :], ytf[:, :],
                                                         ps_lb[:, :], ALU.mult),
                     ins=[ytf, ps_lb], outs=[yt_sb[h]])
            seg = qc // 2
            m = D >> seg
            mk = KD >> seg
            for qt in range(4):
                row = c0 + qt * PT
                P.dma("y", x_res[:, :], x_d[row:row + PT, :], outs=[x_res])
                ln_tile(P, cp, x_res, ho_f, tco, tqo)
                P.op("v", lambda e: e.tensor_add(ho_f[:, :], ho_f[:, :], x_res[:, :]),
                     ins=[x_res, ho_f], outs=[ho_f])
                for n2 in range((m + 511) // 512):
                    w = min(512, m - n2 * 512)
                    for kt in range(mk):
                        P.op("t", lambda e, qt=qt, n2=n2, kt=kt, w=w, mk=mk: e.matmul(
                            ps_o[:, 0:w], yt_sb[kt][:, qt * PT:(qt + 1) * PT],
                            owt[kt][:, n2 * 512:n2 * 512 + w],
                            start=(kt == 0), stop=(kt == mk - 1)),
                            ins=[yt_sb[kt], owt[kt]], outs=[ps_o])
                    osl = ho_f[:, n2 * 512:n2 * 512 + w]
                    P.op("v", lambda e, osl=osl, w=w: e.scalar_tensor_tensor(
                        osl, ps_o[:, 0:w], 1.0, osl, ALU.mult, ALU.add),
                        ins=[ps_o, ho_f], outs=[ho_f])
                    obsl = ob_bc[:, n2 * 512:n2 * 512 + w]
                    P.op("v", lambda e, osl=osl, obsl=obsl: e.tensor_add(osl, osl, obsl),
                         ins=[ob_bc, ho_f], outs=[ho_f])
                P.dma("g", xo_d[row:row + PT, :], ho_f[:, :], ins=[ho_f])
        if standalone:
            P.fence_outputs()
            P.emit()
    return nc


def l1f_inputs(inp, pr, cst, x):
    return {"x": x, "c_ident": cst["c_ident"],
            "t_qwt": pr["t_qwt"], "t_kwt": pr["t_kwt"], "t_vwt": pr["t_vwt"],
            "t_owt": pr["t_owt"],
            "q_b": np.asarray(inp["L1_ffn_q_b"]), "k_b": np.asarray(inp["L1_ffn_k_b"]),
            "v_b": np.asarray(inp["L1_ffn_v_b"]), "o_b": np.asarray(inp["L1_ffn_o_b"]),
            "norm_w": np.asarray(inp["L1_ffn_norm_w"]),
            "norm_b": np.asarray(inp["L1_ffn_norm_b"])}



def build_l1ab(ctx=None):
    nc, P, x_d, xo_d, uid, standalone = _ctx(ctx)
    NH, HD = 16, 64
    qwt_d = P.gd("t_qwt2", [D, D], BF16)
    kwt_d = P.gd("t_kwt2", [D, D], BF16)
    vwt_d = P.gd("t_vwt2", [D, D], BF16)
    owt_d = P.gd("t_owt2", [D, D], BF16)
    ob_d = P.gd("out_b", [D], F32)
    hsel_d = P.gd("c_hsel", [16, D], BF16)
    sc_ht = P.gd("sc_ht" + uid, [T // PT, D, PT], BF16, kind="Internal")
    sc_q = P.gd("sc_q" + uid, [D, T], BF16, kind="Internal")
    NCH, NTT = T // 512, T // PT
    with contextlib.ExitStack() as st:
        cm = Common(P, st)
        cp = cm.cp
        make_hT(P, st, cm, x_d, sink_dram=sc_ht)
        hsel = sb(P, st, "hsel", [16, D], BF16)
        P.dma("y", hsel[:, :], hsel_d[:, :], outs=[hsel])
        kT = [sb(P, st, f"kT{i}", [PT, T], BF16) for i in range(KD)]
        v_sb = [sb(P, st, f"v{i}", [PT, D], BF16) for i in range(NTT)]
        hTt = [sb(P, st, f"hTt{i}", [PT, PT], BF16) for i in range(KD)]
        ps_m = ps(P, st, "ps_m", [PT, 512], F32)
        bst = contextlib.ExitStack()
        qwt = load_tiles(P, bst, "qwt", qwt_d, D, D, BF16)
        kwt = load_tiles(P, bst, "kwt", kwt_d, D, D, BF16)
        vwt = load_tiles(P, bst, "vwt", vwt_d, D, D, BF16)
        qstage = sb(P, bst, "qstage", [PT, PT], BF16)
        for tt in range(NTT):
            for kt in range(KD):
                P.dma("y", hTt[kt][:, :], sc_ht[tt, kt * PT:(kt + 1) * PT, :],
                      outs=[hTt[kt]])
            for dpo in range(KD):
                for kt in range(KD):
                    P.op("t", lambda e, dpo=dpo, kt=kt: e.matmul(
                        ps_m[:, 0:PT], kwt[kt][:, dpo * PT:(dpo + 1) * PT], hTt[kt][:, :],
                        start=(kt == 0), stop=(kt == KD - 1)),
                        ins=[kwt[kt], hTt[kt]], outs=[ps_m])
                P.op("v", lambda e, dpo=dpo, tt=tt: e.tensor_copy(
                    kT[dpo][:, tt * PT:(tt + 1) * PT], ps_m[:, 0:PT]),
                    ins=[ps_m], outs=[kT[dpo]])
                for kt in range(KD):
                    P.op("t", lambda e, dpo=dpo, kt=kt: e.matmul(
                        ps_m[:, 0:PT], qwt[kt][:, dpo * PT:(dpo + 1) * PT], hTt[kt][:, :],
                        start=(kt == 0), stop=(kt == KD - 1)),
                        ins=[qwt[kt], hTt[kt]], outs=[ps_m])
                P.op("v", lambda e, dpo=dpo: e.tensor_scalar_mul(
                    qstage[:, :], ps_m[:, 0:PT], 0.125), ins=[ps_m], outs=[qstage])
                P.dma("g", sc_q[dpo * PT:(dpo + 1) * PT, tt * PT:(tt + 1) * PT],
                      qstage[:, :], ins=[qstage])
            for n2 in range(2):
                for kt in range(KD):
                    P.op("t", lambda e, n2=n2, kt=kt: e.matmul(
                        ps_m[:, :], hTt[kt][:, :], vwt[kt][:, n2 * 512:(n2 + 1) * 512],
                        start=(kt == 0), stop=(kt == KD - 1)),
                        ins=[hTt[kt], vwt[kt]], outs=[ps_m])
                P.op("v", lambda e, n2=n2, tt=tt: e.tensor_copy(
                    v_sb[tt][:, n2 * 512:(n2 + 1) * 512], ps_m[:, :]),
                    ins=[ps_m], outs=[v_sb[tt]])
        P.barrier()
        bst.close()
        qt_sb = [sb(P, st, f"qt{i}", [PT, PT], BF16) for i in range(KD)]
        mc = sb(P, st, "mc", [PT, 12], F32)
        es = sb(P, st, "es", [PT, HD], F32)
        dinv = sb(P, st, "dinv", [PT, 16], F32)
        dinv_bf = sb(P, st, "dinv_bf", [PT, 16], BF16)
        dinvT = sb(P, st, "dinvT", [16, T], BF16)
        ps_s = ps(P, st, "ps_s", [PT, 512], F32)
        ps_tr = ps(P, st, "ps_tr", [16, PT], BF16)
        for qt in range(NTT):
            for dp in range(KD):
                P.dma("y", qt_sb[dp][:, :], sc_q[dp * PT:(dp + 1) * PT,
                                                 qt * PT:(qt + 1) * PT], outs=[qt_sb[dp]])
            for h in range(NH):
                dp, b = h // 2, (h % 2) * HD
                for kc in range(8):
                    P.op("t", lambda e, dp=dp, b=b, kc=kc: e.matmul(
                        ps_s[:, :], qt_sb[dp][b:b + HD, :],
                        kT[dp][b:b + HD, kc * 512:(kc + 1) * 512],
                        start=True, stop=True), ins=[qt_sb[dp], kT[dp]], outs=[ps_s])
                    if kc < 7:
                        P.op("v", lambda e, kc=kc: e.reduce_max(
                            mc[:, kc:kc + 1], ps_s[:, :], axis=AX.X), ins=[ps_s], outs=[mc])
                    else:
                        P.op("v", lambda e: e.reduce_max(
                            mc[:, 7:8], ps_s[:, 0:448], axis=AX.X), ins=[ps_s], outs=[mc])
                        P.op("v", lambda e: e.reduce_max(
                            mc[:, 8:9], ps_s[:, 448:512], axis=AX.X), ins=[ps_s], outs=[mc])
                P.op("v", lambda e: e.reduce_max(mc[:, 9:10], mc[:, 0:8], axis=AX.X),
                     ins=[mc], outs=[mc])
                P.op("v", lambda e: e.tensor_tensor(mc[:, 10:11], mc[:, 9:10], mc[:, 8:9],
                                                    ALU.max), ins=[mc], outs=[mc])
                P.op("v", lambda e: e.tensor_scalar_mul(mc[:, 11:12], mc[:, 10:11], -1.0),
                     ins=[mc], outs=[mc])
                P.op("s", lambda e: e.activation(es[:, :], ps_s[:, 448:512], AF.Exp,
                                                 bias=mc[:, 11:12]),
                     ins=[ps_s, mc], outs=[es])
                P.op("v", lambda e: e.reduce_sum(mc[:, 11:12], es[:, :], axis=AX.X),
                     ins=[es], outs=[mc])
                P.op("s", lambda e: e.activation(mc[:, 9:10], mc[:, 9:10], AF.Exp),
                     ins=[mc], outs=[mc])
                P.op("s", lambda e: e.activation(mc[:, 10:11], mc[:, 10:11], AF.Exp),
                     ins=[mc], outs=[mc])
                P.op("v", lambda e: e.tensor_tensor(mc[:, 9:10], mc[:, 9:10], mc[:, 10:11],
                                                    ALU.add), ins=[mc], outs=[mc])
                P.op("v", lambda e: e.tensor_tensor(mc[:, 9:10], mc[:, 9:10], mc[:, 11:12],
                                                    ALU.mult), ins=[mc], outs=[mc])
                P.op("v", lambda e, h=h: e.reciprocal(dinv[:, h:h + 1], mc[:, 9:10]),
                     ins=[mc], outs=[dinv])
            P.op("v", lambda e: e.tensor_copy(dinv_bf[:, :], dinv[:, :]),
                 ins=[dinv], outs=[dinv_bf])
            P.op("t", lambda e: e.transpose(ps_tr[:, :], dinv_bf[:, :], cm.ident[:, :]),
                 ins=[dinv_bf, cm.ident], outs=[ps_tr])
            P.op("v", lambda e, qt=qt: e.tensor_copy(dinvT[:, qt * PT:(qt + 1) * PT],
                                                     ps_tr[:, :]), ins=[ps_tr], outs=[dinvT])
        qc_sb = [sb(P, st, f"qc{i}", [PT, 512], BF16) for i in range(KD)]
        slab = sb(P, st, "slab", [PT, 512], BF16)
        ytf = sb(P, st, "ytf", [HD, 512], F32)
        yt_sb = [sb(P, st, f"yts{i}", [PT, 512], BF16) for i in range(KD)]
        x_res = sb(P, st, "x_res", [PT, D], F32)
        o_f = sb(P, st, "o_f", [PT, D], F32)
        ps_yT = ps(P, st, "ps_yT", [HD, 512], F32)
        ps_db = ps(P, st, "ps_db", [HD, 512], F32)
        ps_o = ps(P, st, "ps_o", [PT, 512], F32)
        owt = load_tiles(P, st, "owt", owt_d, D, D, BF16)
        ps_b2 = ps(P, st, "ps_b2", [PT, 512], F32)
        ob_bc = bcast_row(P, st, "ob_bc", ob_d.rearrange("(a d) -> a d", a=1), D,
                          cm.ones_f, ps_b2)
        for qc in range(NCH):
            c0 = qc * 512
            for dp in range(KD):
                P.dma("y", qc_sb[dp][:, :], sc_q[dp * PT:(dp + 1) * PT, c0:c0 + 512],
                      outs=[qc_sb[dp]])
            for h in range(NH):
                dp, b = h // 2, (h % 2) * HD
                for kt in range(NTT):
                    P.op("t", lambda e, dp=dp, b=b, kt=kt: e.matmul(
                        ps_s[:, :], kT[dp][b:b + HD, kt * PT:(kt + 1) * PT],
                        qc_sb[dp][b:b + HD, :], start=True, stop=True),
                        ins=[kT[dp], qc_sb[dp]], outs=[ps_s])
                    P.op("s", lambda e: e.activation(slab[:, :], ps_s[:, :], AF.Exp),
                         ins=[ps_s], outs=[slab])
                    P.op("t", lambda e, h=h, kt=kt: e.matmul(
                        ps_yT[:, :], v_sb[kt][:, h * HD:(h + 1) * HD], slab[:, :],
                        start=(kt == 0), stop=(kt == NTT - 1)),
                        ins=[v_sb[kt], slab], outs=[ps_yT])
                P.op("t", lambda e, h=h, c0=c0: e.matmul(
                    ps_db[:, :], hsel[:, h * HD:(h + 1) * HD], dinvT[:, c0:c0 + 512],
                    start=True, stop=True), ins=[hsel, dinvT], outs=[ps_db])
                P.op("s", lambda e: e.copy(ytf[:, :], ps_yT[:, :]), ins=[ps_yT], outs=[ytf])
                P.op("v", lambda e, dp=dp, b=b: e.tensor_tensor(
                    yt_sb[dp][b:b + HD, :], ytf[:, :], ps_db[:, :], ALU.mult),
                    ins=[ytf, ps_db], outs=[yt_sb[dp]])
            for qt in range(4):
                row = c0 + qt * PT
                P.dma("y", x_res[:, :], x_d[row:row + PT, :], outs=[x_res])
                for n2 in range(2):
                    for kt in range(KD):
                        P.op("t", lambda e, qt=qt, n2=n2, kt=kt: e.matmul(
                            ps_o[:, :], yt_sb[kt][:, qt * PT:(qt + 1) * PT],
                            owt[kt][:, n2 * 512:(n2 + 1) * 512],
                            start=(kt == 0), stop=(kt == KD - 1)),
                            ins=[yt_sb[kt], owt[kt]], outs=[ps_o])
                    P.op("v", lambda e, n2=n2: e.tensor_tensor(
                        o_f[:, n2 * 512:(n2 + 1) * 512], ps_o[:, :],
                        ob_bc[:, n2 * 512:(n2 + 1) * 512], ALU.add),
                        ins=[ps_o, ob_bc], outs=[o_f])
                P.op("v", lambda e: e.tensor_add(o_f[:, :], o_f[:, :], x_res[:, :]),
                     ins=[o_f, x_res], outs=[o_f])
                P.dma("g", xo_d[row:row + PT, :], o_f[:, :], ins=[o_f])
        if standalone:
            P.fence_outputs()
            P.emit()
    return nc


def l1ab_inputs(inp, pr, cst, x):
    return {"x": x, "c_ident": cst["c_ident"], "c_hsel": cst["c_hsel"],
            "t_qwt2": pr["t_qwt2"], "t_kwt2": pr["t_kwt2"], "t_vwt2": pr["t_vwt2"],
            "t_owt2": pr["t_owt2"], "out_b": np.asarray(inp["L1_attn_b_out_b"])}


# version 04
# speedup vs baseline: 1.1714x; 1.1714x over previous
import contextlib
import math

import numpy as np
import ml_dtypes

import concourse.bass as bass
import concourse.mybir as mybir
from concourse.bass_utils import run_bass_kernel_spmd

F32 = mybir.dt.float32
BF16 = mybir.dt.bfloat16
AF = mybir.ActivationFunctionType
ALU = mybir.AluOpType
AX = mybir.AxisListType

NCORES = 8
T = 4096
D = 1024
PT = 128
KD = D // PT

BF = ml_dtypes.bfloat16


ENG_SECTION = {"t": "tensor", "v": "vector", "s": "scalar", "g": "gpsimd", "y": "sync"}
ENGS = tuple(ENG_SECTION)


SEMSEG = 16384


class Prog:
    def __init__(self, nc):
        self.nc = nc
        self.q = {e: [] for e in ENGS}
        self.cnt = {e: 0 for e in ENGS}
        self.waited = {}
        self.wmap = {}
        self.rmap = {}
        self.sems = {}
        self._qtok = {}
        self.nuniq = 0
        self.drams = {}

    def uname(self, name):
        self.nuniq += 1
        return f"{name}_u{self.nuniq}"

    def gd(self, name, shape, dtype, kind="ExternalInput"):
        if name not in self.drams:
            self.drams[name] = self.nc.dram_tensor(name, list(shape), dtype,
                                                   kind=kind).ap()
        return self.drams[name]

    def _wait(self, engobj, te, tv):
        seg = (tv - 1) // SEMSEG
        engobj.wait_ge(self.sems[te][seg], tv - seg * SEMSEG)

    def _key(self, h):
        return id(h)

    def _deps(self, e, ins, outs):
        need = {}

        def add(te, tv):
            if te == e and e == "t":
                return
            if need.get(te, -1) < tv:
                need[te] = tv

        for h in ins:
            k = self._key(h)
            if k in self.wmap:
                add(*self.wmap[k])
        for h in outs:
            k = self._key(h)
            if k in self.wmap:
                add(*self.wmap[k])
            for te, tv in self.rmap.get(k, {}).items():
                add(te, tv)
        waits = []
        for te, tv in sorted(need.items()):
            if tv > self.waited.get((e, te), 0):
                waits.append((te, tv))
                self.waited[(e, te)] = tv
        return waits

    def op(self, e, fn, ins=(), outs=(), inc=1):
        waits = self._deps(e, ins, outs)
        self.cnt[e] += inc
        myval = self.cnt[e]

        def emit(engobj, waits=waits, fn=fn, e=e, inc=inc, myval=myval):
            for te, tv in waits:
                self._wait(engobj, te, tv)
            instr = fn(engobj)
            seg = (myval - 1) // SEMSEG
            instr.then_inc(self.sems[e][seg], inc)

        self.q[e].append(emit)
        for h in outs:
            k = self._key(h)
            self.wmap[k] = (e, myval)
            self.rmap[k] = {}
        for h in ins:
            self.rmap.setdefault(self._key(h), {})[e] = myval
        return myval

    def dma(self, e, out_ap, in_ap, ins=(), outs=(), slow=False):
        tok = self._qtok.setdefault(e, object())
        return self.op(e, lambda eng: eng.dma_start(
            out=out_ap, in_=in_ap, allow_slow_non_contiguous=slow),
            ins=ins, outs=list(outs) + [tok], inc=16)

    def barrier(self):
        snap = dict(self.cnt)
        for e in ENGS:
            waits = []
            for te, tv in sorted(snap.items()):
                if te != e and tv > 0 and tv > self.waited.get((e, te), 0):
                    waits.append((te, tv))
                    self.waited[(e, te)] = tv
            if waits:
                def emit(engobj, waits=waits):
                    for te, tv in waits:
                        self._wait(engobj, te, tv)
                self.q[e].append(emit)

    def fence_outputs(self):
        snap = dict(self.cnt)

        def emit(engobj, snap=snap):
            for te, tv in snap.items():
                nseg = (tv - 1) // SEMSEG + 1 if tv > 0 else 0
                for seg in range(nseg):
                    top = min(tv, (seg + 1) * SEMSEG)
                    self._wait(engobj, te, top)

        self.q["y"].append(emit)

    def emit(self):
        nc = self.nc
        with contextlib.ExitStack() as st:
            for e in ENGS:
                nseg = self.cnt[e] // SEMSEG + 1
                self.sems[e] = [st.enter_context(nc.semaphore(f"sem_{e}{i}"))
                                for i in range(nseg)]
            blk = st.enter_context(nc.Block())
            for e in ENGS:
                fns = self.q[e]
                if not fns:
                    continue
                section = getattr(blk, ENG_SECTION[e])

                def body(engobj, fns=fns):
                    for f in fns:
                        f(engobj)

                section(body)



def sb(P, st, name, shape, dtype):
    h = st.enter_context(P.nc.sbuf_tensor(P.uname(name), list(shape), dtype))
    return h


def ps(P, st, name, shape, dtype=F32):
    h = st.enter_context(P.nc.psum_tensor(P.uname(name), list(shape), dtype))
    return h


class CPool:

    def __init__(self, P, st, name="cpool", n=24):
        self.P = P
        self.tile = sb(P, st, name, [128, n], F32)
        self.map = {}
        self.n = n

    def col(self, val):
        val = float(val)
        if val not in self.map:
            i = len(self.map)
            assert i < self.n, "CPool full"
            self.map[val] = i
            self.P.op("v", lambda e, i=i, val=val: e.memset(self.tile[:, i:i + 1], val),
                      outs=[self.tile])
        return self.tile[:, self.map[val]:self.map[val] + 1]


def ln_tile(P, cp, x_t, h_t, tmp_col, tmp_sq, eps=1e-5, n=D):
    mean = tmp_col[:, 0:1]
    var = tmp_col[:, 1:2]
    rstd = tmp_col[:, 2:3]
    P.op("v", lambda e: e.reduce_sum(mean, x_t[:, :], axis=AX.X), ins=[x_t], outs=[tmp_col])
    P.op("v", lambda e: e.tensor_scalar_mul(mean, mean, 1.0 / n), ins=[tmp_col], outs=[tmp_col])
    P.op("v", lambda e: e.tensor_scalar(h_t[:, :], x_t[:, :], mean, None, ALU.subtract),
         ins=[x_t, tmp_col], outs=[h_t])
    P.op("s", lambda e: e.activation(tmp_sq[:, :], h_t[:, :], AF.Square),
         ins=[h_t], outs=[tmp_sq])
    P.op("v", lambda e: e.reduce_sum(var, tmp_sq[:, :], axis=AX.X), ins=[tmp_sq], outs=[tmp_col])
    c_eps, c_invn, c_half = cp.col(eps), cp.col(1.0 / n), cp.col(-0.5)
    P.op("s", lambda e: e.activation(var, var, AF.Ln, bias=c_eps, scale=c_invn),
         ins=[tmp_col, cp.tile], outs=[tmp_col])
    P.op("s", lambda e: e.activation(rstd, var, AF.Exp, scale=c_half),
         ins=[tmp_col, cp.tile], outs=[tmp_col])
    P.op("v", lambda e: e.tensor_scalar_mul(h_t[:, :], h_t[:, :], rstd),
         ins=[h_t, tmp_col], outs=[h_t])


def softplus_tile(P, cp, out, in_, tmp1, tmp2):
    P.op("s", lambda e: e.activation(tmp1[:, :], in_[:, :], AF.Abs),
         ins=[in_], outs=[tmp1])
    c_m1, c_1 = cp.col(-1.0), cp.col(1.0)
    P.op("s", lambda e: e.activation(tmp1[:, :], tmp1[:, :], AF.Exp, scale=c_m1),
         ins=[tmp1, cp.tile], outs=[tmp1])
    P.op("s", lambda e: e.activation(tmp1[:, :], tmp1[:, :], AF.Ln, bias=c_1),
         ins=[tmp1, cp.tile], outs=[tmp1])
    P.op("v", lambda e: e.tensor_scalar_max(tmp2[:, :], in_[:, :], 0.0),
         ins=[in_], outs=[tmp2])
    P.op("v", lambda e: e.tensor_add(out[:, :], tmp1[:, :], tmp2[:, :]),
         ins=[tmp1, tmp2], outs=[out])




def load_tiles(P, st, name, dram_ap, rows, cols, dtype, q="y"):
    nt = rows // PT
    tiles = []
    for i in range(nt):
        t = sb(P, st, f"{name}{i}", [PT, cols], dtype)
        P.dma(q, t[:, :], dram_ap[i * PT:(i + 1) * PT, :], outs=[t])
        tiles.append(t)
    return tiles


def load_cols(P, st, name, dram_1d, n, q="y"):
    t = sb(P, st, name, [PT, n], F32)
    v = dram_1d.rearrange("(a p) -> a p", p=PT)
    for j in range(n):
        P.dma(q, t[:, j:j + 1], v[j:j + 1].rearrange("a p -> p a"), outs=[t])
    return t


def bcast_row(P, st, name, row_ap, n, ones_f, psum_f, row_h=None):
    stage = sb(P, st, name + "_r", [1, n], F32)
    out = sb(P, st, name, [PT, n], F32)
    if row_h is None:
        P.dma("y", stage[:, :], row_ap, outs=[stage])
        src = stage
    else:
        src = row_ap
    srch = stage if row_h is None else row_h
    for j in range(0, n, 512):
        w = min(512, n - j)
        P.op("t", lambda e, j=j, w=w: e.matmul(psum_f[:, 0:w], ones_f[0:1, :],
                                               src[:, j:j + w], start=True, stop=True),
             ins=[ones_f, srch], outs=[psum_f])
        P.op("v", lambda e, j=j, w=w: e.tensor_copy(out[:, j:j + w], psum_f[:, 0:w]),
             ins=[psum_f], outs=[out])
    return out


class Common:

    def __init__(self, P, st):
        nc = P.nc
        self.cp = CPool(P, st)
        self.ident = sb(P, st, "c_ident_sb", [PT, PT], BF16)
        self.ones_f = sb(P, st, "c_ones_sb", [PT, PT], F32)
        self.ones_bf = sb(P, st, "c_onesbf_sb", [PT, PT], BF16)
        i_d = P.gd("c_ident", [PT, PT], BF16)
        P.dma("y", self.ident[:, :], i_d[:, :], outs=[self.ident])
        P.op("v", lambda e: e.memset(self.ones_f[:, :], 1.0), outs=[self.ones_f])
        P.op("v", lambda e: e.memset(self.ones_bf[:, :], 1.0), outs=[self.ones_bf])


def make_hT(P, st, cm, x_d, sink_dram=None, sink_tiles=None, mode="plain",
            gw_bc=None, gb_bc=None, tag="", router=None):
    cp = cm.cp
    ntt = T // PT
    st = contextlib.ExitStack()
    xt = sb(P, st, f"ht_x{tag}", [PT, D], F32)
    hf = sb(P, st, f"ht_h{tag}", [PT, D], F32)
    h2 = sb(P, st, f"ht_h2{tag}", [PT, D], F32) if mode == "double_affine" else None
    hbf = sb(P, st, f"ht_hbf{tag}", [PT, D], BF16)
    tsq = xt
    tcol = sb(P, st, f"ht_tcol{tag}", [PT, 8], F32)
    stage = sb(P, st, f"ht_stage{tag}", [PT, PT], BF16)
    ptr = [ps(P, st, f"ht_ps{tag}{i}", [PT, PT], BF16) for i in range(2)]
    for tt in range(ntt):
        P.dma("y", xt[:, :], x_d[tt * PT:(tt + 1) * PT, :], outs=[xt])
        ln_tile(P, cp, xt, hf, tcol, tsq)
        if mode == "double_affine":
            ln_tile(P, cp, hf, h2, tcol, tsq)
            P.op("v", lambda e: e.tensor_mul(h2[:, :], h2[:, :], gw_bc[:, :]),
                 ins=[h2, gw_bc], outs=[h2])
            P.op("v", lambda e: e.tensor_add(h2[:, :], h2[:, :], gb_bc[:, :]),
                 ins=[h2, gb_bc], outs=[h2])
            src = h2
        else:
            src = hf
        P.op("v", lambda e, src=src: e.tensor_copy(hbf[:, :], src[:, :]), ins=[src], outs=[hbf])
        for dp in range(KD):
            pp = ptr[dp % 2]
            P.op("t", lambda e, dp=dp, pp=pp: e.transpose(
                pp[:, :], hbf[:, dp * PT:(dp + 1) * PT], cm.ident[:, :]),
                ins=[hbf, cm.ident], outs=[pp])
            if sink_tiles is not None:
                dst = sink_tiles[dp]
                P.op("v", lambda e, dp=dp, pp=pp, dst=dst, tt=tt: e.tensor_copy(
                    dst[:, tt * PT:(tt + 1) * PT], pp[:, :]), ins=[pp], outs=[dst])
            else:
                P.op("v", lambda e, pp=pp: e.tensor_copy(stage[:, :], pp[:, :]),
                     ins=[pp], outs=[stage])
                P.dma("g", sink_dram[tt, dp * PT:(dp + 1) * PT, :], stage[:, :], ins=[stage])
                if router is not None:
                    wg_tiles, w_out, ps_r = router
                    P.op("t", lambda e, dp=dp, wgt=wg_tiles[dp], ps_r=ps_r: e.matmul(
                        ps_r[:, :], stage[:, :], wgt[:, :],
                        start=(dp == 0), stop=(dp == KD - 1)),
                        ins=[stage, wg_tiles[dp]], outs=[ps_r])
        if router is not None:
            _top2(P, cp, router[2], router[1], tt)
    P.barrier()
    st.close()


def _top2(P, cp, ps_r, w_out, tt):
    st_ = P._top2_scratch
    r, m1, m2, r2, eq1, eq2, e1, den = st_
    P.op("v", lambda e: e.tensor_copy(r[:, :], ps_r[:, 0:8]), ins=[ps_r], outs=[r])
    P.op("v", lambda e: e.reduce_max(m1[:, 0:1], r[:, :], axis=AX.X), ins=[r], outs=[m1])
    P.op("v", lambda e: e.tensor_scalar(eq1[:, :], r[:, :], m1[:, 0:1], None, ALU.is_equal),
         ins=[r, m1], outs=[eq1])
    P.op("v", lambda e: e.scalar_tensor_tensor(r2[:, :], eq1[:, :], -1e30, r[:, :],
                                               ALU.mult, ALU.add), ins=[eq1, r], outs=[r2])
    P.op("v", lambda e: e.reduce_max(m2[:, 0:1], r2[:, :], axis=AX.X), ins=[r2], outs=[m2])
    P.op("v", lambda e: e.tensor_scalar(eq2[:, :], r2[:, :], m2[:, 0:1], None, ALU.is_equal),
         ins=[r2, m2], outs=[eq2])
    negm1 = m1[:, 1:2]
    P.op("v", lambda e: e.tensor_scalar_mul(negm1, m1[:, 0:1], -1.0), ins=[m1], outs=[m1])
    P.op("s", lambda e: e.activation(e1[:, :], r[:, :], AF.Exp, bias=negm1),
         ins=[r, m1], outs=[e1])
    P.op("v", lambda e: e.tensor_scalar(den[:, 0:1], m2[:, 0:1], m1[:, 0:1], None, ALU.subtract),
         ins=[m2, m1], outs=[den])
    P.op("s", lambda e: e.activation(den[:, 0:1], den[:, 0:1], AF.Exp), ins=[den], outs=[den])
    P.op("v", lambda e: e.tensor_scalar_add(den[:, 0:1], den[:, 0:1], 1.0), ins=[den], outs=[den])
    P.op("v", lambda e: e.reciprocal(den[:, 1:2], den[:, 0:1]), ins=[den], outs=[den])
    P.op("v", lambda e: e.tensor_add(eq1[:, :], eq1[:, :], eq2[:, :]), ins=[eq1, eq2], outs=[eq1])
    P.op("v", lambda e: e.tensor_mul(e1[:, :], e1[:, :], eq1[:, :]), ins=[e1, eq1], outs=[e1])
    P.op("v", lambda e, tt=tt: e.tensor_scalar(w_out[:, tt * 8:tt * 8 + 8], e1[:, :],
                                               den[:, 1:2], None, ALU.mult),
         ins=[e1, den], outs=[w_out])


CAST_SPECS = [
    ("wot", 1024, 1024), ("wg", 1024, 8),
    ("w1", 8192, 1408), ("w2", 8192, 1408), ("w3", 11264, 1024),
    ("ws1", 1024, 2816), ("ws2", 1024, 2816), ("ws3", 2816, 1024),
    ("wint", 1024, 2048), ("wdtt", 1024, 1024), ("wxt", 1024, 32),
    ("woutt", 1024, 1024),
    ("wqkv", 1024, 3072), ("wout1a", 1024, 1024),
    ("qwt", 1024, 1024), ("kwt", 1024, 1024), ("vwt", 1024, 1024),
    ("owt", 1024, 1024),
    ("qwt2", 1024, 1024), ("kwt2", 1024, 1024), ("vwt2", 1024, 1024),
    ("owt2", 1024, 1024),
]




def _ctx(ctx):
    if ctx is None:
        nc = bass.Bass("TRN2", target_bir_lowering=False)
        P = Prog(nc)
        x_d = nc.dram_tensor("x", [T, D], F32, kind="ExternalInput").ap()
        xo_d = nc.dram_tensor("xo", [T, D], F32, kind="ExternalOutput").ap()
        return nc, P, x_d, xo_d, "", True
    nc, P, x_d, xo_d, uid = ctx
    return nc, P, x_d, xo_d, uid, False

def build_prep(ctx=None):
    nc, P, _, _, uid, standalone = _ctx(ctx)
    okind = "ExternalOutput" if standalone else "Internal"
    with contextlib.ExitStack() as st:
        bufs = [sb(P, st, f"cb{i}", [PT, 3072], F32) for i in range(2)]
        obufs = [sb(P, st, f"co{i}", [PT, 3072], BF16) for i in range(2)]
        for name, rows, cols in CAST_SPECS:
            i_ap = P.gd("p_" + name, [rows, cols], F32)
            o_ap = P.gd("t_" + name, [rows, cols], BF16, kind=okind)
            for i in range(rows // PT):
                b, o = bufs[i % 2], obufs[i % 2]
                P.dma("y", b[:, 0:cols], i_ap[i * PT:(i + 1) * PT, :], outs=[b])
                P.op("v", lambda e, b=b, o=o, cols=cols: e.tensor_copy(o[:, 0:cols], b[:, 0:cols]),
                     ins=[b], outs=[o])
                P.dma("g", o_ap[i * PT:(i + 1) * PT, :], o[:, 0:cols], ins=[o])
        NH, HD, RK = 16, 64, 32
        v_d = P.gd("p_v", [HD, RK], F32)
        ut_d = P.gd("p_u_t", [RK, HD], F32)
        ut_o = P.gd("t_ut", [RK, HD], BF16, kind=okind)
        v_sb = sb(P, st, "v_sb", [PT, RK], F32)
        vs_sb = sb(P, st, "vs_sb", [PT, RK], F32)
        ut_sb = sb(P, st, "ut_sbf", [RK, HD], F32)
        ut_bf = sb(P, st, "ut_bf", [RK, HD], BF16)
        P.dma("y", v_sb[0:HD, :], v_d[:, :], outs=[v_sb])
        P.dma("y", v_sb[HD:PT, :], v_d[:, :], outs=[v_sb])
        P.dma("y", ut_sb[:, :], ut_d[:, :], outs=[ut_sb])
        P.op("v", lambda e: e.tensor_scalar_mul(vs_sb[:, :], v_sb[:, :], 1.0 / math.sqrt(RK)),
             ins=[v_sb], outs=[vs_sb])
        P.op("v", lambda e: e.tensor_copy(ut_bf[:, :], ut_sb[:, :]), ins=[ut_sb], outs=[ut_bf])
        P.dma("g", ut_o[:, :], ut_bf[:, :], ins=[ut_bf])
        ps_f = ps(P, st, "ps_f", [PT, 512], F32)
        ps_b = ps(P, st, "ps_b", [RK, 16], F32)
        for wname, vv, bname, scaled, aug in [
            ("wq", vs_sb, "bq", True, False), ("wk", v_sb, "bk", False, False),
            ("wv", v_sb, "bv", False, True),
        ]:
            w_d = P.gd("p_" + wname, [D, D], F32)
            w_tiles = load_tiles(P, st, wname + "_t", w_d, D, D, F32)
            b_cols = load_cols(P, st, wname + "_b", P.gd("p_" + bname, [D], F32), KD)
            cw = 33 if aug else 32
            o_ap = P.gd("t_" + ("wvv" if aug else ("wqv" if wname == "wq" else "wkv")),
                        [D, NH * cw], BF16, kind=okind)
            stage = sb(P, st, wname + "_st", [PT, NH * cw], BF16)
            for ds_ in range(KD):
                P.op("v", lambda e, stage=stage: e.memset(stage[:, :], 0.0), outs=[stage])
                for h in range(NH):
                    base = (h % 2) * HD
                    lhs = w_tiles[h // 2][base:base + HD, ds_ * PT:(ds_ + 1) * PT]
                    P.op("t", lambda e, lhs=lhs, vv=vv, base=base: e.matmul(
                        ps_f[:, 0:RK], lhs, vv[base:base + HD, :], start=True, stop=True),
                        ins=[w_tiles[h // 2], vv], outs=[ps_f])
                    P.op("v", lambda e, h=h, cw=cw, stage=stage: e.tensor_copy(
                        stage[:, h * cw:h * cw + RK], ps_f[:, 0:RK]), ins=[ps_f], outs=[stage])
                P.dma("g", o_ap[ds_ * PT:(ds_ + 1) * PT, :], stage[:, :], ins=[stage])
            if aug:
                o9 = P.gd("t_wvv9", [1, NH * cw], BF16, kind=okind)
                r9 = sb(P, st, "wvv9_st", [1, NH * cw], BF16)
                for h in range(NH):
                    base = (h % 2) * HD
                    bcol = b_cols[base:base + HD, (h // 2):(h // 2) + 1]
                    P.op("t", lambda e, bcol=bcol, vv=vv, base=base: e.matmul(
                        ps_f[0:1, 0:RK],
                        bcol, vv[base:base + HD, :], start=True, stop=True),
                        ins=[b_cols, vv], outs=[ps_f])
                    P.op("v", lambda e, h=h, r9=r9, cw=cw: e.tensor_copy(r9[0:1, h * cw:h * cw + RK],
                                                           ps_f[0:1, 0:RK]),
                         ins=[ps_f], outs=[r9])
                    P.op("v", lambda e, h=h, r9=r9, cw=cw: e.memset(
                        r9[0:1, h * cw + RK:h * cw + RK + 1], 1.0),
                         outs=[r9])
                P.dma("g", o9[:, :], r9[:, :], ins=[r9])
            else:
                o_b = P.gd("t_b" + wname[1] + "v", [RK, NH], F32, kind=okind)
                bstage = sb(P, st, wname + "_bst", [RK, NH], F32)
                for h in range(NH):
                    base = (h % 2) * HD
                    bcol = b_cols[base:base + HD, (h // 2):(h // 2) + 1]
                    P.op("t", lambda e, bcol=bcol, vv=vv, base=base: e.matmul(
                        ps_b[:, 0:1], vv[base:base + HD, :], bcol, start=True, stop=True),
                        ins=[b_cols, vv], outs=[ps_b])
                    P.op("v", lambda e, h=h, bstage=bstage: e.tensor_copy(
                        bstage[:, h:h + 1], ps_b[:, 0:1]),
                         ins=[ps_b], outs=[bstage])
                P.dma("g", o_b[:, :], bstage[:, :], ins=[bstage])
        if standalone:
            P.fence_outputs()
            P.emit()
    return nc


def build_l0aa(ctx=None):
    nc, P, x_d, xo_d, uid, standalone = _ctx(ctx)
    NH, HD, RK = 16, 64, 32
    CW = RK + 1
    wqv_d = P.gd("t_wqv", [D, NH * RK], BF16)
    wkv_d = P.gd("t_wkv", [D, NH * RK], BF16)
    wvv_d = P.gd("t_wvv", [D, NH * CW], BF16)
    wvv9_d = P.gd("t_wvv9", [1, NH * CW], BF16)
    bqv_d = P.gd("t_bqv", [RK, NH], F32)
    bkv_d = P.gd("t_bkv", [RK, NH], F32)
    ut_d = P.gd("t_ut", [RK, HD], BF16)
    wot_d = P.gd("t_wot", [D, D], BF16)
    bo_d = P.gd("bo", [D], F32)
    with contextlib.ExitStack() as st:
        cm = Common(P, st)
        bqv = sb(P, st, "bqv", [RK, NH], F32)
        bkv = sb(P, st, "bkv", [RK, NH], F32)
        P.dma("y", bqv[:, :], bqv_d[:, :], outs=[bqv])
        P.dma("y", bkv[:, :], bkv_d[:, :], outs=[bkv])
        zq = [sb(P, st, f"zq{i}", [PT, T], BF16) for i in range(6)]
        zk = [sb(P, st, f"zk{i}", [PT, T], BF16) for i in range(6)]
        zv = [sb(P, st, f"zv{h}", [PT, 32 * CW], BF16) for h in range(NH)]
        NTT = T // PT
        NCH = T // 512
        zst = contextlib.ExitStack()
        hT = [sb(P, zst, f"hT{i}", [PT, T], BF16) for i in range(KD)]
        make_hT(P, zst, cm, x_d, sink_tiles=hT)
        ps_z = ps(P, st, "ps_z", [RK, 512], F32)
        ps_v = ps(P, st, "ps_v", [PT, CW], F32)
        for wname, w_dram, ztarget, b_sb in [
            ("wqv", wqv_d, zq, None), ("wkv", wkv_d, zk, None), ("wvv", wvv_d, zv, None),
        ]:
            wst = contextlib.ExitStack()
            if wname == "wvv":
                wtiles = load_tiles(P, wst, "wvv", wvv_d, D, NH * CW, BF16)
                wvv9 = sb(P, wst, "wvv9", [1, NH * CW], BF16)
                P.dma("y", wvv9[:, :], wvv9_d[:, :], outs=[wvv9])
                for h in range(NH):
                    for tt in range(NTT):
                        for kt in range(KD):
                            P.op("t", lambda e, h=h, tt=tt, kt=kt, wtiles=wtiles: e.matmul(
                                ps_v[:, :], hT[kt][:, tt * PT:(tt + 1) * PT],
                                wtiles[kt][:, h * CW:(h + 1) * CW],
                                start=(kt == 0), stop=False),
                                ins=[hT[kt], wtiles[kt]], outs=[ps_v])
                        P.op("t", lambda e, h=h, tt=tt, wvv9=wvv9: e.matmul(
                            ps_v[:, :], cm.ones_bf[0:1, :], wvv9[0:1, h * CW:(h + 1) * CW],
                            start=False, stop=True),
                            ins=[cm.ones_bf, wvv9], outs=[ps_v])
                        P.op("v", lambda e, h=h, tt=tt: e.tensor_copy(
                            zv[h][:, tt * CW:(tt + 1) * CW], ps_v[:, :]),
                            ins=[ps_v], outs=[zv[h]])
            else:
                wtiles = load_tiles(P, wst, wname, w_dram, D, NH * RK, BF16)
                bias = bqv if wname == "wqv" else bkv
                for h in range(NH):
                    pk, r0 = h // 3, (h % 3) * RK
                    for nch in range(NCH):
                        for kt in range(KD):
                            P.op("t", lambda e, h=h, nch=nch, kt=kt, wtiles=wtiles: e.matmul(
                                ps_z[:, :], wtiles[kt][:, h * RK:(h + 1) * RK],
                                hT[kt][:, nch * 512:(nch + 1) * 512],
                                start=(kt == 0), stop=(kt == KD - 1)),
                                ins=[wtiles[kt], hT[kt]], outs=[ps_z])
                        P.op("v", lambda e, nch=nch, pk=pk, r0=r0, ztarget=ztarget,
                             h=h, bias=bias: e.tensor_scalar(
                            ztarget[pk][r0:r0 + RK, nch * 512:(nch + 1) * 512], ps_z[:, :],
                            bias[:, h:h + 1], None, ALU.add),
                            ins=[ps_z, bias], outs=[ztarget[pk]])
            P.barrier()
            wst.close()
        P.barrier()
        zst.close()
        ut = sb(P, st, "ut", [RK, HD], BF16)
        P.dma("y", ut[:, :], ut_d[:, :], outs=[ut])
        wot = load_tiles(P, st, "wot", wot_d, D, D, BF16)
        ps_b = ps(P, st, "ps_b", [PT, 512], F32)
        bo_bc = bcast_row(P, st, "bo_bc", bo_d.rearrange("(a d) -> a d", a=1), D,
                          cm.ones_f, ps_b)
        ps_s = ps(P, st, "ps_s", [PT, 512], F32)
        ps_sA = [ps_s, ps_b]
        ps_y = ps(P, st, "ps_y", [CW, 512], F32)
        ps_lb = ps(P, st, "ps_lb", [RK, 512], F32)
        ps_u = ps(P, st, "ps_u", [HD, 512], F32)
        ps_o = ps(P, st, "ps_o", [PT, 512], F32)
        slabA = [sb(P, st, f"slab{i}", [PT, 512], BF16) for i in range(2)]
        rl_sb = sb(P, st, "rl_sb", [1, 512], F32)
        yr_f = sb(P, st, "yr_f", [RK, 512], F32)
        yrn = sb(P, st, "yrn", [RK, 512], BF16)
        yT = [sb(P, st, f"yT{i}", [PT, 512], BF16) for i in range(KD)]
        x_res = sb(P, st, "x_res", [PT, D], F32)
        o_f = sb(P, st, "o_f", [PT, D], F32)
        for qc in range(NCH):
            for h in range(NH):
                pk, r0 = h // 3, (h % 3) * RK
                for kt in range(NTT):
                    pss = ps_sA[kt % 2]
                    slab = slabA[kt % 2]
                    P.op("t", lambda e, h=h, qc=qc, kt=kt, pk=pk, r0=r0, pss=pss: e.matmul(
                        pss[:, :], zk[pk][r0:r0 + RK, kt * PT:(kt + 1) * PT],
                        zq[pk][r0:r0 + RK, qc * 512:(qc + 1) * 512],
                        start=True, stop=True), ins=[zk[pk], zq[pk]], outs=[pss])
                    P.op("s", lambda e, pss=pss, slab=slab: e.activation(
                        slab[:, :], pss[:, :], AF.Exp), ins=[pss], outs=[slab])
                    P.op("t", lambda e, h=h, kt=kt, slab=slab: e.matmul(
                        ps_y[:, :], zv[h][:, kt * CW:(kt + 1) * CW], slab[:, :],
                        start=(kt == 0), stop=(kt == NTT - 1)),
                        ins=[zv[h], slab], outs=[ps_y])
                P.op("v", lambda e: e.reciprocal(rl_sb[:, :], ps_y[RK:CW, :]),
                     ins=[ps_y], outs=[rl_sb])
                P.op("t", lambda e: e.matmul(ps_lb[:, :], cm.ones_f[0:1, 0:RK],
                                             rl_sb[:, :], start=True, stop=True),
                     ins=[cm.ones_f, rl_sb], outs=[ps_lb])
                P.op("s", lambda e: e.copy(yr_f[:, :], ps_y[0:RK, :]), ins=[ps_y], outs=[yr_f])
                P.op("v", lambda e: e.tensor_tensor(yrn[:, :], yr_f[:, :], ps_lb[:, :],
                                                    ALU.mult), ins=[yr_f, ps_lb], outs=[yrn])
                P.op("t", lambda e: e.matmul(ps_u[:, :], ut[:, :], yrn[:, :],
                                             start=True, stop=True), ins=[ut, yrn], outs=[ps_u])
                dp, rr = h // 2, (h % 2) * HD
                P.op("v", lambda e, dp=dp, rr=rr: e.tensor_copy(
                    yT[dp][rr:rr + HD, :], ps_u[:, :]), ins=[ps_u], outs=[yT[dp]])
            for qt in range(4):
                row = qc * 512 + qt * PT
                P.dma("y", x_res[:, :], x_d[row:row + PT, :], outs=[x_res])
                for n2 in range(2):
                    for kt in range(KD):
                        P.op("t", lambda e, qt=qt, n2=n2, kt=kt: e.matmul(
                            ps_o[:, :], yT[kt][:, qt * PT:(qt + 1) * PT],
                            wot[kt][:, n2 * 512:(n2 + 1) * 512],
                            start=(kt == 0), stop=(kt == KD - 1)),
                            ins=[yT[kt], wot[kt]], outs=[ps_o])
                    P.op("v", lambda e, n2=n2: e.tensor_tensor(
                        o_f[:, n2 * 512:(n2 + 1) * 512], ps_o[:, :],
                        bo_bc[:, n2 * 512:(n2 + 1) * 512], ALU.add),
                        ins=[ps_o, bo_bc], outs=[o_f])
                P.op("v", lambda e: e.tensor_add(o_f[:, :], o_f[:, :], x_res[:, :]),
                     ins=[o_f, x_res], outs=[o_f])
                P.dma("g", xo_d[row:row + PT, :], o_f[:, :], ins=[o_f])
        if standalone:
            P.fence_outputs()
            P.emit()
    return nc



def host_consts():
    ident = np.eye(PT, dtype=BF)
    half = 32
    theta = 1.0 / 10000 ** (np.arange(half, dtype=np.float64) / half)
    pos = np.arange(T, dtype=np.float64)
    ang = pos[None, :] * theta[:, None]
    cos32 = np.cos(ang).astype(np.float32)
    sin32 = np.sin(ang).astype(np.float32)
    cosp = np.zeros((PT, T), np.float32)
    sinp = np.zeros((PT, T), np.float32)
    for r in range(PT):
        f = r % half
        cosp[r] = cos32[f]
        sinp[r] = -sin32[f] if (r % 64) < half else sin32[f]
    swap = np.array([i + half if (i % 64) < half else i - half for i in range(PT)])
    psw = np.zeros((PT, PT), np.float32)
    for i in range(PT):
        psw[swap[i], i] = 1.0
    hsel = np.zeros((16, 1024), np.float32)
    for h in range(16):
        hsel[h, h * 64:(h + 1) * 64] = 1.0
    return {"c_ident": ident, "c_cosp": cosp.astype(BF), "c_sinp": sinp.astype(BF),
            "c_pswap": psw.astype(BF), "c_hsel": hsel.astype(BF)}


def prep_inputs(inp):
    tr = lambda a: np.ascontiguousarray(np.asarray(a).T)
    d = {
        "p_wot": tr(inp["L0_attn_a_Wo"]),
        "p_wg": np.asarray(inp["L0_ffn_Wg"]),
        "p_w1": np.asarray(inp["L0_ffn_W1"]).reshape(8192, 1408),
        "p_w2": np.asarray(inp["L0_ffn_W2"]).reshape(8192, 1408),
        "p_w3": np.asarray(inp["L0_ffn_W3"]).reshape(11264, 1024),
        "p_ws1": np.asarray(inp["L0_ffn_Ws1"]),
        "p_ws2": np.asarray(inp["L0_ffn_Ws2"]),
        "p_ws3": np.asarray(inp["L0_ffn_Ws3"]),
        "p_wint": tr(inp["L0_attn_b_W_in"]),
        "p_wdtt": tr(inp["L0_attn_b_W_dt"]),
        "p_wxt": tr(inp["L0_attn_b_W_x"]),
        "p_woutt": tr(inp["L0_attn_b_W_out"]),
        "p_wout1a": np.asarray(inp["L1_attn_a_w_out"]),
        "p_qwt": tr(inp["L1_ffn_q_w"]), "p_kwt": tr(inp["L1_ffn_k_w"]),
        "p_vwt": tr(inp["L1_ffn_v_w"]), "p_owt": tr(inp["L1_ffn_o_w"]),
        "p_qwt2": tr(inp["L1_attn_b_q_w"]), "p_kwt2": tr(inp["L1_attn_b_k_w"]),
        "p_vwt2": tr(inp["L1_attn_b_v_w"]), "p_owt2": tr(inp["L1_attn_b_out_w"]),
        "p_wq": np.asarray(inp["L0_attn_a_Wq"]), "p_wk": np.asarray(inp["L0_attn_a_Wk"]),
        "p_wv": np.asarray(inp["L0_attn_a_Wv"]),
        "p_bq": np.asarray(inp["L0_attn_a_bq"]), "p_bk": np.asarray(inp["L0_attn_a_bk"]),
        "p_bv": np.asarray(inp["L0_attn_a_bv"]),
        "p_v": np.asarray(inp["L0_attn_a_V"]),
        "p_u_t": tr(inp["L0_attn_a_U"]),
    }
    wqkv = np.asarray(inp["L1_attn_a_w_qkv"]).copy()
    bqkv = np.asarray(inp["L1_attn_a_b_qkv"]).copy()
    perm = np.arange(3072)
    for blk in range(2):
        for h in range(16):
            base = blk * 1024 + h * 64
            perm[base:base + 32] = blk * 1024 + h * 64 + np.arange(0, 64, 2)
            perm[base + 32:base + 64] = blk * 1024 + h * 64 + np.arange(1, 64, 2)
    d["p_wqkv"] = np.ascontiguousarray(wqkv[:, perm])
    d["_b_qkv_perm"] = bqkv[perm]
    return d


def l0aa_inputs(inp, pr, cst, x):
    return {
        "x": x, "c_ident": cst["c_ident"],
        "t_wqv": pr["t_wqv"], "t_wkv": pr["t_wkv"], "t_wvv": pr["t_wvv"],
        "t_wvv9": pr["t_wvv9"], "t_bqv": pr["t_bqv"], "t_bkv": pr["t_bkv"],
        "t_ut": pr["t_ut"], "t_wot": pr["t_wot"],
        "bo": np.asarray(inp["L0_attn_a_bo"]),
    }


def build_fused():
    nc = bass.Bass("TRN2", target_bir_lowering=False)
    P = Prog(nc)
    x_in = nc.dram_tensor("x", [T, D], F32, kind="ExternalInput").ap()
    xo_out = nc.dram_tensor("xo", [T, D], F32, kind="ExternalOutput").ap()
    build_prep(ctx=(nc, P, None, None, ""))
    P.barrier()
    seq = [build_l0aa, build_l0f, build_l0ab, build_l0f,
           build_l1aa, build_l1f, build_l1ab, build_l1f]
    cur = x_in
    for i, b in enumerate(seq):
        nxt = xo_out if i == len(seq) - 1 else P.gd(f"xi{i}", [T, D], F32,
                                                    kind="Internal")
        b(ctx=(nc, P, cur, nxt, f"_{i}"))
        P.barrier()
        cur = nxt
    P.fence_outputs()
    P.emit()
    return nc, P


def fused_inputs(inp, cst, pin, x):
    d = {k: v for k, v in pin.items() if not k.startswith("_")}
    d.update(cst)
    d["bqkvp"] = pin["_b_qkv_perm"]
    d["x"] = x
    for nm, key in [("bo", "L0_attn_a_bo"), ("gamma", "L0_attn_b_gamma"),
                    ("beta", "L0_attn_b_beta"), ("b_dt", "L0_attn_b_b_dt"),
                    ("a_log", "L0_attn_b_A_log"), ("d_par", "L0_attn_b_D_param"),
                    ("b_out", "L1_attn_a_b_out"), ("q_b", "L1_ffn_q_b"),
                    ("k_b", "L1_ffn_k_b"), ("v_b", "L1_ffn_v_b"),
                    ("o_b", "L1_ffn_o_b"), ("norm_w", "L1_ffn_norm_w"),
                    ("norm_b", "L1_ffn_norm_b"), ("out_b", "L1_attn_b_out_b")]:
        d[nm] = np.asarray(inp[key])
    d["conv_w"] = np.ascontiguousarray(np.asarray(inp["L0_attn_b_conv_w"])[:, 0, :])
    return d


def _launches():
    return [(build_l0aa, l0aa_inputs), (build_l0f, l0f_inputs),
            (build_l0ab, l0ab_inputs), (build_l0f, l0f_inputs),
            (build_l1aa, l1aa_inputs), (build_l1f, l1f_inputs),
            (build_l1ab, l1ab_inputs), (build_l1f, l1f_inputs)]


def kernel(**inp):
    return kernel_fused(**inp)


def kernel_multilaunch(**inp):
    assert np.asarray(inp["L1_ffn_router_prob"]).shape[-1] == 4
    cores = list(range(NCORES))
    x_full = np.ascontiguousarray(np.asarray(inp["L0_attn_a_x"], np.float32))
    xs = [np.ascontiguousarray(x_full[c]) for c in range(NCORES)]
    cst = host_consts()
    pin = prep_inputs(inp)
    prep_nc = build_prep()
    pr = run_bass_kernel_spmd(prep_nc, [{k: v for k, v in pin.items()
                                         if not k.startswith("_")}],
                              core_ids=[0]).results[0]
    pr = dict(pr)
    pr["_b_qkv_perm"] = pin["_b_qkv_perm"]
    nc_cache = {}
    for builder, mkin in _launches():
        if builder not in nc_cache:
            nc_cache[builder] = builder()
        nc = nc_cache[builder]
        in_maps = [mkin(inp, pr, cst, xs[c]) for c in range(NCORES)]
        rr = run_bass_kernel_spmd(nc, in_maps, core_ids=cores)
        xs = [np.ascontiguousarray(np.asarray(rr.results[c]["xo"])) for c in range(NCORES)]
    return np.stack(xs).astype(np.float32)


def kernel_fused(**inp):
    assert np.asarray(inp["L1_ffn_router_prob"]).shape[-1] == 4
    cores = list(range(NCORES))
    x_full = np.ascontiguousarray(np.asarray(inp["L0_attn_a_x"], np.float32))
    xs = [np.ascontiguousarray(x_full[c]) for c in range(NCORES)]
    cst = host_consts()
    pin = prep_inputs(inp)
    nc, _ = build_fused()
    in_maps = [fused_inputs(inp, cst, pin, xs[c]) for c in range(NCORES)]
    rr = run_bass_kernel_spmd(nc, in_maps, core_ids=cores)
    return np.stack([np.ascontiguousarray(np.asarray(rr.results[c]["xo"]))
                     for c in range(NCORES)]).astype(np.float32)


def build_l0f(ctx=None):
    nc, P, x_d, xo_d, uid, standalone = _ctx(ctx)
    E, HH, HS = 8, 1408, 2816
    NHH = HH // PT
    wg_d = P.gd("t_wg", [D, E], BF16)
    w1_d = P.gd("t_w1", [E * D, HH], BF16)
    w2_d = P.gd("t_w2", [E * D, HH], BF16)
    w3_d = P.gd("t_w3", [E * HH, D], BF16)
    ws1_d = P.gd("t_ws1", [D, HS], BF16)
    ws2_d = P.gd("t_ws2", [D, HS], BF16)
    ws3_d = P.gd("t_ws3", [HS, D], BF16)
    sc_ht = P.gd("sc_ht" + uid, [T // PT, D, PT], BF16, kind="Internal")
    NTT, NCH = T // PT, T // 512
    with contextlib.ExitStack() as st:
        cm = Common(P, st)
        cp = cm.cp
        wg = load_tiles(P, st, "wg", wg_d, D, E, BF16)
        w_sb = sb(P, st, "w_sb", [PT, NTT * E], F32)
        P._top2_scratch = (
            sb(P, st, "t2_r", [PT, E], F32), sb(P, st, "t2_m1", [PT, 2], F32),
            sb(P, st, "t2_m2", [PT, 1], F32), sb(P, st, "t2_r2", [PT, E], F32),
            sb(P, st, "t2_eq1", [PT, E], F32), sb(P, st, "t2_eq2", [PT, E], F32),
            sb(P, st, "t2_e1", [PT, E], F32), sb(P, st, "t2_den", [PT, 2], F32),
        )
        ps_r = ps(P, st, "ps_r", [PT, E], F32)
        make_hT(P, st, cm, x_d, sink_dram=sc_ht, router=(wg, w_sb, ps_r))
        yacc = [sb(P, st, f"yacc{i}", [PT, D], BF16) for i in range(NTT)]
        hTc = [sb(P, st, f"hTc{i}", [PT, 512], BF16) for i in range(KD)]
        het = [sb(P, st, f"het{i}", [PT, 512], BF16) for i in range(NHH)]
        g1sA = [sb(P, st, f"g1s{i}", [PT, 512], BF16) for i in range(2)]
        x_res = sb(P, st, "x_res", [PT, D], F32)
        o_f = sb(P, st, "o_f", [PT, D], F32)
        ps_g1A = [ps(P, st, f"ps_g1{i}", [PT, 512], F32) for i in range(2)]
        ps_g2A = [ps(P, st, f"ps_g2{i}", [PT, 512], F32) for i in range(2)]
        ps_yA = [ps(P, st, f"ps_y{i}", [PT, 512], F32) for i in range(2)]
        units = [("e", e) for e in range(E)] + [("s", 0), ("s", 1)]
        for ui, (kind, idx) in enumerate(units):
            wst = contextlib.ExitStack()
            if kind == "e":
                w1u = load_tiles(P, wst, f"w1u{ui}", w1_d[idx * D:(idx + 1) * D, :],
                                 D, HH, BF16)
                w2u = load_tiles(P, wst, f"w2u{ui}", w2_d[idx * D:(idx + 1) * D, :],
                                 D, HH, BF16)
                w3u = load_tiles(P, wst, f"w3u{ui}", w3_d[idx * HH:(idx + 1) * HH, :],
                                 HH, D, BF16)
            else:
                c0 = idx * HH
                w1u = load_tiles(P, wst, f"w1u{ui}", ws1_d[:, c0:c0 + HH], D, HH, BF16)
                w2u = load_tiles(P, wst, f"w2u{ui}", ws2_d[:, c0:c0 + HH], D, HH, BF16)
                w3u = load_tiles(P, wst, f"w3u{ui}", ws3_d[c0:c0 + HH, :], HH, D, BF16)
            for tc in range(NCH):
                for dp in range(KD):
                    for j in range(4):
                        tt = tc * 4 + j
                        P.dma("y", hTc[dp][:, j * PT:(j + 1) * PT],
                              sc_ht[tt, dp * PT:(dp + 1) * PT, :], outs=[hTc[dp]])
                for hh in range(NHH):
                    ps_g1 = ps_g1A[hh % 2]
                    ps_g2 = ps_g2A[hh % 2]
                    g1s = g1sA[hh % 2]
                    for kt in range(KD):
                        P.op("t", lambda e, hh=hh, kt=kt, w1u=w1u, ps_g1=ps_g1: e.matmul(
                            ps_g1[:, :], w1u[kt][:, hh * PT:(hh + 1) * PT], hTc[kt][:, :],
                            start=(kt == 0), stop=(kt == KD - 1)),
                            ins=[w1u[kt], hTc[kt]], outs=[ps_g1])
                    P.op("s", lambda e, ps_g1=ps_g1, g1s=g1s: e.activation(
                        g1s[:, :], ps_g1[:, :], AF.Silu), ins=[ps_g1], outs=[g1s])
                    for kt in range(KD):
                        P.op("t", lambda e, hh=hh, kt=kt, w2u=w2u, ps_g2=ps_g2: e.matmul(
                            ps_g2[:, :], w2u[kt][:, hh * PT:(hh + 1) * PT], hTc[kt][:, :],
                            start=(kt == 0), stop=(kt == KD - 1)),
                            ins=[w2u[kt], hTc[kt]], outs=[ps_g2])
                    P.op("v", lambda e, hh=hh, g1s=g1s, ps_g2=ps_g2: e.tensor_tensor(
                        het[hh][:, :], g1s[:, :], ps_g2[:, :], ALU.mult),
                        ins=[g1s, ps_g2], outs=[het[hh]])
                for j in range(4):
                    tt = tc * 4 + j
                    if ui == len(units) - 1:
                        P.dma("y", x_res[:, :], x_d[tt * PT:(tt + 1) * PT, :], outs=[x_res])
                    for n2 in range(2):
                        ps_y = ps_yA[n2]
                        for hh in range(NHH):
                            P.op("t", lambda e, j=j, n2=n2, hh=hh, w3u=w3u,
                                 ps_y=ps_y: e.matmul(
                                ps_y[:, :], het[hh][:, j * PT:(j + 1) * PT],
                                w3u[hh][:, n2 * 512:(n2 + 1) * 512],
                                start=(hh == 0), stop=(hh == NHH - 1)),
                                ins=[het[hh], w3u[hh]], outs=[ps_y])
                        ysl = yacc[tt][:, n2 * 512:(n2 + 1) * 512]
                        if kind == "e":
                            wcol = w_sb[:, tt * E + idx:tt * E + idx + 1]
                            if ui == 0:
                                P.op("v", lambda e, ysl=ysl, wcol=wcol,
                                     ps_y=ps_y: e.tensor_scalar(
                                    ysl, ps_y[:, :], wcol, None, ALU.mult),
                                    ins=[ps_y, w_sb], outs=[yacc[tt]])
                            else:
                                P.op("v", lambda e, ysl=ysl, wcol=wcol,
                                     ps_y=ps_y: e.scalar_tensor_tensor(
                                    ysl, ps_y[:, :], wcol, ysl, ALU.mult, ALU.add),
                                    ins=[ps_y, w_sb, yacc[tt]], outs=[yacc[tt]])
                        elif ui == len(units) - 2:
                            P.op("v", lambda e, ysl=ysl, ps_y=ps_y: e.scalar_tensor_tensor(
                                ysl, ps_y[:, :], 1.0, ysl, ALU.mult, ALU.add),
                                ins=[ps_y, yacc[tt]], outs=[yacc[tt]])
                        else:
                            osl = o_f[:, n2 * 512:(n2 + 1) * 512]
                            P.op("v", lambda e, ysl=ysl, osl=osl,
                                 ps_y=ps_y: e.scalar_tensor_tensor(
                                osl, ps_y[:, :], 1.0, ysl, ALU.mult, ALU.add),
                                ins=[ps_y, yacc[tt]], outs=[o_f])
                    if ui == len(units) - 1:
                        P.op("v", lambda e: e.tensor_add(o_f[:, :], o_f[:, :], x_res[:, :]),
                             ins=[o_f, x_res], outs=[o_f])
                        P.dma("g", xo_d[tt * PT:(tt + 1) * PT, :], o_f[:, :], ins=[o_f])
            P.barrier()
            wst.close()
        if standalone:
            P.fence_outputs()
            P.emit()
    return nc


def l0f_inputs(inp, pr, cst, x):
    return {"x": x, "c_ident": cst["c_ident"],
            "t_wg": pr["t_wg"], "t_w1": pr["t_w1"], "t_w2": pr["t_w2"],
            "t_w3": pr["t_w3"], "t_ws1": pr["t_ws1"], "t_ws2": pr["t_ws2"],
            "t_ws3": pr["t_ws3"]}


def build_l0ab(ctx=None):
    nc, P, x_d, xo_d, uid, standalone = _ctx(ctx)
    S = 16
    wint_d = P.gd("t_wint", [D, 2 * D], BF16)
    wdtt_d = P.gd("t_wdtt", [D, D], BF16)
    wxt_d = P.gd("t_wxt", [D, 2 * S], BF16)
    woutt_d = P.gd("t_woutt", [D, D], BF16)
    gam_d = P.gd("gamma", [D], F32)
    bet_d = P.gd("beta", [D], F32)
    bdt_d = P.gd("b_dt", [D], F32)
    alog_d = P.gd("a_log", [D, S], F32)
    dpar_d = P.gd("d_par", [D], F32)
    conv_d = P.gd("conv_w", [D, 3], F32)
    sc_ht = P.gd("sc_ht" + uid, [T // PT, D, PT], BF16, kind="Internal")
    sc_xx = P.gd("sc_xx" + uid, [D, T + 2], BF16, kind="Internal")
    sc_zs = P.gd("sc_zs" + uid, [D, T], BF16, kind="Internal")
    NCH = T // 512
    with contextlib.ExitStack() as st:
        cm = Common(P, st)
        cp = cm.cp
        ps_b = ps(P, st, "ps_bx", [PT, 512], F32)
        gw_bc = bcast_row(P, st, "gw_bc", gam_d.rearrange("(a d) -> a d", a=1), D,
                          cm.ones_f, ps_b)
        gb_bc = bcast_row(P, st, "gb_bc", bet_d.rearrange("(a d) -> a d", a=1), D,
                          cm.ones_f, ps_b)
        make_hT(P, st, cm, x_d, sink_dram=sc_ht, mode="double_affine",
                gw_bc=gw_bc, gb_bc=gb_bc)
        wdtt = load_tiles(P, st, "wdtt", wdtt_d, D, D, BF16)
        wxt = load_tiles(P, st, "wxt", wxt_d, D, 2 * S, BF16)
        woutt = load_tiles(P, st, "woutt", woutt_d, D, D, BF16)
        a_sbt = load_tiles(P, st, "a_sbt", alog_d, D, S, F32)
        for dp in range(KD):
            P.op("s", lambda e, dp=dp: e.activation(a_sbt[dp][:, :], a_sbt[dp][:, :], AF.Exp),
                 ins=[a_sbt[dp]], outs=[a_sbt[dp]])
            P.op("v", lambda e, dp=dp: e.tensor_scalar_mul(a_sbt[dp][:, :], a_sbt[dp][:, :],
                                                           -1.0), ins=[a_sbt[dp]], outs=[a_sbt[dp]])
        conv_c = load_tiles(P, st, "conv_c", conv_d, D, 3, F32)
        bdt_c = load_cols(P, st, "bdt_c", bdt_d, KD)
        dpar_c = load_cols(P, st, "dpar_c", dpar_d, KD)
        xzst = contextlib.ExitStack()
        wint = load_tiles(P, xzst, "wint", wint_d, D, 2 * D, BF16)
        hTc = [sb(P, xzst, f"hTc{i}", [PT, 512], BF16) for i in range(KD)]
        stage = sb(P, xzst, "stg", [PT, 512], BF16)
        zcol = sb(P, xzst, "zcol", [PT, 1], BF16)
        P.op("v", lambda e: e.memset(zcol[:, :], 0.0), outs=[zcol])
        ps_m = ps(P, st, "ps_m", [PT, 512], F32)
        for dp in range(KD):
            P.dma("g", sc_xx[dp * PT:(dp + 1) * PT, 0:1], zcol[:, :], ins=[zcol], slow=True)
            P.dma("g", sc_xx[dp * PT:(dp + 1) * PT, T + 1:T + 2], zcol[:, :], ins=[zcol],
                  slow=True)
        for nch in range(NCH):
            c0 = nch * 512
            for dp in range(KD):
                for j in range(4):
                    tt = nch * 4 + j
                    P.dma("y", hTc[dp][:, j * PT:(j + 1) * PT],
                          sc_ht[tt, dp * PT:(dp + 1) * PT, :], outs=[hTc[dp]])
            for ms in range(16):
                for kt in range(KD):
                    P.op("t", lambda e, ms=ms, kt=kt: e.matmul(
                        ps_m[:, :], wint[kt][:, ms * PT:(ms + 1) * PT], hTc[kt][:, :],
                        start=(kt == 0), stop=(kt == KD - 1)),
                        ins=[wint[kt], hTc[kt]], outs=[ps_m])
                if ms < 8:
                    P.op("v", lambda e: e.tensor_copy(stage[:, :], ps_m[:, :]),
                         ins=[ps_m], outs=[stage])
                    P.dma("g", sc_xx[ms * PT:(ms + 1) * PT, 1 + c0:1 + c0 + 512],
                          stage[:, :], ins=[stage])
                else:
                    P.op("s", lambda e: e.activation(stage[:, :], ps_m[:, :], AF.Silu),
                         ins=[ps_m], outs=[stage])
                    P.dma("g", sc_zs[(ms - 8) * PT:(ms - 7) * PT, c0:c0 + 512],
                          stage[:, :], ins=[stage])
        P.barrier()
        xzst.close()
        xxc = [sb(P, st, f"xxc{i}", [PT, 514], BF16) for i in range(KD)]
        xc_f = [sb(P, st, f"xcf{i}", [PT, 512], F32) for i in range(KD)]
        xc_bf = [sb(P, st, f"xcb{i}", [PT, 512], BF16) for i in range(KD)]
        dt_f = [sb(P, st, f"dtf{i}", [PT, 512], F32) for i in range(KD)]
        zs_bf = [sb(P, st, f"zsb{i}", [PT, 512], BF16) for i in range(KD)]
        y2_bf = [sb(P, st, f"y2b{i}", [PT, 512], BF16) for i in range(KD)]
        y_acc = [sb(P, st, f"yac{i}", [PT, 512], F32) for i in range(KD)]
        sp1 = sb(P, st, "sp1", [PT, 512], F32)
        sp2 = sb(P, st, "sp2", [PT, 512], F32)
        bc_sb = sb(P, st, "bc_sb", [32, 512], F32)
        brow = sb(P, st, "brow", [1, 512], F32)
        crow = sb(P, st, "crow", [1, 512], F32)
        a_t = sb(P, st, "a_t", [PT, 512], F32)
        b_t = sb(P, st, "b_t", [PT, 512], F32)
        h_t = sb(P, st, "h_t2", [PT, 512], F32)
        hc_t = sb(P, st, "hc_t", [PT, 512], F32)
        carry = sb(P, st, "carry", [PT, PT], F32)
        oT_sb = sb(P, st, "oT_sb", [PT, 512], BF16)
        out_f = [sb(P, st, f"outf{i}", [PT, D], F32) for i in range(4)]
        hof = sb(P, st, "hof", [PT, D], F32)
        tqo = sb(P, st, "tqo", [PT, D], F32)
        tco = sb(P, st, "tco", [PT, 8], F32)
        P.op("v", lambda e: e.memset(carry[:, :], 0.0), outs=[carry])
        ps_bc = ps(P, st, "ps_bc", [32, 512], F32)
        ps_bb = ps(P, st, "ps_bb", [PT, 512], F32)
        ps_cb = ps(P, st, "ps_cb", [PT, 512], F32)
        ps_tr = ps(P, st, "ps_tr", [PT, PT], BF16)
        for tc in range(NCH):
            c0 = tc * 512
            for dp in range(KD):
                P.dma("y", xxc[dp][:, :], sc_xx[dp * PT:(dp + 1) * PT, c0:c0 + 514],
                      outs=[xxc[dp]])
                P.dma("y", zs_bf[dp][:, :], sc_zs[dp * PT:(dp + 1) * PT, c0:c0 + 512],
                      outs=[zs_bf[dp]])
            for dp in range(KD):
                w0 = conv_c[dp][:, 0:1]
                w1 = conv_c[dp][:, 1:2]
                w2 = conv_c[dp][:, 2:3]
                P.op("v", lambda e, dp=dp, w1=w1: e.tensor_scalar(
                    xc_f[dp][:, :], xxc[dp][:, 1:513], w1, None, ALU.mult),
                    ins=[xxc[dp], conv_c[dp]], outs=[xc_f[dp]])
                P.op("v", lambda e, dp=dp, w0=w0: e.scalar_tensor_tensor(
                    xc_f[dp][:, :], xxc[dp][:, 0:512], w0, xc_f[dp][:, :],
                    ALU.mult, ALU.add), ins=[xxc[dp], conv_c[dp], xc_f[dp]], outs=[xc_f[dp]])
                P.op("v", lambda e, dp=dp, w2=w2: e.scalar_tensor_tensor(
                    xc_f[dp][:, :], xxc[dp][:, 2:514], w2, xc_f[dp][:, :],
                    ALU.mult, ALU.add), ins=[xxc[dp], conv_c[dp], xc_f[dp]], outs=[xc_f[dp]])
                P.op("s", lambda e, dp=dp: e.activation(xc_f[dp][:, :], xc_f[dp][:, :],
                                                        AF.Silu), ins=[xc_f[dp]], outs=[xc_f[dp]])
                P.op("v", lambda e, dp=dp: e.tensor_copy(xc_bf[dp][:, :], xc_f[dp][:, :]),
                     ins=[xc_f[dp]], outs=[xc_bf[dp]])
            for dp in range(KD):
                for kt in range(KD):
                    P.op("t", lambda e, dp=dp, kt=kt: e.matmul(
                        ps_m[:, :], wdtt[kt][:, dp * PT:(dp + 1) * PT], xc_bf[kt][:, :],
                        start=(kt == 0), stop=(kt == KD - 1)),
                        ins=[wdtt[kt], xc_bf[kt]], outs=[ps_m])
                P.op("v", lambda e, dp=dp: e.tensor_scalar(
                    dt_f[dp][:, :], ps_m[:, :], bdt_c[:, dp:dp + 1], None, ALU.add),
                    ins=[ps_m, bdt_c], outs=[dt_f[dp]])
                softplus_tile(P, cp, dt_f[dp], dt_f[dp], sp1, sp2)
            for kt in range(KD):
                P.op("t", lambda e, kt=kt: e.matmul(
                    ps_bc[:, :], wxt[kt][:, 0:32], xc_bf[kt][:, :],
                    start=(kt == 0), stop=(kt == KD - 1)),
                    ins=[wxt[kt], xc_bf[kt]], outs=[ps_bc])
            P.op("v", lambda e: e.tensor_copy(bc_sb[:, :], ps_bc[:, :]), ins=[ps_bc],
                 outs=[bc_sb])
            for s_ in range(S):
                P.dma("y", brow[:, :], bc_sb[s_:s_ + 1, :], ins=[bc_sb], outs=[brow])
                P.dma("y", crow[:, :], bc_sb[S + s_:S + s_ + 1, :], ins=[bc_sb], outs=[crow])
                P.op("t", lambda e: e.matmul(ps_bb[:, :], cm.ones_f[0:1, :], brow[:, :],
                                             start=True, stop=True),
                     ins=[cm.ones_f, brow], outs=[ps_bb])
                P.op("t", lambda e: e.matmul(ps_cb[:, :], cm.ones_f[0:1, :], crow[:, :],
                                             start=True, stop=True),
                     ins=[cm.ones_f, crow], outs=[ps_cb])
                for dp in range(KD):
                    acol = a_sbt[dp][:, s_:s_ + 1]
                    P.op("s", lambda e, dp=dp, acol=acol: e.activation(
                        a_t[:, :], dt_f[dp][:, :], AF.Exp, scale=acol),
                        ins=[dt_f[dp], a_sbt[dp]], outs=[a_t])
                    P.op("v", lambda e, dp=dp: e.tensor_tensor(
                        b_t[:, :], dt_f[dp][:, :], ps_bb[:, :], ALU.mult),
                        ins=[dt_f[dp], ps_bb], outs=[b_t])
                    ccol = carry[:, dp * S + s_:dp * S + s_ + 1]
                    P.op("v", lambda e, ccol=ccol: e.tensor_tensor_scan(
                        h_t[:, :], a_t[:, :], b_t[:, :], ccol, ALU.mult, ALU.add),
                        ins=[a_t, b_t, carry], outs=[h_t])
                    P.op("v", lambda e, ccol=ccol: e.tensor_copy(ccol, h_t[:, 511:512]),
                         ins=[h_t], outs=[carry])
                    P.op("v", lambda e: e.tensor_tensor(hc_t[:, :], h_t[:, :], ps_cb[:, :],
                                                        ALU.mult), ins=[h_t, ps_cb], outs=[hc_t])
                    if s_ == 0:
                        P.op("v", lambda e, dp=dp: e.tensor_copy(y_acc[dp][:, :], hc_t[:, :]),
                             ins=[hc_t], outs=[y_acc[dp]])
                    else:
                        P.op("v", lambda e, dp=dp: e.tensor_add(
                            y_acc[dp][:, :], y_acc[dp][:, :], hc_t[:, :]),
                            ins=[hc_t, y_acc[dp]], outs=[y_acc[dp]])
            for dp in range(KD):
                P.op("v", lambda e, dp=dp: e.scalar_tensor_tensor(
                    y_acc[dp][:, :], xc_f[dp][:, :], dpar_c[:, dp:dp + 1], y_acc[dp][:, :],
                    ALU.mult, ALU.add), ins=[xc_f[dp], dpar_c, y_acc[dp]], outs=[y_acc[dp]])
                P.op("v", lambda e, dp=dp: e.tensor_tensor(
                    y2_bf[dp][:, :], y_acc[dp][:, :], zs_bf[dp][:, :], ALU.mult),
                    ins=[y_acc[dp], zs_bf[dp]], outs=[y2_bf[dp]])
            for j in range(4):
                tt = tc * 4 + j
                P.dma("y", out_f[j][:, :], x_d[tt * PT:(tt + 1) * PT, :], outs=[out_f[j]])
                ln_tile(P, cp, out_f[j], hof, tco, tqo)
                P.op("v", lambda e, j=j: e.tensor_add(out_f[j][:, :], out_f[j][:, :],
                                                      hof[:, :]),
                     ins=[hof, out_f[j]], outs=[out_f[j]])
            for dpo in range(KD):
                for kt in range(KD):
                    P.op("t", lambda e, dpo=dpo, kt=kt: e.matmul(
                        ps_m[:, :], woutt[kt][:, dpo * PT:(dpo + 1) * PT], y2_bf[kt][:, :],
                        start=(kt == 0), stop=(kt == KD - 1)),
                        ins=[woutt[kt], y2_bf[kt]], outs=[ps_m])
                P.op("v", lambda e: e.tensor_copy(oT_sb[:, :], ps_m[:, :]),
                     ins=[ps_m], outs=[oT_sb])
                for j in range(4):
                    P.op("t", lambda e, j=j: e.transpose(
                        ps_tr[:, :], oT_sb[:, j * PT:(j + 1) * PT], cm.ident[:, :]),
                        ins=[oT_sb, cm.ident], outs=[ps_tr])
                    osl = out_f[j][:, dpo * PT:(dpo + 1) * PT]
                    P.op("v", lambda e, osl=osl, j=j: e.scalar_tensor_tensor(
                        osl, ps_tr[:, :], 1.0, osl, ALU.mult, ALU.add),
                        ins=[ps_tr, out_f[j]], outs=[out_f[j]])
            for j in range(4):
                tt = tc * 4 + j
                P.dma("g", xo_d[tt * PT:(tt + 1) * PT, :], out_f[j][:, :], ins=[out_f[j]])
        if standalone:
            P.fence_outputs()
            P.emit()
    return nc


def l0ab_inputs(inp, pr, cst, x):
    return {"x": x, "c_ident": cst["c_ident"],
            "t_wint": pr["t_wint"], "t_wdtt": pr["t_wdtt"], "t_wxt": pr["t_wxt"],
            "t_woutt": pr["t_woutt"],
            "gamma": np.asarray(inp["L0_attn_b_gamma"]),
            "beta": np.asarray(inp["L0_attn_b_beta"]),
            "b_dt": np.asarray(inp["L0_attn_b_b_dt"]),
            "a_log": np.asarray(inp["L0_attn_b_A_log"]),
            "d_par": np.asarray(inp["L0_attn_b_D_param"]),
            "conv_w": np.ascontiguousarray(np.asarray(inp["L0_attn_b_conv_w"])[:, 0, :])}


def build_l1aa(ctx=None):
    nc, P, x_d, xo_d, uid, standalone = _ctx(ctx)
    NH, HD = 16, 64
    wqkv_d = P.gd("t_wqkv", [D, 3 * D], BF16)
    wout_d = P.gd("t_wout1a", [D, D], BF16)
    bqkv_d = P.gd("bqkvp", [3 * D], F32)
    bout_d = P.gd("b_out", [D], F32)
    cosp_d = P.gd("c_cosp", [PT, T], BF16)
    sinp_d = P.gd("c_sinp", [PT, T], BF16)
    psw_d = P.gd("c_pswap", [PT, PT], BF16)
    sc_ht = P.gd("sc_ht" + uid, [T // PT, D, PT], BF16, kind="Internal")
    sc_qte = P.gd("sc_qte" + uid, [D, T], BF16, kind="Internal")
    sc_rq = P.gd("sc_rq" + uid, [D, T], BF16, kind="Internal")
    sc_yt = P.gd("sc_yt" + uid, [D, T], BF16, kind="Internal")
    NCH, NTT = T // 512, T // PT
    with contextlib.ExitStack() as st:
        cm = Common(P, st)
        make_hT(P, st, cm, x_d, sink_dram=sc_ht)
        cosp = sb(P, st, "cosp", [PT, T], BF16)
        sinp = sb(P, st, "sinp", [PT, T], BF16)
        P.dma("y", cosp[:, :], cosp_d[:, :], outs=[cosp])
        P.dma("y", sinp[:, :], sinp_d[:, :], outs=[sinp])
        psw = sb(P, st, "psw", [PT, PT], BF16)
        P.dma("y", psw[:, :], psw_d[:, :], outs=[psw])
        bq_c = load_cols(P, st, "bq_c", bqkv_d[0:D], KD)
        bk_c = load_cols(P, st, "bk_c", bqkv_d[D:2 * D], KD)
        bv_c = load_cols(P, st, "bv_c", bqkv_d[2 * D:3 * D], KD)
        hTc = [sb(P, st, f"hTc{i}", [PT, 512], BF16) for i in range(KD)]
        ps_m = ps(P, st, "ps_m", [PT, 512], F32)
        ps_sw = ps(P, st, "ps_sw", [PT, 512], F32)
        qraw = sb(P, st, "qraw", [PT, 512], F32)
        e1t = sb(P, st, "e1t", [PT, 512], F32)
        t2t = sb(P, st, "t2t", [PT, 512], F32)
        r1t = sb(P, st, "r1t", [PT, 512], F32)
        r2t = sb(P, st, "r2t", [PT, 512], F32)

        def elu1(dst_ap, dst_h, bias_col, bias_h):
            P.op("v", lambda e: e.tensor_scalar(qraw[:, :], ps_m[:, :], bias_col,
                                                None, ALU.add),
                 ins=[ps_m, bias_h], outs=[qraw])
            P.op("v", lambda e: e.tensor_scalar_min(e1t[:, :], qraw[:, :], 0.0),
                 ins=[qraw], outs=[e1t])
            P.op("s", lambda e: e.activation(e1t[:, :], e1t[:, :], AF.Exp),
                 ins=[e1t], outs=[e1t])
            P.op("v", lambda e: e.tensor_scalar_max(t2t[:, :], qraw[:, :], 0.0),
                 ins=[qraw], outs=[t2t])
            P.op("v", lambda e: e.tensor_add(dst_ap, t2t[:, :], e1t[:, :]),
                 ins=[t2t, e1t], outs=[dst_h])

        def rope(dst_ap, dst_h, src_ap, src_h, c0):
            P.op("t", lambda e: e.matmul(ps_sw[:, :], psw[:, :], src_ap,
                                         start=True, stop=True),
                 ins=[psw, src_h], outs=[ps_sw])
            P.op("v", lambda e, c0=c0: e.tensor_tensor(r1t[:, :], src_ap,
                                                       cosp[:, c0:c0 + 512], ALU.mult),
                 ins=[src_h, cosp], outs=[r1t])
            P.op("v", lambda e, c0=c0: e.tensor_tensor(r2t[:, :], ps_sw[:, :],
                                                       sinp[:, c0:c0 + 512], ALU.mult),
                 ins=[ps_sw, sinp], outs=[r2t])
            P.op("v", lambda e: e.tensor_add(dst_ap, r1t[:, :], r2t[:, :]),
                 ins=[r1t, r2t], outs=[dst_h])

        def load_htc(nch):
            for dp in range(KD):
                for j in range(4):
                    tt = nch * 4 + j
                    P.dma("y", hTc[dp][:, j * PT:(j + 1) * PT],
                          sc_ht[tt, dp * PT:(dp + 1) * PT, :], outs=[hTc[dp]])

        qst = contextlib.ExitStack()
        wq = load_tiles(P, qst, "wq", wqkv_d[:, 0:D], D, D, BF16)
        qte_c = [sb(P, qst, f"qtec{i}", [PT, 512], BF16) for i in range(KD)]
        rq_c = [sb(P, qst, f"rqc{i}", [PT, 512], BF16) for i in range(KD)]
        for nch in range(NCH):
            c0 = nch * 512
            load_htc(nch)
            for dp in range(KD):
                for kt in range(KD):
                    P.op("t", lambda e, dp=dp, kt=kt, wq=wq: e.matmul(
                        ps_m[:, :], wq[kt][:, dp * PT:(dp + 1) * PT], hTc[kt][:, :],
                        start=(kt == 0), stop=(kt == KD - 1)),
                        ins=[wq[kt], hTc[kt]], outs=[ps_m])
                elu1(qte_c[dp][:, :], qte_c[dp], bq_c[:, dp:dp + 1], bq_c)
                rope(rq_c[dp][:, :], rq_c[dp], qte_c[dp][:, :], qte_c[dp], c0)
                P.dma("g", sc_qte[dp * PT:(dp + 1) * PT, c0:c0 + 512], qte_c[dp][:, :],
                      ins=[qte_c[dp]])
                P.dma("g", sc_rq[dp * PT:(dp + 1) * PT, c0:c0 + 512], rq_c[dp][:, :],
                      ins=[rq_c[dp]])
        P.barrier()
        qst.close()
        den_fA = [sb(P, st, f"den_f{i}", [PT, 512], F32) for i in range(2)]
        slabA = [sb(P, st, f"slab{i}", [PT, 512], BF16) for i in range(2)]
        ps_n = ps(P, st, "ps_n", [PT, 512], F32)
        ps_d = ps(P, st, "ps_d", [PT, 512], F32)
        ps_nA = [ps_n, ps_m]
        ps_dA = [ps_d, ps_sw]
        ps_y = ps(P, st, "ps_y", [HD, 512], F32)
        ps_tv = ps(P, st, "ps_tv", [PT, HD], BF16)
        for hg in range(2):
            gst = contextlib.ExitStack()
            kte = [sb(P, gst, f"kte{hg}{i}", [PT, T], BF16) for i in range(4)]
            rkT = [sb(P, gst, f"rkT{hg}{i}", [PT, T], BF16) for i in range(4)]
            vh_all = [sb(P, gst, f"vh{hg}_{i}", [PT, NTT * HD], BF16) for i in range(8)]
            qe_c = [sb(P, gst, f"qec{hg}_{i}", [PT, 512], BF16) for i in range(4)]
            rqc2 = [sb(P, gst, f"rqc2{hg}_{i}", [PT, 512], BF16) for i in range(4)]
            yt_sb = [sb(P, gst, f"yt{hg}_{i}", [PT, 512], BF16) for i in range(4)]
            vst = contextlib.ExitStack()
            wk = load_tiles(P, vst, f"wk{hg}", wqkv_d[:, D + hg * 512:D + hg * 512 + 512],
                            D, 512, BF16)
            wv = load_tiles(P, vst, f"wv{hg}", wqkv_d[:, 2 * D + hg * 512:2 * D + hg * 512 + 512],
                            D, 512, BF16)
            vT = [sb(P, vst, f"vT{hg}{i}", [PT, T], BF16) for i in range(4)]
            for nch in range(NCH):
                c0 = nch * 512
                load_htc(nch)
                for dl in range(4):
                    dg = hg * 4 + dl
                    for kt in range(KD):
                        P.op("t", lambda e, dl=dl, kt=kt, wk=wk: e.matmul(
                            ps_m[:, :], wk[kt][:, dl * PT:(dl + 1) * PT], hTc[kt][:, :],
                            start=(kt == 0), stop=(kt == KD - 1)),
                            ins=[wk[kt], hTc[kt]], outs=[ps_m])
                    elu1(kte[dl][:, c0:c0 + 512], kte[dl], bk_c[:, dg:dg + 1], bk_c)
                    for kt in range(KD):
                        P.op("t", lambda e, dl=dl, kt=kt, wv=wv: e.matmul(
                            ps_m[:, :], wv[kt][:, dl * PT:(dl + 1) * PT], hTc[kt][:, :],
                            start=(kt == 0), stop=(kt == KD - 1)),
                            ins=[wv[kt], hTc[kt]], outs=[ps_m])
                    P.op("v", lambda e, dl=dl, c0=c0, dg=dg, vT=vT: e.tensor_scalar(
                        vT[dl][:, c0:c0 + 512], ps_m[:, :], bv_c[:, dg:dg + 1],
                        None, ALU.add), ins=[ps_m, bv_c], outs=[vT[dl]])
            for dl in range(4):
                for nch in range(NCH):
                    c0 = nch * 512
                    rope(rkT[dl][:, c0:c0 + 512], rkT[dl], kte[dl][:, c0:c0 + 512],
                         kte[dl], c0)
            for hl in range(8):
                b = (hl % 2) * HD
                vh_sb = vh_all[hl]
                for tt in range(NTT):
                    P.op("t", lambda e, hl=hl, tt=tt, b=b, vT=vT: e.transpose(
                        ps_tv[:, :], vT[hl // 2][b:b + HD, tt * PT:(tt + 1) * PT],
                        cm.ident[b:b + HD, b:b + HD]),
                        ins=[vT[hl // 2], cm.ident], outs=[ps_tv])
                    P.op("v", lambda e, tt=tt, vh_sb=vh_sb: e.tensor_copy(
                        vh_sb[:, tt * HD:(tt + 1) * HD], ps_tv[:, :]),
                        ins=[ps_tv], outs=[vh_sb])
            P.barrier()
            vst.close()
            for qc in range(NCH):
                c0 = qc * 512
                for dl in range(4):
                    dq = hg * 4 + dl
                    P.dma("y", qe_c[dl][:, :],
                          sc_qte[dq * PT:(dq + 1) * PT, c0:c0 + 512], outs=[qe_c[dl]])
                    P.dma("y", rqc2[dl][:, :],
                          sc_rq[dq * PT:(dq + 1) * PT, c0:c0 + 512], outs=[rqc2[dl]])
                for hl in range(8):
                    b = (hl % 2) * HD
                    dl = hl // 2
                    vh_sb = vh_all[hl]
                    for kt in range(NTT):
                        psn, psd = ps_nA[kt % 2], ps_dA[kt % 2]
                        den_f, slab = den_fA[kt % 2], slabA[kt % 2]
                        P.op("t", lambda e, dl=dl, kt=kt, b=b, rkT=rkT,
                             rqc2=rqc2, psn=psn: e.matmul(
                            psn[:, :], rkT[dl][b:b + HD, kt * PT:(kt + 1) * PT],
                            rqc2[dl][b:b + HD, :], start=True, stop=True),
                            ins=[rkT[dl], rqc2[dl]], outs=[psn])
                        P.op("t", lambda e, dl=dl, kt=kt, b=b, kte=kte,
                             qe_c=qe_c, psd=psd: e.matmul(
                            psd[:, :], kte[dl][b:b + HD, kt * PT:(kt + 1) * PT],
                            qe_c[dl][b:b + HD, :], start=True, stop=True),
                            ins=[kte[dl], qe_c[dl]], outs=[psd])
                        P.op("v", lambda e, psd=psd, den_f=den_f: e.reciprocal(
                            den_f[:, :], psd[:, :]), ins=[psd], outs=[den_f])
                        P.op("v", lambda e, psn=psn, den_f=den_f, slab=slab:
                             e.tensor_tensor(slab[:, :], psn[:, :],
                                             den_f[:, :], ALU.mult),
                             ins=[psn, den_f], outs=[slab])
                        P.op("t", lambda e, kt=kt, vh_sb=vh_sb, slab=slab: e.matmul(
                            ps_y[:, :], vh_sb[:, kt * HD:(kt + 1) * HD], slab[:, :],
                            start=(kt == 0), stop=(kt == NTT - 1)),
                            ins=[vh_sb, slab], outs=[ps_y])
                    P.op("v", lambda e, dl=dl, b=b, yt_sb=yt_sb: e.tensor_copy(
                        yt_sb[dl][b:b + HD, :], ps_y[:, :]),
                        ins=[ps_y], outs=[yt_sb[dl]])
                for dl in range(4):
                    P.dma("g", sc_yt[hg * 512 + dl * PT:hg * 512 + (dl + 1) * PT,
                                     c0:c0 + 512], yt_sb[dl][:, :], ins=[yt_sb[dl]])
            P.barrier()
            gst.close()
        wout = load_tiles(P, st, "wout", wout_d, D, D, BF16)
        ps_b2 = ps(P, st, "ps_b2", [PT, 512], F32)
        bo_bc = bcast_row(P, st, "bo_bc", bout_d.rearrange("(a d) -> a d", a=1), D,
                          cm.ones_f, ps_b2)
        ytq = [sb(P, st, f"ytq{i}", [PT, PT], BF16) for i in range(KD)]
        x_res = sb(P, st, "x_res", [PT, D], F32)
        o_f = sb(P, st, "o_f", [PT, D], F32)
        ps_o = ps(P, st, "ps_o", [PT, 512], F32)
        for qt in range(NTT):
            for kt in range(KD):
                P.dma("y", ytq[kt][:, :], sc_yt[kt * PT:(kt + 1) * PT,
                                                qt * PT:(qt + 1) * PT], outs=[ytq[kt]])
            P.dma("y", x_res[:, :], x_d[qt * PT:(qt + 1) * PT, :], outs=[x_res])
            for n2 in range(2):
                for kt in range(KD):
                    P.op("t", lambda e, n2=n2, kt=kt: e.matmul(
                        ps_o[:, :], ytq[kt][:, :], wout[kt][:, n2 * 512:(n2 + 1) * 512],
                        start=(kt == 0), stop=(kt == KD - 1)),
                        ins=[ytq[kt], wout[kt]], outs=[ps_o])
                P.op("v", lambda e, n2=n2: e.tensor_tensor(
                    o_f[:, n2 * 512:(n2 + 1) * 512], ps_o[:, :],
                    bo_bc[:, n2 * 512:(n2 + 1) * 512], ALU.add),
                    ins=[ps_o, bo_bc], outs=[o_f])
            P.op("v", lambda e: e.tensor_add(o_f[:, :], o_f[:, :], x_res[:, :]),
                 ins=[o_f, x_res], outs=[o_f])
            P.dma("g", xo_d[qt * PT:(qt + 1) * PT, :], o_f[:, :], ins=[o_f])
        if standalone:
            P.fence_outputs()
            P.emit()
    return nc


def l1aa_inputs(inp, pr, cst, x):
    return {"x": x, "c_ident": cst["c_ident"], "c_cosp": cst["c_cosp"],
            "c_sinp": cst["c_sinp"], "c_pswap": cst["c_pswap"],
            "t_wqkv": pr["t_wqkv"], "t_wout1a": pr["t_wout1a"],
            "bqkvp": pr["_b_qkv_perm"], "b_out": np.asarray(inp["L1_attn_a_b_out"])}


def build_l1f(ctx=None):
    nc, P, x_d, xo_d, uid, standalone = _ctx(ctx)
    NHD = 8
    qwt_d = P.gd("t_qwt", [D, D], BF16)
    kwt_d = P.gd("t_kwt", [D, D], BF16)
    vwt_d = P.gd("t_vwt", [D, D], BF16)
    owt_d = P.gd("t_owt", [D, D], BF16)
    qb_d = P.gd("q_b", [D], F32)
    kb_d = P.gd("k_b", [D], F32)
    vb_d = P.gd("v_b", [D], F32)
    ob_d = P.gd("o_b", [D], F32)
    nw_d = P.gd("norm_w", [D], F32)
    nb_d = P.gd("norm_b", [D], F32)
    sc_ht = P.gd("sc_ht" + uid, [T // PT, D, PT], BF16, kind="Internal")
    sc_q = P.gd("sc_q" + uid, [D, T], BF16, kind="Internal")
    NCH, NTT = T // 512, T // PT
    with contextlib.ExitStack() as st:
        cm = Common(P, st)
        cp = cm.cp
        ps_bx = ps(P, st, "ps_bx", [PT, 512], F32)
        ob_bc = bcast_row(P, st, "ob_bc", ob_d.rearrange("(a d) -> a d", a=1), D,
                          cm.ones_f, ps_bx)
        nst = contextlib.ExitStack()
        gw_bc = bcast_row(P, nst, "gw_bc", nw_d.rearrange("(a d) -> a d", a=1), D,
                          cm.ones_f, ps_bx)
        gb_bc = bcast_row(P, nst, "gb_bc", nb_d.rearrange("(a d) -> a d", a=1), D,
                          cm.ones_f, ps_bx)
        make_hT(P, st, cm, x_d, sink_dram=sc_ht, mode="double_affine",
                gw_bc=gw_bc, gb_bc=gb_bc)
        P.barrier()
        nst.close()
        qb_c = load_cols(P, st, "qb_c", qb_d, KD)
        kb_c = load_cols(P, st, "kb_c", kb_d, KD)
        vb_row = sb(P, st, "vb_row", [1, D], F32)
        P.dma("y", vb_row[:, :], vb_d.rearrange("(a d) -> a d", a=1), outs=[vb_row])
        vb_bf = sb(P, st, "vb_bf", [1, D], BF16)
        P.op("v", lambda e: e.tensor_copy(vb_bf[:, :], vb_row[:, :]),
             ins=[vb_row], outs=[vb_bf])
        kT = [sb(P, st, f"kT{i}", [PT, T], BF16) for i in range(KD)]
        v_sb = [sb(P, st, f"v{i}", [PT, D], BF16) for i in range(NTT)]
        hTt = [sb(P, st, f"hTt{i}", [PT, PT], BF16) for i in range(KD)]
        ps_m = ps(P, st, "ps_m", [PT, 512], F32)
        for wpass in range(3):
            bst = contextlib.ExitStack()
            wname = ["kwt", "qwt", "vwt"][wpass]
            wd = [kwt_d, qwt_d, vwt_d][wpass]
            wt = load_tiles(P, bst, wname, wd, D, D, BF16)
            qstage = sb(P, bst, f"qstage{wpass}", [PT, PT], BF16)
            for tt in range(NTT):
                seg = tt // 8
                mk = 8 >> seg
                for kt in range(mk):
                    P.dma("y", hTt[kt][:, :], sc_ht[tt, kt * PT:(kt + 1) * PT, :],
                          outs=[hTt[kt]])
                if wpass == 0:
                    for dpo in range(KD):
                        for kt in range(mk):
                            P.op("t", lambda e, dpo=dpo, kt=kt, mk=mk, wt=wt: e.matmul(
                                ps_m[:, 0:PT], wt[kt][:, dpo * PT:(dpo + 1) * PT],
                                hTt[kt][:, :],
                                start=(kt == 0), stop=(kt == mk - 1)),
                                ins=[wt[kt], hTt[kt]], outs=[ps_m])
                        P.op("v", lambda e, dpo=dpo, tt=tt: e.tensor_scalar(
                            kT[dpo][:, tt * PT:(tt + 1) * PT], ps_m[:, 0:PT],
                            kb_c[:, dpo:dpo + 1], None, ALU.add),
                            ins=[ps_m, kb_c], outs=[kT[dpo]])
                elif wpass == 1:
                    for dpo in range(KD):
                        for kt in range(mk):
                            P.op("t", lambda e, dpo=dpo, kt=kt, mk=mk, wt=wt: e.matmul(
                                ps_m[:, 0:PT], wt[kt][:, dpo * PT:(dpo + 1) * PT],
                                hTt[kt][:, :],
                                start=(kt == 0), stop=(kt == mk - 1)),
                                ins=[wt[kt], hTt[kt]], outs=[ps_m])
                        P.op("v", lambda e, dpo=dpo, qstage=qstage: e.tensor_scalar(
                            qstage[:, :], ps_m[:, 0:PT],
                            qb_c[:, dpo:dpo + 1], None, ALU.add),
                            ins=[ps_m, qb_c], outs=[qstage])
                        P.dma("g", sc_q[dpo * PT:(dpo + 1) * PT, tt * PT:(tt + 1) * PT],
                              qstage[:, :], ins=[qstage])
                else:
                    for n2 in range(2):
                        for kt in range(mk):
                            P.op("t", lambda e, n2=n2, kt=kt, mk=mk, wt=wt: e.matmul(
                                ps_m[:, :], hTt[kt][:, :],
                                wt[kt][:, n2 * 512:(n2 + 1) * 512],
                                start=(kt == 0), stop=False),
                                ins=[hTt[kt], wt[kt]], outs=[ps_m])
                        P.op("t", lambda e, n2=n2: e.matmul(
                            ps_m[:, :], cm.ones_bf[0:1, :],
                            vb_bf[0:1, n2 * 512:(n2 + 1) * 512],
                            start=False, stop=True), ins=[cm.ones_bf, vb_bf], outs=[ps_m])
                        P.op("v", lambda e, n2=n2, tt=tt: e.tensor_copy(
                            v_sb[tt][:, n2 * 512:(n2 + 1) * 512], ps_m[:, :]),
                            ins=[ps_m], outs=[v_sb[tt]])
            P.barrier()
            bst.close()
        owt = load_tiles(P, st, "owt", owt_d, D, D, BF16)
        qc_sb = [sb(P, st, f"qc{i}", [PT, 512], BF16) for i in range(KD)]
        slabA = [sb(P, st, f"slab{i}", [PT, 512], BF16) for i in range(2)]
        linv = sb(P, st, "linv", [1, 512], F32)
        ytf = sb(P, st, "ytf", [PT, 512], F32)
        yt_sb = [sb(P, st, f"yts{i}", [PT, 512], BF16) for i in range(KD)]
        x_res = sb(P, st, "x_res", [PT, D], F32)
        ho_f = sb(P, st, "ho_f", [PT, D], F32)
        tqo = sb(P, st, "tqo", [PT, D], F32)
        tco = sb(P, st, "tco", [PT, 8], F32)
        ps_s = ps(P, st, "ps_s", [PT, 512], F32)
        ps_sA = [ps_s, ps_m]
        ps_l = ps(P, st, "ps_l", [1, 512], F32)
        ps_lb = ps(P, st, "ps_lb", [PT, 512], F32)
        ps_yT = ps(P, st, "ps_yT", [PT, 512], F32)
        ps_o = ps(P, st, "ps_o", [PT, 512], F32)
        c_sc = cp.col(1.0 / 32.0)
        for qc in range(NCH):
            c0 = qc * 512
            for dp in range(KD):
                P.dma("y", qc_sb[dp][:, :], sc_q[dp * PT:(dp + 1) * PT, c0:c0 + 512],
                      outs=[qc_sb[dp]])
            for h in range(NHD):
                for kt in range(NTT):
                    pss = ps_sA[kt % 2]
                    slab = slabA[kt % 2]
                    P.op("t", lambda e, h=h, kt=kt, pss=pss: e.matmul(
                        pss[:, :], kT[h][:, kt * PT:(kt + 1) * PT], qc_sb[h][:, :],
                        start=True, stop=True), ins=[kT[h], qc_sb[h]], outs=[pss])
                    P.op("s", lambda e, pss=pss, slab=slab: e.activation(
                        slab[:, :], pss[:, :], AF.Exp, scale=c_sc),
                         ins=[pss, cp.tile], outs=[slab])
                    P.op("t", lambda e, kt=kt, slab=slab: e.matmul(
                        ps_l[:, :], cm.ones_bf[:, 0:1], slab[:, :],
                        start=(kt == 0), stop=(kt == NTT - 1)),
                        ins=[cm.ones_bf, slab], outs=[ps_l])
                    P.op("t", lambda e, h=h, kt=kt, slab=slab: e.matmul(
                        ps_yT[:, :], v_sb[kt][:, h * PT:(h + 1) * PT], slab[:, :],
                        start=(kt == 0), stop=(kt == NTT - 1)),
                        ins=[v_sb[kt], slab], outs=[ps_yT])
                P.op("v", lambda e: e.reciprocal(linv[:, :], ps_l[:, :]),
                     ins=[ps_l], outs=[linv])
                P.op("t", lambda e: e.matmul(ps_lb[:, :], cm.ones_f[0:1, :], linv[:, :],
                                             start=True, stop=True),
                     ins=[cm.ones_f, linv], outs=[ps_lb])
                P.op("s", lambda e: e.copy(ytf[:, :], ps_yT[:, :]), ins=[ps_yT], outs=[ytf])
                P.op("v", lambda e, h=h: e.tensor_tensor(yt_sb[h][:, :], ytf[:, :],
                                                         ps_lb[:, :], ALU.mult),
                     ins=[ytf, ps_lb], outs=[yt_sb[h]])
            seg = qc // 2
            m = D >> seg
            mk = KD >> seg
            for qt in range(4):
                row = c0 + qt * PT
                P.dma("y", x_res[:, :], x_d[row:row + PT, :], outs=[x_res])
                ln_tile(P, cp, x_res, ho_f, tco, tqo)
                P.op("v", lambda e: e.tensor_add(ho_f[:, :], ho_f[:, :], x_res[:, :]),
                     ins=[x_res, ho_f], outs=[ho_f])
                for n2 in range((m + 511) // 512):
                    w = min(512, m - n2 * 512)
                    for kt in range(mk):
                        P.op("t", lambda e, qt=qt, n2=n2, kt=kt, w=w, mk=mk: e.matmul(
                            ps_o[:, 0:w], yt_sb[kt][:, qt * PT:(qt + 1) * PT],
                            owt[kt][:, n2 * 512:n2 * 512 + w],
                            start=(kt == 0), stop=(kt == mk - 1)),
                            ins=[yt_sb[kt], owt[kt]], outs=[ps_o])
                    osl = ho_f[:, n2 * 512:n2 * 512 + w]
                    P.op("v", lambda e, osl=osl, w=w: e.scalar_tensor_tensor(
                        osl, ps_o[:, 0:w], 1.0, osl, ALU.mult, ALU.add),
                        ins=[ps_o, ho_f], outs=[ho_f])
                    obsl = ob_bc[:, n2 * 512:n2 * 512 + w]
                    P.op("v", lambda e, osl=osl, obsl=obsl: e.tensor_add(osl, osl, obsl),
                         ins=[ob_bc, ho_f], outs=[ho_f])
                P.dma("g", xo_d[row:row + PT, :], ho_f[:, :], ins=[ho_f])
        if standalone:
            P.fence_outputs()
            P.emit()
    return nc


def l1f_inputs(inp, pr, cst, x):
    return {"x": x, "c_ident": cst["c_ident"],
            "t_qwt": pr["t_qwt"], "t_kwt": pr["t_kwt"], "t_vwt": pr["t_vwt"],
            "t_owt": pr["t_owt"],
            "q_b": np.asarray(inp["L1_ffn_q_b"]), "k_b": np.asarray(inp["L1_ffn_k_b"]),
            "v_b": np.asarray(inp["L1_ffn_v_b"]), "o_b": np.asarray(inp["L1_ffn_o_b"]),
            "norm_w": np.asarray(inp["L1_ffn_norm_w"]),
            "norm_b": np.asarray(inp["L1_ffn_norm_b"])}



def build_l1ab(ctx=None):
    nc, P, x_d, xo_d, uid, standalone = _ctx(ctx)
    NH, HD = 16, 64
    qwt_d = P.gd("t_qwt2", [D, D], BF16)
    kwt_d = P.gd("t_kwt2", [D, D], BF16)
    vwt_d = P.gd("t_vwt2", [D, D], BF16)
    owt_d = P.gd("t_owt2", [D, D], BF16)
    ob_d = P.gd("out_b", [D], F32)
    hsel_d = P.gd("c_hsel", [16, D], BF16)
    sc_ht = P.gd("sc_ht" + uid, [T // PT, D, PT], BF16, kind="Internal")
    sc_q = P.gd("sc_q" + uid, [D, T], BF16, kind="Internal")
    NCH, NTT = T // 512, T // PT
    with contextlib.ExitStack() as st:
        cm = Common(P, st)
        cp = cm.cp
        make_hT(P, st, cm, x_d, sink_dram=sc_ht)
        hsel = sb(P, st, "hsel", [16, D], BF16)
        P.dma("y", hsel[:, :], hsel_d[:, :], outs=[hsel])
        kT = [sb(P, st, f"kT{i}", [PT, T], BF16) for i in range(KD)]
        v_sb = [sb(P, st, f"v{i}", [PT, D], BF16) for i in range(NTT)]
        hTt = [sb(P, st, f"hTt{i}", [PT, PT], BF16) for i in range(KD)]
        ps_m = ps(P, st, "ps_m", [PT, 512], F32)
        bst = contextlib.ExitStack()
        qwt = load_tiles(P, bst, "qwt", qwt_d, D, D, BF16)
        kwt = load_tiles(P, bst, "kwt", kwt_d, D, D, BF16)
        vwt = load_tiles(P, bst, "vwt", vwt_d, D, D, BF16)
        qstage = sb(P, bst, "qstage", [PT, PT], BF16)
        for tt in range(NTT):
            for kt in range(KD):
                P.dma("y", hTt[kt][:, :], sc_ht[tt, kt * PT:(kt + 1) * PT, :],
                      outs=[hTt[kt]])
            for dpo in range(KD):
                for kt in range(KD):
                    P.op("t", lambda e, dpo=dpo, kt=kt: e.matmul(
                        ps_m[:, 0:PT], kwt[kt][:, dpo * PT:(dpo + 1) * PT], hTt[kt][:, :],
                        start=(kt == 0), stop=(kt == KD - 1)),
                        ins=[kwt[kt], hTt[kt]], outs=[ps_m])
                P.op("v", lambda e, dpo=dpo, tt=tt: e.tensor_copy(
                    kT[dpo][:, tt * PT:(tt + 1) * PT], ps_m[:, 0:PT]),
                    ins=[ps_m], outs=[kT[dpo]])
                for kt in range(KD):
                    P.op("t", lambda e, dpo=dpo, kt=kt: e.matmul(
                        ps_m[:, 0:PT], qwt[kt][:, dpo * PT:(dpo + 1) * PT], hTt[kt][:, :],
                        start=(kt == 0), stop=(kt == KD - 1)),
                        ins=[qwt[kt], hTt[kt]], outs=[ps_m])
                P.op("v", lambda e, dpo=dpo: e.tensor_scalar_mul(
                    qstage[:, :], ps_m[:, 0:PT], 0.125), ins=[ps_m], outs=[qstage])
                P.dma("g", sc_q[dpo * PT:(dpo + 1) * PT, tt * PT:(tt + 1) * PT],
                      qstage[:, :], ins=[qstage])
            for n2 in range(2):
                for kt in range(KD):
                    P.op("t", lambda e, n2=n2, kt=kt: e.matmul(
                        ps_m[:, :], hTt[kt][:, :], vwt[kt][:, n2 * 512:(n2 + 1) * 512],
                        start=(kt == 0), stop=(kt == KD - 1)),
                        ins=[hTt[kt], vwt[kt]], outs=[ps_m])
                P.op("v", lambda e, n2=n2, tt=tt: e.tensor_copy(
                    v_sb[tt][:, n2 * 512:(n2 + 1) * 512], ps_m[:, :]),
                    ins=[ps_m], outs=[v_sb[tt]])
        P.barrier()
        bst.close()
        qt_sb = [sb(P, st, f"qt{i}", [PT, PT], BF16) for i in range(KD)]
        mc = sb(P, st, "mc", [PT, 12], F32)
        es = sb(P, st, "es", [PT, HD], F32)
        dinv = sb(P, st, "dinv", [PT, 16], F32)
        dinv_bf = sb(P, st, "dinv_bf", [PT, 16], BF16)
        dinvT = sb(P, st, "dinvT", [16, T], BF16)
        ps_s = ps(P, st, "ps_s", [PT, 512], F32)
        ps_tr = ps(P, st, "ps_tr", [16, PT], BF16)
        for qt in range(NTT):
            for dp in range(KD):
                P.dma("y", qt_sb[dp][:, :], sc_q[dp * PT:(dp + 1) * PT,
                                                 qt * PT:(qt + 1) * PT], outs=[qt_sb[dp]])
            for h in range(NH):
                dp, b = h // 2, (h % 2) * HD
                for kc in range(8):
                    P.op("t", lambda e, dp=dp, b=b, kc=kc: e.matmul(
                        ps_s[:, :], qt_sb[dp][b:b + HD, :],
                        kT[dp][b:b + HD, kc * 512:(kc + 1) * 512],
                        start=True, stop=True), ins=[qt_sb[dp], kT[dp]], outs=[ps_s])
                    if kc < 7:
                        P.op("v", lambda e, kc=kc: e.reduce_max(
                            mc[:, kc:kc + 1], ps_s[:, :], axis=AX.X), ins=[ps_s], outs=[mc])
                    else:
                        P.op("v", lambda e: e.reduce_max(
                            mc[:, 7:8], ps_s[:, 0:448], axis=AX.X), ins=[ps_s], outs=[mc])
                        P.op("v", lambda e: e.reduce_max(
                            mc[:, 8:9], ps_s[:, 448:512], axis=AX.X), ins=[ps_s], outs=[mc])
                P.op("v", lambda e: e.reduce_max(mc[:, 9:10], mc[:, 0:8], axis=AX.X),
                     ins=[mc], outs=[mc])
                P.op("v", lambda e: e.tensor_tensor(mc[:, 10:11], mc[:, 9:10], mc[:, 8:9],
                                                    ALU.max), ins=[mc], outs=[mc])
                P.op("v", lambda e: e.tensor_scalar_mul(mc[:, 11:12], mc[:, 10:11], -1.0),
                     ins=[mc], outs=[mc])
                P.op("s", lambda e: e.activation(es[:, :], ps_s[:, 448:512], AF.Exp,
                                                 bias=mc[:, 11:12]),
                     ins=[ps_s, mc], outs=[es])
                P.op("v", lambda e: e.reduce_sum(mc[:, 11:12], es[:, :], axis=AX.X),
                     ins=[es], outs=[mc])
                P.op("s", lambda e: e.activation(mc[:, 9:10], mc[:, 9:10], AF.Exp),
                     ins=[mc], outs=[mc])
                P.op("s", lambda e: e.activation(mc[:, 10:11], mc[:, 10:11], AF.Exp),
                     ins=[mc], outs=[mc])
                P.op("v", lambda e: e.tensor_tensor(mc[:, 9:10], mc[:, 9:10], mc[:, 10:11],
                                                    ALU.add), ins=[mc], outs=[mc])
                P.op("v", lambda e: e.tensor_tensor(mc[:, 9:10], mc[:, 9:10], mc[:, 11:12],
                                                    ALU.mult), ins=[mc], outs=[mc])
                P.op("v", lambda e, h=h: e.reciprocal(dinv[:, h:h + 1], mc[:, 9:10]),
                     ins=[mc], outs=[dinv])
            P.op("v", lambda e: e.tensor_copy(dinv_bf[:, :], dinv[:, :]),
                 ins=[dinv], outs=[dinv_bf])
            P.op("t", lambda e: e.transpose(ps_tr[:, :], dinv_bf[:, :], cm.ident[:, :]),
                 ins=[dinv_bf, cm.ident], outs=[ps_tr])
            P.op("v", lambda e, qt=qt: e.tensor_copy(dinvT[:, qt * PT:(qt + 1) * PT],
                                                     ps_tr[:, :]), ins=[ps_tr], outs=[dinvT])
        qc_sb = [sb(P, st, f"qc{i}", [PT, 512], BF16) for i in range(KD)]
        slabA = [sb(P, st, f"slab{i}", [PT, 512], BF16) for i in range(2)]
        ytf = sb(P, st, "ytf", [HD, 512], F32)
        yt_sb = [sb(P, st, f"yts{i}", [PT, 512], BF16) for i in range(KD)]
        x_res = sb(P, st, "x_res", [PT, D], F32)
        o_f = sb(P, st, "o_f", [PT, D], F32)
        ps_yT = ps(P, st, "ps_yT", [HD, 512], F32)
        ps_db = ps(P, st, "ps_db", [HD, 512], F32)
        ps_o = ps(P, st, "ps_o", [PT, 512], F32)
        ps_sA2 = [ps_s, ps_m]
        owt = load_tiles(P, st, "owt", owt_d, D, D, BF16)
        ps_b2 = ps(P, st, "ps_b2", [PT, 512], F32)
        ob_bc = bcast_row(P, st, "ob_bc", ob_d.rearrange("(a d) -> a d", a=1), D,
                          cm.ones_f, ps_b2)
        for qc in range(NCH):
            c0 = qc * 512
            for dp in range(KD):
                P.dma("y", qc_sb[dp][:, :], sc_q[dp * PT:(dp + 1) * PT, c0:c0 + 512],
                      outs=[qc_sb[dp]])
            for h in range(NH):
                dp, b = h // 2, (h % 2) * HD
                for kt in range(NTT):
                    pss = ps_sA2[kt % 2]
                    slab = slabA[kt % 2]
                    P.op("t", lambda e, dp=dp, b=b, kt=kt, pss=pss: e.matmul(
                        pss[:, :], kT[dp][b:b + HD, kt * PT:(kt + 1) * PT],
                        qc_sb[dp][b:b + HD, :], start=True, stop=True),
                        ins=[kT[dp], qc_sb[dp]], outs=[pss])
                    P.op("s", lambda e, pss=pss, slab=slab: e.activation(
                        slab[:, :], pss[:, :], AF.Exp), ins=[pss], outs=[slab])
                    P.op("t", lambda e, h=h, kt=kt, slab=slab: e.matmul(
                        ps_yT[:, :], v_sb[kt][:, h * HD:(h + 1) * HD], slab[:, :],
                        start=(kt == 0), stop=(kt == NTT - 1)),
                        ins=[v_sb[kt], slab], outs=[ps_yT])
                P.op("t", lambda e, h=h, c0=c0: e.matmul(
                    ps_db[:, :], hsel[:, h * HD:(h + 1) * HD], dinvT[:, c0:c0 + 512],
                    start=True, stop=True), ins=[hsel, dinvT], outs=[ps_db])
                P.op("s", lambda e: e.copy(ytf[:, :], ps_yT[:, :]), ins=[ps_yT], outs=[ytf])
                P.op("v", lambda e, dp=dp, b=b: e.tensor_tensor(
                    yt_sb[dp][b:b + HD, :], ytf[:, :], ps_db[:, :], ALU.mult),
                    ins=[ytf, ps_db], outs=[yt_sb[dp]])
            for qt in range(4):
                row = c0 + qt * PT
                P.dma("y", x_res[:, :], x_d[row:row + PT, :], outs=[x_res])
                for n2 in range(2):
                    for kt in range(KD):
                        P.op("t", lambda e, qt=qt, n2=n2, kt=kt: e.matmul(
                            ps_o[:, :], yt_sb[kt][:, qt * PT:(qt + 1) * PT],
                            owt[kt][:, n2 * 512:(n2 + 1) * 512],
                            start=(kt == 0), stop=(kt == KD - 1)),
                            ins=[yt_sb[kt], owt[kt]], outs=[ps_o])
                    P.op("v", lambda e, n2=n2: e.tensor_tensor(
                        o_f[:, n2 * 512:(n2 + 1) * 512], ps_o[:, :],
                        ob_bc[:, n2 * 512:(n2 + 1) * 512], ALU.add),
                        ins=[ps_o, ob_bc], outs=[o_f])
                P.op("v", lambda e: e.tensor_add(o_f[:, :], o_f[:, :], x_res[:, :]),
                     ins=[o_f, x_res], outs=[o_f])
                P.dma("g", xo_d[row:row + PT, :], o_f[:, :], ins=[o_f])
        if standalone:
            P.fence_outputs()
            P.emit()
    return nc


def l1ab_inputs(inp, pr, cst, x):
    return {"x": x, "c_ident": cst["c_ident"], "c_hsel": cst["c_hsel"],
            "t_qwt2": pr["t_qwt2"], "t_kwt2": pr["t_kwt2"], "t_vwt2": pr["t_vwt2"],
            "t_owt2": pr["t_owt2"], "out_b": np.asarray(inp["L1_attn_b_out_b"])}


# version 06 (final)
# speedup vs baseline: 1.1962x; 1.0212x over previous
import contextlib
import math

import numpy as np
import ml_dtypes

import concourse.bass as bass
import concourse.mybir as mybir
from concourse.bass_utils import run_bass_kernel_spmd

F32 = mybir.dt.float32
BF16 = mybir.dt.bfloat16
AF = mybir.ActivationFunctionType
ALU = mybir.AluOpType
AX = mybir.AxisListType

NCORES = 8
T = 4096
D = 1024
PT = 128
KD = D // PT

BF = ml_dtypes.bfloat16


ENG_SECTION = {"t": "tensor", "v": "vector", "s": "scalar", "g": "gpsimd", "y": "sync"}
ENGS = tuple(ENG_SECTION)


SEMSEG = 16384


class Prog:
    def __init__(self, nc):
        self.nc = nc
        self.q = {e: [] for e in ENGS}
        self.cnt = {e: 0 for e in ENGS}
        self.waited = {}
        self.wmap = {}
        self.rmap = {}
        self.sems = {}
        self._qtok = {}
        self.nuniq = 0
        self.drams = {}

    def uname(self, name):
        self.nuniq += 1
        return f"{name}_u{self.nuniq}"

    def gd(self, name, shape, dtype, kind="ExternalInput"):
        if name not in self.drams:
            self.drams[name] = self.nc.dram_tensor(name, list(shape), dtype,
                                                   kind=kind).ap()
        return self.drams[name]

    def _wait(self, engobj, te, tv):
        seg = (tv - 1) // SEMSEG
        engobj.wait_ge(self.sems[te][seg], tv - seg * SEMSEG)

    def _key(self, h):
        return id(h)

    def _deps(self, e, ins, outs):
        need = {}

        def add(te, tv):
            if te == e and e == "t":
                return
            if need.get(te, -1) < tv:
                need[te] = tv

        for h in ins:
            k = self._key(h)
            if k in self.wmap:
                add(*self.wmap[k])
        for h in outs:
            k = self._key(h)
            if k in self.wmap:
                add(*self.wmap[k])
            for te, tv in self.rmap.get(k, {}).items():
                add(te, tv)
        waits = []
        for te, tv in sorted(need.items()):
            if tv > self.waited.get((e, te), 0):
                waits.append((te, tv))
                self.waited[(e, te)] = tv
        return waits

    def op(self, e, fn, ins=(), outs=(), inc=1):
        waits = self._deps(e, ins, outs)
        self.cnt[e] += inc
        myval = self.cnt[e]

        def emit(engobj, waits=waits, fn=fn, e=e, inc=inc, myval=myval):
            for te, tv in waits:
                self._wait(engobj, te, tv)
            instr = fn(engobj)
            seg = (myval - 1) // SEMSEG
            instr.then_inc(self.sems[e][seg], inc)

        self.q[e].append(emit)
        for h in outs:
            k = self._key(h)
            self.wmap[k] = (e, myval)
            self.rmap[k] = {}
        for h in ins:
            self.rmap.setdefault(self._key(h), {})[e] = myval
        return myval

    def dma(self, e, out_ap, in_ap, ins=(), outs=(), slow=False):
        self._rr = getattr(self, "_rr", 0) + 1
        e = ("y", "g")[self._rr % 2]
        tok = self._qtok.setdefault(e, object())
        return self.op(e, lambda eng: eng.dma_start(
            out=out_ap, in_=in_ap, allow_slow_non_contiguous=slow),
            ins=ins, outs=list(outs) + [tok], inc=16)

    def barrier(self):
        snap = dict(self.cnt)
        for e in ENGS:
            waits = []
            for te, tv in sorted(snap.items()):
                if te != e and tv > 0 and tv > self.waited.get((e, te), 0):
                    waits.append((te, tv))
                    self.waited[(e, te)] = tv
            if waits:
                def emit(engobj, waits=waits):
                    for te, tv in waits:
                        self._wait(engobj, te, tv)
                self.q[e].append(emit)

    def fence_outputs(self):
        snap = dict(self.cnt)

        def emit(engobj, snap=snap):
            for te, tv in snap.items():
                nseg = (tv - 1) // SEMSEG + 1 if tv > 0 else 0
                for seg in range(nseg):
                    top = min(tv, (seg + 1) * SEMSEG)
                    self._wait(engobj, te, top)

        self.q["y"].append(emit)

    def emit(self):
        nc = self.nc
        with contextlib.ExitStack() as st:
            for e in ENGS:
                nseg = self.cnt[e] // SEMSEG + 1
                self.sems[e] = [st.enter_context(nc.semaphore(f"sem_{e}{i}"))
                                for i in range(nseg)]
            blk = st.enter_context(nc.Block())
            for e in ENGS:
                fns = self.q[e]
                if not fns:
                    continue
                section = getattr(blk, ENG_SECTION[e])

                def body(engobj, fns=fns):
                    for f in fns:
                        f(engobj)

                section(body)



def sb(P, st, name, shape, dtype):
    h = st.enter_context(P.nc.sbuf_tensor(P.uname(name), list(shape), dtype))
    return h


def ps(P, st, name, shape, dtype=F32):
    h = st.enter_context(P.nc.psum_tensor(P.uname(name), list(shape), dtype))
    return h


class CPool:

    def __init__(self, P, st, name="cpool", n=24):
        self.P = P
        self.tile = sb(P, st, name, [128, n], F32)
        self.map = {}
        self.n = n

    def col(self, val):
        val = float(val)
        if val not in self.map:
            i = len(self.map)
            assert i < self.n, "CPool full"
            self.map[val] = i
            self.P.op("v", lambda e, i=i, val=val: e.memset(self.tile[:, i:i + 1], val),
                      outs=[self.tile])
        return self.tile[:, self.map[val]:self.map[val] + 1]


def ln_tile(P, cp, x_t, h_t, tmp_col, tmp_sq, eps=1e-5, n=D):
    mean = tmp_col[:, 0:1]
    var = tmp_col[:, 1:2]
    rstd = tmp_col[:, 2:3]
    P.op("v", lambda e: e.reduce_sum(mean, x_t[:, :], axis=AX.X), ins=[x_t], outs=[tmp_col])
    P.op("v", lambda e: e.tensor_scalar_mul(mean, mean, 1.0 / n), ins=[tmp_col], outs=[tmp_col])
    P.op("v", lambda e: e.tensor_scalar(h_t[:, :], x_t[:, :], mean, None, ALU.subtract),
         ins=[x_t, tmp_col], outs=[h_t])
    P.op("s", lambda e: e.activation(tmp_sq[:, :], h_t[:, :], AF.Square),
         ins=[h_t], outs=[tmp_sq])
    P.op("v", lambda e: e.reduce_sum(var, tmp_sq[:, :], axis=AX.X), ins=[tmp_sq], outs=[tmp_col])
    c_eps, c_invn, c_half = cp.col(eps), cp.col(1.0 / n), cp.col(-0.5)
    P.op("s", lambda e: e.activation(var, var, AF.Ln, bias=c_eps, scale=c_invn),
         ins=[tmp_col, cp.tile], outs=[tmp_col])
    P.op("s", lambda e: e.activation(rstd, var, AF.Exp, scale=c_half),
         ins=[tmp_col, cp.tile], outs=[tmp_col])
    P.op("v", lambda e: e.tensor_scalar_mul(h_t[:, :], h_t[:, :], rstd),
         ins=[h_t, tmp_col], outs=[h_t])


def softplus_tile(P, cp, out, in_, tmp1, tmp2):
    P.op("s", lambda e: e.activation(tmp1[:, :], in_[:, :], AF.Abs),
         ins=[in_], outs=[tmp1])
    c_m1, c_1 = cp.col(-1.0), cp.col(1.0)
    P.op("s", lambda e: e.activation(tmp1[:, :], tmp1[:, :], AF.Exp, scale=c_m1),
         ins=[tmp1, cp.tile], outs=[tmp1])
    P.op("s", lambda e: e.activation(tmp1[:, :], tmp1[:, :], AF.Ln, bias=c_1),
         ins=[tmp1, cp.tile], outs=[tmp1])
    P.op("v", lambda e: e.tensor_scalar_max(tmp2[:, :], in_[:, :], 0.0),
         ins=[in_], outs=[tmp2])
    P.op("v", lambda e: e.tensor_add(out[:, :], tmp1[:, :], tmp2[:, :]),
         ins=[tmp1, tmp2], outs=[out])




def load_tiles(P, st, name, dram_ap, rows, cols, dtype, q="y"):
    nt = rows // PT
    tiles = []
    for i in range(nt):
        t = sb(P, st, f"{name}{i}", [PT, cols], dtype)
        P.dma(q, t[:, :], dram_ap[i * PT:(i + 1) * PT, :], outs=[t])
        tiles.append(t)
    return tiles


def load_cols(P, st, name, dram_1d, n, q="y"):
    t = sb(P, st, name, [PT, n], F32)
    v = dram_1d.rearrange("(a p) -> a p", p=PT)
    for j in range(n):
        P.dma(q, t[:, j:j + 1], v[j:j + 1].rearrange("a p -> p a"), outs=[t])
    return t


def bcast_row(P, st, name, row_ap, n, ones_f, psum_f, row_h=None):
    stage = sb(P, st, name + "_r", [1, n], F32)
    out = sb(P, st, name, [PT, n], F32)
    if row_h is None:
        P.dma("y", stage[:, :], row_ap, outs=[stage])
        src = stage
    else:
        src = row_ap
    srch = stage if row_h is None else row_h
    for j in range(0, n, 512):
        w = min(512, n - j)
        P.op("t", lambda e, j=j, w=w: e.matmul(psum_f[:, 0:w], ones_f[0:1, :],
                                               src[:, j:j + w], start=True, stop=True),
             ins=[ones_f, srch], outs=[psum_f])
        P.op("v", lambda e, j=j, w=w: e.tensor_copy(out[:, j:j + w], psum_f[:, 0:w]),
             ins=[psum_f], outs=[out])
    return out


class Common:

    def __init__(self, P, st):
        nc = P.nc
        self.cp = CPool(P, st)
        self.ident = sb(P, st, "c_ident_sb", [PT, PT], BF16)
        self.ones_f = sb(P, st, "c_ones_sb", [PT, PT], F32)
        self.ones_bf = sb(P, st, "c_onesbf_sb", [PT, PT], BF16)
        i_d = P.gd("c_ident", [PT, PT], BF16)
        P.dma("y", self.ident[:, :], i_d[:, :], outs=[self.ident])
        P.op("v", lambda e: e.memset(self.ones_f[:, :], 1.0), outs=[self.ones_f])
        P.op("v", lambda e: e.memset(self.ones_bf[:, :], 1.0), outs=[self.ones_bf])


def make_hT(P, st, cm, x_d, sink_dram=None, sink_tiles=None, mode="plain",
            gw_bc=None, gb_bc=None, tag="", router=None):
    cp = cm.cp
    ntt = T // PT
    st = contextlib.ExitStack()
    xt = sb(P, st, f"ht_x{tag}", [PT, D], F32)
    hf = sb(P, st, f"ht_h{tag}", [PT, D], F32)
    h2 = sb(P, st, f"ht_h2{tag}", [PT, D], F32) if mode == "double_affine" else None
    hbf = sb(P, st, f"ht_hbf{tag}", [PT, D], BF16)
    tsq = xt
    tcol = sb(P, st, f"ht_tcol{tag}", [PT, 8], F32)
    stage = sb(P, st, f"ht_stage{tag}", [PT, PT], BF16)
    ptr = [ps(P, st, f"ht_ps{tag}{i}", [PT, PT], BF16) for i in range(2)]
    for tt in range(ntt):
        P.dma("y", xt[:, :], x_d[tt * PT:(tt + 1) * PT, :], outs=[xt])
        ln_tile(P, cp, xt, hf, tcol, tsq)
        if mode == "double_affine":
            ln_tile(P, cp, hf, h2, tcol, tsq)
            P.op("v", lambda e: e.tensor_mul(h2[:, :], h2[:, :], gw_bc[:, :]),
                 ins=[h2, gw_bc], outs=[h2])
            P.op("v", lambda e: e.tensor_add(h2[:, :], h2[:, :], gb_bc[:, :]),
                 ins=[h2, gb_bc], outs=[h2])
            src = h2
        else:
            src = hf
        P.op("v", lambda e, src=src: e.tensor_copy(hbf[:, :], src[:, :]), ins=[src], outs=[hbf])
        for dp in range(KD):
            pp = ptr[dp % 2]
            P.op("t", lambda e, dp=dp, pp=pp: e.transpose(
                pp[:, :], hbf[:, dp * PT:(dp + 1) * PT], cm.ident[:, :]),
                ins=[hbf, cm.ident], outs=[pp])
            if sink_tiles is not None:
                dst = sink_tiles[dp]
                P.op("v", lambda e, dp=dp, pp=pp, dst=dst, tt=tt: e.tensor_copy(
                    dst[:, tt * PT:(tt + 1) * PT], pp[:, :]), ins=[pp], outs=[dst])
            else:
                P.op("v", lambda e, pp=pp: e.tensor_copy(stage[:, :], pp[:, :]),
                     ins=[pp], outs=[stage])
                P.dma("g", sink_dram[tt, dp * PT:(dp + 1) * PT, :], stage[:, :], ins=[stage])
                if router is not None:
                    wg_tiles, w_out, ps_r = router
                    P.op("t", lambda e, dp=dp, wgt=wg_tiles[dp], ps_r=ps_r: e.matmul(
                        ps_r[:, :], stage[:, :], wgt[:, :],
                        start=(dp == 0), stop=(dp == KD - 1)),
                        ins=[stage, wg_tiles[dp]], outs=[ps_r])
        if router is not None:
            _top2(P, cp, router[2], router[1], tt)
    P.barrier()
    st.close()


def _top2(P, cp, ps_r, w_out, tt):
    st_ = P._top2_scratch
    r, m1, m2, r2, eq1, eq2, e1, den = st_
    P.op("v", lambda e: e.tensor_copy(r[:, :], ps_r[:, 0:8]), ins=[ps_r], outs=[r])
    P.op("v", lambda e: e.reduce_max(m1[:, 0:1], r[:, :], axis=AX.X), ins=[r], outs=[m1])
    P.op("v", lambda e: e.tensor_scalar(eq1[:, :], r[:, :], m1[:, 0:1], None, ALU.is_equal),
         ins=[r, m1], outs=[eq1])
    P.op("v", lambda e: e.scalar_tensor_tensor(r2[:, :], eq1[:, :], -1e30, r[:, :],
                                               ALU.mult, ALU.add), ins=[eq1, r], outs=[r2])
    P.op("v", lambda e: e.reduce_max(m2[:, 0:1], r2[:, :], axis=AX.X), ins=[r2], outs=[m2])
    P.op("v", lambda e: e.tensor_scalar(eq2[:, :], r2[:, :], m2[:, 0:1], None, ALU.is_equal),
         ins=[r2, m2], outs=[eq2])
    negm1 = m1[:, 1:2]
    P.op("v", lambda e: e.tensor_scalar_mul(negm1, m1[:, 0:1], -1.0), ins=[m1], outs=[m1])
    P.op("s", lambda e: e.activation(e1[:, :], r[:, :], AF.Exp, bias=negm1),
         ins=[r, m1], outs=[e1])
    P.op("v", lambda e: e.tensor_scalar(den[:, 0:1], m2[:, 0:1], m1[:, 0:1], None, ALU.subtract),
         ins=[m2, m1], outs=[den])
    P.op("s", lambda e: e.activation(den[:, 0:1], den[:, 0:1], AF.Exp), ins=[den], outs=[den])
    P.op("v", lambda e: e.tensor_scalar_add(den[:, 0:1], den[:, 0:1], 1.0), ins=[den], outs=[den])
    P.op("v", lambda e: e.reciprocal(den[:, 1:2], den[:, 0:1]), ins=[den], outs=[den])
    P.op("v", lambda e: e.tensor_add(eq1[:, :], eq1[:, :], eq2[:, :]), ins=[eq1, eq2], outs=[eq1])
    P.op("v", lambda e: e.tensor_mul(e1[:, :], e1[:, :], eq1[:, :]), ins=[e1, eq1], outs=[e1])
    P.op("v", lambda e, tt=tt: e.tensor_scalar(w_out[:, tt * 8:tt * 8 + 8], e1[:, :],
                                               den[:, 1:2], None, ALU.mult),
         ins=[e1, den], outs=[w_out])


CAST_SPECS = [
    ("wot", 1024, 1024), ("wg", 1024, 8),
    ("w1", 8192, 1408), ("w2", 8192, 1408), ("w3", 11264, 1024),
    ("ws1", 1024, 2816), ("ws2", 1024, 2816), ("ws3", 2816, 1024),
    ("wint", 1024, 2048), ("wdtt", 1024, 1024), ("wxt", 1024, 32),
    ("woutt", 1024, 1024),
    ("wqkv", 1024, 3072), ("wout1a", 1024, 1024),
    ("qwt", 1024, 1024), ("kwt", 1024, 1024), ("vwt", 1024, 1024),
    ("owt", 1024, 1024),
    ("qwt2", 1024, 1024), ("kwt2", 1024, 1024), ("vwt2", 1024, 1024),
    ("owt2", 1024, 1024),
]




def _ctx(ctx):
    if ctx is None:
        nc = bass.Bass("TRN2", target_bir_lowering=False)
        P = Prog(nc)
        x_d = nc.dram_tensor("x", [T, D], F32, kind="ExternalInput").ap()
        xo_d = nc.dram_tensor("xo", [T, D], F32, kind="ExternalOutput").ap()
        return nc, P, x_d, xo_d, "", True
    nc, P, x_d, xo_d, uid = ctx
    return nc, P, x_d, xo_d, uid, False

def build_prep(ctx=None):
    nc, P, _, _, uid, standalone = _ctx(ctx)
    okind = "ExternalOutput" if standalone else "Internal"
    with contextlib.ExitStack() as st:
        bufs = [sb(P, st, f"cb{i}", [PT, 3072], F32) for i in range(2)]
        obufs = [sb(P, st, f"co{i}", [PT, 3072], BF16) for i in range(2)]
        for name, rows, cols in CAST_SPECS:
            i_ap = P.gd("p_" + name, [rows, cols], F32)
            o_ap = P.gd("t_" + name, [rows, cols], BF16, kind=okind)
            for i in range(rows // PT):
                b, o = bufs[i % 2], obufs[i % 2]
                P.dma("y", b[:, 0:cols], i_ap[i * PT:(i + 1) * PT, :], outs=[b])
                P.op("v", lambda e, b=b, o=o, cols=cols: e.tensor_copy(o[:, 0:cols], b[:, 0:cols]),
                     ins=[b], outs=[o])
                P.dma("g", o_ap[i * PT:(i + 1) * PT, :], o[:, 0:cols], ins=[o])
        NH, HD, RK = 16, 64, 32
        v_d = P.gd("p_v", [HD, RK], F32)
        ut_d = P.gd("p_u_t", [RK, HD], F32)
        ut_o = P.gd("t_ut", [RK, HD], BF16, kind=okind)
        v_sb = sb(P, st, "v_sb", [PT, RK], F32)
        vs_sb = sb(P, st, "vs_sb", [PT, RK], F32)
        ut_sb = sb(P, st, "ut_sbf", [RK, HD], F32)
        ut_bf = sb(P, st, "ut_bf", [RK, HD], BF16)
        P.dma("y", v_sb[0:HD, :], v_d[:, :], outs=[v_sb])
        P.dma("y", v_sb[HD:PT, :], v_d[:, :], outs=[v_sb])
        P.dma("y", ut_sb[:, :], ut_d[:, :], outs=[ut_sb])
        P.op("v", lambda e: e.tensor_scalar_mul(vs_sb[:, :], v_sb[:, :], 1.0 / math.sqrt(RK)),
             ins=[v_sb], outs=[vs_sb])
        P.op("v", lambda e: e.tensor_copy(ut_bf[:, :], ut_sb[:, :]), ins=[ut_sb], outs=[ut_bf])
        P.dma("g", ut_o[:, :], ut_bf[:, :], ins=[ut_bf])
        ps_f = ps(P, st, "ps_f", [PT, 512], F32)
        ps_b = ps(P, st, "ps_b", [RK, 16], F32)
        for wname, vv, bname, scaled, aug in [
            ("wq", vs_sb, "bq", True, False), ("wk", v_sb, "bk", False, False),
            ("wv", v_sb, "bv", False, True),
        ]:
            w_d = P.gd("p_" + wname, [D, D], F32)
            w_tiles = load_tiles(P, st, wname + "_t", w_d, D, D, F32)
            b_cols = load_cols(P, st, wname + "_b", P.gd("p_" + bname, [D], F32), KD)
            cw = 33 if aug else 32
            o_ap = P.gd("t_" + ("wvv" if aug else ("wqv" if wname == "wq" else "wkv")),
                        [D, NH * cw], BF16, kind=okind)
            stage = sb(P, st, wname + "_st", [PT, NH * cw], BF16)
            for ds_ in range(KD):
                P.op("v", lambda e, stage=stage: e.memset(stage[:, :], 0.0), outs=[stage])
                for h in range(NH):
                    base = (h % 2) * HD
                    lhs = w_tiles[h // 2][base:base + HD, ds_ * PT:(ds_ + 1) * PT]
                    P.op("t", lambda e, lhs=lhs, vv=vv, base=base: e.matmul(
                        ps_f[:, 0:RK], lhs, vv[base:base + HD, :], start=True, stop=True),
                        ins=[w_tiles[h // 2], vv], outs=[ps_f])
                    P.op("v", lambda e, h=h, cw=cw, stage=stage: e.tensor_copy(
                        stage[:, h * cw:h * cw + RK], ps_f[:, 0:RK]), ins=[ps_f], outs=[stage])
                P.dma("g", o_ap[ds_ * PT:(ds_ + 1) * PT, :], stage[:, :], ins=[stage])
            if aug:
                o9 = P.gd("t_wvv9", [1, NH * cw], BF16, kind=okind)
                r9 = sb(P, st, "wvv9_st", [1, NH * cw], BF16)
                for h in range(NH):
                    base = (h % 2) * HD
                    bcol = b_cols[base:base + HD, (h // 2):(h // 2) + 1]
                    P.op("t", lambda e, bcol=bcol, vv=vv, base=base: e.matmul(
                        ps_f[0:1, 0:RK],
                        bcol, vv[base:base + HD, :], start=True, stop=True),
                        ins=[b_cols, vv], outs=[ps_f])
                    P.op("v", lambda e, h=h, r9=r9, cw=cw: e.tensor_copy(r9[0:1, h * cw:h * cw + RK],
                                                           ps_f[0:1, 0:RK]),
                         ins=[ps_f], outs=[r9])
                    P.op("v", lambda e, h=h, r9=r9, cw=cw: e.memset(
                        r9[0:1, h * cw + RK:h * cw + RK + 1], 1.0),
                         outs=[r9])
                P.dma("g", o9[:, :], r9[:, :], ins=[r9])
            else:
                o_b = P.gd("t_b" + wname[1] + "v", [RK, NH], F32, kind=okind)
                bstage = sb(P, st, wname + "_bst", [RK, NH], F32)
                for h in range(NH):
                    base = (h % 2) * HD
                    bcol = b_cols[base:base + HD, (h // 2):(h // 2) + 1]
                    P.op("t", lambda e, bcol=bcol, vv=vv, base=base: e.matmul(
                        ps_b[:, 0:1], vv[base:base + HD, :], bcol, start=True, stop=True),
                        ins=[b_cols, vv], outs=[ps_b])
                    P.op("v", lambda e, h=h, bstage=bstage: e.tensor_copy(
                        bstage[:, h:h + 1], ps_b[:, 0:1]),
                         ins=[ps_b], outs=[bstage])
                P.dma("g", o_b[:, :], bstage[:, :], ins=[bstage])
        if standalone:
            P.fence_outputs()
            P.emit()
    return nc


def build_l0aa(ctx=None):
    nc, P, x_d, xo_d, uid, standalone = _ctx(ctx)
    NH, HD, RK = 16, 64, 32
    CW = RK + 1
    wqv_d = P.gd("t_wqv", [D, NH * RK], BF16)
    wkv_d = P.gd("t_wkv", [D, NH * RK], BF16)
    wvv_d = P.gd("t_wvv", [D, NH * CW], BF16)
    wvv9_d = P.gd("t_wvv9", [1, NH * CW], BF16)
    bqv_d = P.gd("t_bqv", [RK, NH], F32)
    bkv_d = P.gd("t_bkv", [RK, NH], F32)
    ut_d = P.gd("t_ut", [RK, HD], BF16)
    wot_d = P.gd("t_wot", [D, D], BF16)
    bo_d = P.gd("bo", [D], F32)
    with contextlib.ExitStack() as st:
        cm = Common(P, st)
        bqv = sb(P, st, "bqv", [RK, NH], F32)
        bkv = sb(P, st, "bkv", [RK, NH], F32)
        P.dma("y", bqv[:, :], bqv_d[:, :], outs=[bqv])
        P.dma("y", bkv[:, :], bkv_d[:, :], outs=[bkv])
        zq = [sb(P, st, f"zq{i}", [PT, T], BF16) for i in range(6)]
        zk = [sb(P, st, f"zk{i}", [PT, T], BF16) for i in range(6)]
        zv = [sb(P, st, f"zv{h}", [PT, 32 * CW], BF16) for h in range(NH)]
        NTT = T // PT
        NCH = T // 512
        zst = contextlib.ExitStack()
        hT = [sb(P, zst, f"hT{i}", [PT, T], BF16) for i in range(KD)]
        make_hT(P, zst, cm, x_d, sink_tiles=hT)
        ps_z = ps(P, st, "ps_z", [RK, 512], F32)
        ps_v = ps(P, st, "ps_v", [PT, CW], F32)
        for wname, w_dram, ztarget, b_sb in [
            ("wqv", wqv_d, zq, None), ("wkv", wkv_d, zk, None), ("wvv", wvv_d, zv, None),
        ]:
            wst = contextlib.ExitStack()
            if wname == "wvv":
                wtiles = load_tiles(P, wst, "wvv", wvv_d, D, NH * CW, BF16)
                wvv9 = sb(P, wst, "wvv9", [1, NH * CW], BF16)
                P.dma("y", wvv9[:, :], wvv9_d[:, :], outs=[wvv9])
                for h in range(NH):
                    for tt in range(NTT):
                        for kt in range(KD):
                            P.op("t", lambda e, h=h, tt=tt, kt=kt, wtiles=wtiles: e.matmul(
                                ps_v[:, :], hT[kt][:, tt * PT:(tt + 1) * PT],
                                wtiles[kt][:, h * CW:(h + 1) * CW],
                                start=(kt == 0), stop=False),
                                ins=[hT[kt], wtiles[kt]], outs=[ps_v])
                        P.op("t", lambda e, h=h, tt=tt, wvv9=wvv9: e.matmul(
                            ps_v[:, :], cm.ones_bf[0:1, :], wvv9[0:1, h * CW:(h + 1) * CW],
                            start=False, stop=True),
                            ins=[cm.ones_bf, wvv9], outs=[ps_v])
                        P.op("v", lambda e, h=h, tt=tt: e.tensor_copy(
                            zv[h][:, tt * CW:(tt + 1) * CW], ps_v[:, :]),
                            ins=[ps_v], outs=[zv[h]])
            else:
                wtiles = load_tiles(P, wst, wname, w_dram, D, NH * RK, BF16)
                bias = bqv if wname == "wqv" else bkv
                for h in range(NH):
                    pk, r0 = h // 3, (h % 3) * RK
                    for nch in range(NCH):
                        for kt in range(KD):
                            P.op("t", lambda e, h=h, nch=nch, kt=kt, wtiles=wtiles: e.matmul(
                                ps_z[:, :], wtiles[kt][:, h * RK:(h + 1) * RK],
                                hT[kt][:, nch * 512:(nch + 1) * 512],
                                start=(kt == 0), stop=(kt == KD - 1)),
                                ins=[wtiles[kt], hT[kt]], outs=[ps_z])
                        P.op("v", lambda e, nch=nch, pk=pk, r0=r0, ztarget=ztarget,
                             h=h, bias=bias: e.tensor_scalar(
                            ztarget[pk][r0:r0 + RK, nch * 512:(nch + 1) * 512], ps_z[:, :],
                            bias[:, h:h + 1], None, ALU.add),
                            ins=[ps_z, bias], outs=[ztarget[pk]])
            P.barrier()
            wst.close()
        P.barrier()
        zst.close()
        ut = sb(P, st, "ut", [RK, HD], BF16)
        P.dma("y", ut[:, :], ut_d[:, :], outs=[ut])
        wot = load_tiles(P, st, "wot", wot_d, D, D, BF16)
        ps_b = ps(P, st, "ps_b", [PT, 512], F32)
        bo_bc = bcast_row(P, st, "bo_bc", bo_d.rearrange("(a d) -> a d", a=1), D,
                          cm.ones_f, ps_b)
        ps_s = ps(P, st, "ps_s", [PT, 512], F32)
        ps_sA = [ps_s, ps_b]
        ps_y = ps(P, st, "ps_y", [CW, 512], F32)
        ps_lb = ps(P, st, "ps_lb", [RK, 512], F32)
        ps_u = ps(P, st, "ps_u", [HD, 512], F32)
        ps_o = ps(P, st, "ps_o", [PT, 512], F32)
        slabA = [sb(P, st, f"slab{i}", [PT, 512], BF16) for i in range(2)]
        rl_sb = sb(P, st, "rl_sb", [1, 512], F32)
        yr_f = sb(P, st, "yr_f", [RK, 512], F32)
        yrn = sb(P, st, "yrn", [RK, 512], BF16)
        yT = [sb(P, st, f"yT{i}", [PT, 512], BF16) for i in range(KD)]
        x_res = sb(P, st, "x_res", [PT, D], F32)
        o_f = sb(P, st, "o_f", [PT, D], F32)
        for qc in range(NCH):
            for h in range(NH):
                pk, r0 = h // 3, (h % 3) * RK
                for kt in range(NTT):
                    pss = ps_sA[kt % 2]
                    slab = slabA[kt % 2]
                    P.op("t", lambda e, h=h, qc=qc, kt=kt, pk=pk, r0=r0, pss=pss: e.matmul(
                        pss[:, :], zk[pk][r0:r0 + RK, kt * PT:(kt + 1) * PT],
                        zq[pk][r0:r0 + RK, qc * 512:(qc + 1) * 512],
                        start=True, stop=True), ins=[zk[pk], zq[pk]], outs=[pss])
                    P.op("s", lambda e, pss=pss, slab=slab: e.activation(
                        slab[:, :], pss[:, :], AF.Exp), ins=[pss], outs=[slab])
                    P.op("t", lambda e, h=h, kt=kt, slab=slab: e.matmul(
                        ps_y[:, :], zv[h][:, kt * CW:(kt + 1) * CW], slab[:, :],
                        start=(kt == 0), stop=(kt == NTT - 1)),
                        ins=[zv[h], slab], outs=[ps_y])
                P.op("v", lambda e: e.reciprocal(rl_sb[:, :], ps_y[RK:CW, :]),
                     ins=[ps_y], outs=[rl_sb])
                P.op("t", lambda e: e.matmul(ps_lb[:, :], cm.ones_f[0:1, 0:RK],
                                             rl_sb[:, :], start=True, stop=True),
                     ins=[cm.ones_f, rl_sb], outs=[ps_lb])
                P.op("s", lambda e: e.copy(yr_f[:, :], ps_y[0:RK, :]), ins=[ps_y], outs=[yr_f])
                P.op("v", lambda e: e.tensor_tensor(yrn[:, :], yr_f[:, :], ps_lb[:, :],
                                                    ALU.mult), ins=[yr_f, ps_lb], outs=[yrn])
                P.op("t", lambda e: e.matmul(ps_u[:, :], ut[:, :], yrn[:, :],
                                             start=True, stop=True), ins=[ut, yrn], outs=[ps_u])
                dp, rr = h // 2, (h % 2) * HD
                P.op("v", lambda e, dp=dp, rr=rr: e.tensor_copy(
                    yT[dp][rr:rr + HD, :], ps_u[:, :]), ins=[ps_u], outs=[yT[dp]])
            for qt in range(4):
                row = qc * 512 + qt * PT
                P.dma("y", x_res[:, :], x_d[row:row + PT, :], outs=[x_res])
                for n2 in range(2):
                    for kt in range(KD):
                        P.op("t", lambda e, qt=qt, n2=n2, kt=kt: e.matmul(
                            ps_o[:, :], yT[kt][:, qt * PT:(qt + 1) * PT],
                            wot[kt][:, n2 * 512:(n2 + 1) * 512],
                            start=(kt == 0), stop=(kt == KD - 1)),
                            ins=[yT[kt], wot[kt]], outs=[ps_o])
                    P.op("v", lambda e, n2=n2: e.tensor_tensor(
                        o_f[:, n2 * 512:(n2 + 1) * 512], ps_o[:, :],
                        bo_bc[:, n2 * 512:(n2 + 1) * 512], ALU.add),
                        ins=[ps_o, bo_bc], outs=[o_f])
                P.op("v", lambda e: e.tensor_add(o_f[:, :], o_f[:, :], x_res[:, :]),
                     ins=[o_f, x_res], outs=[o_f])
                P.dma("g", xo_d[row:row + PT, :], o_f[:, :], ins=[o_f])
        if standalone:
            P.fence_outputs()
            P.emit()
    return nc



def host_consts():
    ident = np.eye(PT, dtype=BF)
    half = 32
    theta = 1.0 / 10000 ** (np.arange(half, dtype=np.float64) / half)
    pos = np.arange(T, dtype=np.float64)
    ang = pos[None, :] * theta[:, None]
    cos32 = np.cos(ang).astype(np.float32)
    sin32 = np.sin(ang).astype(np.float32)
    cosp = np.zeros((PT, T), np.float32)
    sinp = np.zeros((PT, T), np.float32)
    for r in range(PT):
        f = r % half
        cosp[r] = cos32[f]
        sinp[r] = -sin32[f] if (r % 64) < half else sin32[f]
    swap = np.array([i + half if (i % 64) < half else i - half for i in range(PT)])
    psw = np.zeros((PT, PT), np.float32)
    for i in range(PT):
        psw[swap[i], i] = 1.0
    hsel = np.zeros((16, 1024), np.float32)
    for h in range(16):
        hsel[h, h * 64:(h + 1) * 64] = 1.0
    return {"c_ident": ident, "c_cosp": cosp.astype(BF), "c_sinp": sinp.astype(BF),
            "c_pswap": psw.astype(BF), "c_hsel": hsel.astype(BF)}


def prep_inputs(inp):
    tr = lambda a: np.ascontiguousarray(np.asarray(a).T)
    d = {
        "p_wot": tr(inp["L0_attn_a_Wo"]),
        "p_wg": np.asarray(inp["L0_ffn_Wg"]),
        "p_w1": np.asarray(inp["L0_ffn_W1"]).reshape(8192, 1408),
        "p_w2": np.asarray(inp["L0_ffn_W2"]).reshape(8192, 1408),
        "p_w3": np.asarray(inp["L0_ffn_W3"]).reshape(11264, 1024),
        "p_ws1": np.asarray(inp["L0_ffn_Ws1"]),
        "p_ws2": np.asarray(inp["L0_ffn_Ws2"]),
        "p_ws3": np.asarray(inp["L0_ffn_Ws3"]),
        "p_wint": tr(inp["L0_attn_b_W_in"]),
        "p_wdtt": tr(inp["L0_attn_b_W_dt"]),
        "p_wxt": tr(inp["L0_attn_b_W_x"]),
        "p_woutt": tr(inp["L0_attn_b_W_out"]),
        "p_wout1a": np.asarray(inp["L1_attn_a_w_out"]),
        "p_qwt": tr(inp["L1_ffn_q_w"]), "p_kwt": tr(inp["L1_ffn_k_w"]),
        "p_vwt": tr(inp["L1_ffn_v_w"]), "p_owt": tr(inp["L1_ffn_o_w"]),
        "p_qwt2": tr(inp["L1_attn_b_q_w"]), "p_kwt2": tr(inp["L1_attn_b_k_w"]),
        "p_vwt2": tr(inp["L1_attn_b_v_w"]), "p_owt2": tr(inp["L1_attn_b_out_w"]),
        "p_wq": np.asarray(inp["L0_attn_a_Wq"]), "p_wk": np.asarray(inp["L0_attn_a_Wk"]),
        "p_wv": np.asarray(inp["L0_attn_a_Wv"]),
        "p_bq": np.asarray(inp["L0_attn_a_bq"]), "p_bk": np.asarray(inp["L0_attn_a_bk"]),
        "p_bv": np.asarray(inp["L0_attn_a_bv"]),
        "p_v": np.asarray(inp["L0_attn_a_V"]),
        "p_u_t": tr(inp["L0_attn_a_U"]),
    }
    wqkv = np.asarray(inp["L1_attn_a_w_qkv"]).copy()
    bqkv = np.asarray(inp["L1_attn_a_b_qkv"]).copy()
    perm = np.arange(3072)
    for blk in range(2):
        for h in range(16):
            base = blk * 1024 + h * 64
            perm[base:base + 32] = blk * 1024 + h * 64 + np.arange(0, 64, 2)
            perm[base + 32:base + 64] = blk * 1024 + h * 64 + np.arange(1, 64, 2)
    d["p_wqkv"] = np.ascontiguousarray(wqkv[:, perm])
    d["_b_qkv_perm"] = bqkv[perm]
    return d


def l0aa_inputs(inp, pr, cst, x):
    return {
        "x": x, "c_ident": cst["c_ident"],
        "t_wqv": pr["t_wqv"], "t_wkv": pr["t_wkv"], "t_wvv": pr["t_wvv"],
        "t_wvv9": pr["t_wvv9"], "t_bqv": pr["t_bqv"], "t_bkv": pr["t_bkv"],
        "t_ut": pr["t_ut"], "t_wot": pr["t_wot"],
        "bo": np.asarray(inp["L0_attn_a_bo"]),
    }


def build_fused():
    nc = bass.Bass("TRN2", target_bir_lowering=False)
    P = Prog(nc)
    x_in = nc.dram_tensor("x", [T, D], F32, kind="ExternalInput").ap()
    xo_out = nc.dram_tensor("xo", [T, D], F32, kind="ExternalOutput").ap()
    build_prep(ctx=(nc, P, None, None, ""))
    P.barrier()
    seq = [build_l0aa, build_l0f, build_l0ab, build_l0f,
           build_l1aa, build_l1f, build_l1ab, build_l1f]
    cur = x_in
    for i, b in enumerate(seq):
        nxt = xo_out if i == len(seq) - 1 else P.gd(f"xi{i}", [T, D], F32,
                                                    kind="Internal")
        b(ctx=(nc, P, cur, nxt, f"_{i}"))
        P.barrier()
        cur = nxt
    P.fence_outputs()
    P.emit()
    return nc, P


def fused_inputs(inp, cst, pin, x):
    d = {k: v for k, v in pin.items() if not k.startswith("_")}
    d.update(cst)
    d["bqkvp"] = pin["_b_qkv_perm"]
    d["x"] = x
    for nm, key in [("bo", "L0_attn_a_bo"), ("gamma", "L0_attn_b_gamma"),
                    ("beta", "L0_attn_b_beta"), ("b_dt", "L0_attn_b_b_dt"),
                    ("a_log", "L0_attn_b_A_log"), ("d_par", "L0_attn_b_D_param"),
                    ("b_out", "L1_attn_a_b_out"), ("q_b", "L1_ffn_q_b"),
                    ("k_b", "L1_ffn_k_b"), ("v_b", "L1_ffn_v_b"),
                    ("o_b", "L1_ffn_o_b"), ("norm_w", "L1_ffn_norm_w"),
                    ("norm_b", "L1_ffn_norm_b"), ("out_b", "L1_attn_b_out_b")]:
        d[nm] = np.asarray(inp[key])
    d["conv_w"] = np.ascontiguousarray(np.asarray(inp["L0_attn_b_conv_w"])[:, 0, :])
    return d


def _launches():
    return [(build_l0aa, l0aa_inputs), (build_l0f, l0f_inputs),
            (build_l0ab, l0ab_inputs), (build_l0f, l0f_inputs),
            (build_l1aa, l1aa_inputs), (build_l1f, l1f_inputs),
            (build_l1ab, l1ab_inputs), (build_l1f, l1f_inputs)]


def kernel(**inp):
    return kernel_fused(**inp)


def kernel_multilaunch(**inp):
    assert np.asarray(inp["L1_ffn_router_prob"]).shape[-1] == 4
    cores = list(range(NCORES))
    x_full = np.ascontiguousarray(np.asarray(inp["L0_attn_a_x"], np.float32))
    xs = [np.ascontiguousarray(x_full[c]) for c in range(NCORES)]
    cst = host_consts()
    pin = prep_inputs(inp)
    prep_nc = build_prep()
    pr = run_bass_kernel_spmd(prep_nc, [{k: v for k, v in pin.items()
                                         if not k.startswith("_")}],
                              core_ids=[0]).results[0]
    pr = dict(pr)
    pr["_b_qkv_perm"] = pin["_b_qkv_perm"]
    nc_cache = {}
    for builder, mkin in _launches():
        if builder not in nc_cache:
            nc_cache[builder] = builder()
        nc = nc_cache[builder]
        in_maps = [mkin(inp, pr, cst, xs[c]) for c in range(NCORES)]
        rr = run_bass_kernel_spmd(nc, in_maps, core_ids=cores)
        xs = [np.ascontiguousarray(np.asarray(rr.results[c]["xo"])) for c in range(NCORES)]
    return np.stack(xs).astype(np.float32)


def kernel_fused(**inp):
    assert np.asarray(inp["L1_ffn_router_prob"]).shape[-1] == 4
    cores = list(range(NCORES))
    x_full = np.ascontiguousarray(np.asarray(inp["L0_attn_a_x"], np.float32))
    xs = [np.ascontiguousarray(x_full[c]) for c in range(NCORES)]
    cst = host_consts()
    pin = prep_inputs(inp)
    nc, _ = build_fused()
    in_maps = [fused_inputs(inp, cst, pin, xs[c]) for c in range(NCORES)]
    rr = run_bass_kernel_spmd(nc, in_maps, core_ids=cores)
    return np.stack([np.ascontiguousarray(np.asarray(rr.results[c]["xo"]))
                     for c in range(NCORES)]).astype(np.float32)


def build_l0f(ctx=None):
    nc, P, x_d, xo_d, uid, standalone = _ctx(ctx)
    E, HH, HS = 8, 1408, 2816
    NHH = HH // PT
    wg_d = P.gd("t_wg", [D, E], BF16)
    w1_d = P.gd("t_w1", [E * D, HH], BF16)
    w2_d = P.gd("t_w2", [E * D, HH], BF16)
    w3_d = P.gd("t_w3", [E * HH, D], BF16)
    ws1_d = P.gd("t_ws1", [D, HS], BF16)
    ws2_d = P.gd("t_ws2", [D, HS], BF16)
    ws3_d = P.gd("t_ws3", [HS, D], BF16)
    sc_ht = P.gd("sc_ht" + uid, [T // PT, D, PT], BF16, kind="Internal")
    NTT, NCH = T // PT, T // 512
    with contextlib.ExitStack() as st:
        cm = Common(P, st)
        cp = cm.cp
        wg = load_tiles(P, st, "wg", wg_d, D, E, BF16)
        w_sb = sb(P, st, "w_sb", [PT, NTT * E], F32)
        P._top2_scratch = (
            sb(P, st, "t2_r", [PT, E], F32), sb(P, st, "t2_m1", [PT, 2], F32),
            sb(P, st, "t2_m2", [PT, 1], F32), sb(P, st, "t2_r2", [PT, E], F32),
            sb(P, st, "t2_eq1", [PT, E], F32), sb(P, st, "t2_eq2", [PT, E], F32),
            sb(P, st, "t2_e1", [PT, E], F32), sb(P, st, "t2_den", [PT, 2], F32),
        )
        ps_r = ps(P, st, "ps_r", [PT, E], F32)
        make_hT(P, st, cm, x_d, sink_dram=sc_ht, router=(wg, w_sb, ps_r))
        yacc = [sb(P, st, f"yacc{i}", [PT, D], BF16) for i in range(NTT)]
        hTc = [sb(P, st, f"hTc{i}", [PT, 512], BF16) for i in range(KD)]
        het = [sb(P, st, f"het{i}", [PT, 512], BF16) for i in range(NHH)]
        g1sA = [sb(P, st, f"g1s{i}", [PT, 512], BF16) for i in range(2)]
        x_res = sb(P, st, "x_res", [PT, D], F32)
        o_f = sb(P, st, "o_f", [PT, D], F32)
        ps_g1A = [ps(P, st, f"ps_g1{i}", [PT, 512], F32) for i in range(2)]
        ps_g2A = [ps(P, st, f"ps_g2{i}", [PT, 512], F32) for i in range(2)]
        ps_yA = [ps(P, st, f"ps_y{i}", [PT, 512], F32) for i in range(2)]
        units = [("e", e) for e in range(E)] + [("s", 0), ("s", 1)]
        for ui, (kind, idx) in enumerate(units):
            wst = contextlib.ExitStack()
            if kind == "e":
                w1u = load_tiles(P, wst, f"w1u{ui}", w1_d[idx * D:(idx + 1) * D, :],
                                 D, HH, BF16)
                w2u = load_tiles(P, wst, f"w2u{ui}", w2_d[idx * D:(idx + 1) * D, :],
                                 D, HH, BF16)
                w3u = load_tiles(P, wst, f"w3u{ui}", w3_d[idx * HH:(idx + 1) * HH, :],
                                 HH, D, BF16)
            else:
                c0 = idx * HH
                w1u = load_tiles(P, wst, f"w1u{ui}", ws1_d[:, c0:c0 + HH], D, HH, BF16)
                w2u = load_tiles(P, wst, f"w2u{ui}", ws2_d[:, c0:c0 + HH], D, HH, BF16)
                w3u = load_tiles(P, wst, f"w3u{ui}", ws3_d[c0:c0 + HH, :], HH, D, BF16)
            for tc in range(NCH):
                for dp in range(KD):
                    for j in range(4):
                        tt = tc * 4 + j
                        P.dma("y", hTc[dp][:, j * PT:(j + 1) * PT],
                              sc_ht[tt, dp * PT:(dp + 1) * PT, :], outs=[hTc[dp]])
                for hh in range(NHH):
                    ps_g1 = ps_g1A[hh % 2]
                    ps_g2 = ps_g2A[hh % 2]
                    g1s = g1sA[hh % 2]
                    for kt in range(KD):
                        P.op("t", lambda e, hh=hh, kt=kt, w1u=w1u, ps_g1=ps_g1: e.matmul(
                            ps_g1[:, :], w1u[kt][:, hh * PT:(hh + 1) * PT], hTc[kt][:, :],
                            start=(kt == 0), stop=(kt == KD - 1)),
                            ins=[w1u[kt], hTc[kt]], outs=[ps_g1])
                    P.op("s", lambda e, ps_g1=ps_g1, g1s=g1s: e.activation(
                        g1s[:, :], ps_g1[:, :], AF.Silu), ins=[ps_g1], outs=[g1s])
                    for kt in range(KD):
                        P.op("t", lambda e, hh=hh, kt=kt, w2u=w2u, ps_g2=ps_g2: e.matmul(
                            ps_g2[:, :], w2u[kt][:, hh * PT:(hh + 1) * PT], hTc[kt][:, :],
                            start=(kt == 0), stop=(kt == KD - 1)),
                            ins=[w2u[kt], hTc[kt]], outs=[ps_g2])
                    P.op("v", lambda e, hh=hh, g1s=g1s, ps_g2=ps_g2: e.tensor_tensor(
                        het[hh][:, :], g1s[:, :], ps_g2[:, :], ALU.mult),
                        ins=[g1s, ps_g2], outs=[het[hh]])
                for j in range(4):
                    tt = tc * 4 + j
                    if ui == len(units) - 1:
                        P.dma("y", x_res[:, :], x_d[tt * PT:(tt + 1) * PT, :], outs=[x_res])
                    for n2 in range(2):
                        ps_y = ps_yA[n2]
                        for hh in range(NHH):
                            P.op("t", lambda e, j=j, n2=n2, hh=hh, w3u=w3u,
                                 ps_y=ps_y: e.matmul(
                                ps_y[:, :], het[hh][:, j * PT:(j + 1) * PT],
                                w3u[hh][:, n2 * 512:(n2 + 1) * 512],
                                start=(hh == 0), stop=(hh == NHH - 1)),
                                ins=[het[hh], w3u[hh]], outs=[ps_y])
                        ysl = yacc[tt][:, n2 * 512:(n2 + 1) * 512]
                        if kind == "e":
                            wcol = w_sb[:, tt * E + idx:tt * E + idx + 1]
                            if ui == 0:
                                P.op("v", lambda e, ysl=ysl, wcol=wcol,
                                     ps_y=ps_y: e.tensor_scalar(
                                    ysl, ps_y[:, :], wcol, None, ALU.mult),
                                    ins=[ps_y, w_sb], outs=[yacc[tt]])
                            else:
                                P.op("v", lambda e, ysl=ysl, wcol=wcol,
                                     ps_y=ps_y: e.scalar_tensor_tensor(
                                    ysl, ps_y[:, :], wcol, ysl, ALU.mult, ALU.add),
                                    ins=[ps_y, w_sb, yacc[tt]], outs=[yacc[tt]])
                        elif ui == len(units) - 2:
                            P.op("v", lambda e, ysl=ysl, ps_y=ps_y: e.scalar_tensor_tensor(
                                ysl, ps_y[:, :], 1.0, ysl, ALU.mult, ALU.add),
                                ins=[ps_y, yacc[tt]], outs=[yacc[tt]])
                        else:
                            osl = o_f[:, n2 * 512:(n2 + 1) * 512]
                            P.op("v", lambda e, ysl=ysl, osl=osl,
                                 ps_y=ps_y: e.scalar_tensor_tensor(
                                osl, ps_y[:, :], 1.0, ysl, ALU.mult, ALU.add),
                                ins=[ps_y, yacc[tt]], outs=[o_f])
                    if ui == len(units) - 1:
                        P.op("v", lambda e: e.tensor_add(o_f[:, :], o_f[:, :], x_res[:, :]),
                             ins=[o_f, x_res], outs=[o_f])
                        P.dma("g", xo_d[tt * PT:(tt + 1) * PT, :], o_f[:, :], ins=[o_f])
            P.barrier()
            wst.close()
        if standalone:
            P.fence_outputs()
            P.emit()
    return nc


def l0f_inputs(inp, pr, cst, x):
    return {"x": x, "c_ident": cst["c_ident"],
            "t_wg": pr["t_wg"], "t_w1": pr["t_w1"], "t_w2": pr["t_w2"],
            "t_w3": pr["t_w3"], "t_ws1": pr["t_ws1"], "t_ws2": pr["t_ws2"],
            "t_ws3": pr["t_ws3"]}


def build_l0ab(ctx=None):
    nc, P, x_d, xo_d, uid, standalone = _ctx(ctx)
    S = 16
    wint_d = P.gd("t_wint", [D, 2 * D], BF16)
    wdtt_d = P.gd("t_wdtt", [D, D], BF16)
    wxt_d = P.gd("t_wxt", [D, 2 * S], BF16)
    woutt_d = P.gd("t_woutt", [D, D], BF16)
    gam_d = P.gd("gamma", [D], F32)
    bet_d = P.gd("beta", [D], F32)
    bdt_d = P.gd("b_dt", [D], F32)
    alog_d = P.gd("a_log", [D, S], F32)
    dpar_d = P.gd("d_par", [D], F32)
    conv_d = P.gd("conv_w", [D, 3], F32)
    sc_ht = P.gd("sc_ht" + uid, [T // PT, D, PT], BF16, kind="Internal")
    sc_xx = P.gd("sc_xx" + uid, [D, T + 2], BF16, kind="Internal")
    sc_zs = P.gd("sc_zs" + uid, [D, T], BF16, kind="Internal")
    NCH = T // 512
    with contextlib.ExitStack() as st:
        cm = Common(P, st)
        cp = cm.cp
        ps_b = ps(P, st, "ps_bx", [PT, 512], F32)
        gw_bc = bcast_row(P, st, "gw_bc", gam_d.rearrange("(a d) -> a d", a=1), D,
                          cm.ones_f, ps_b)
        gb_bc = bcast_row(P, st, "gb_bc", bet_d.rearrange("(a d) -> a d", a=1), D,
                          cm.ones_f, ps_b)
        make_hT(P, st, cm, x_d, sink_dram=sc_ht, mode="double_affine",
                gw_bc=gw_bc, gb_bc=gb_bc)
        wdtt = load_tiles(P, st, "wdtt", wdtt_d, D, D, BF16)
        wxt = load_tiles(P, st, "wxt", wxt_d, D, 2 * S, BF16)
        woutt = load_tiles(P, st, "woutt", woutt_d, D, D, BF16)
        a_sbt = load_tiles(P, st, "a_sbt", alog_d, D, S, F32)
        for dp in range(KD):
            P.op("s", lambda e, dp=dp: e.activation(a_sbt[dp][:, :], a_sbt[dp][:, :], AF.Exp),
                 ins=[a_sbt[dp]], outs=[a_sbt[dp]])
            P.op("v", lambda e, dp=dp: e.tensor_scalar_mul(a_sbt[dp][:, :], a_sbt[dp][:, :],
                                                           -1.0), ins=[a_sbt[dp]], outs=[a_sbt[dp]])
        conv_c = load_tiles(P, st, "conv_c", conv_d, D, 3, F32)
        bdt_c = load_cols(P, st, "bdt_c", bdt_d, KD)
        dpar_c = load_cols(P, st, "dpar_c", dpar_d, KD)
        xzst = contextlib.ExitStack()
        wint = load_tiles(P, xzst, "wint", wint_d, D, 2 * D, BF16)
        hTc = [sb(P, xzst, f"hTc{i}", [PT, 512], BF16) for i in range(KD)]
        stage = sb(P, xzst, "stg", [PT, 512], BF16)
        zcol = sb(P, xzst, "zcol", [PT, 1], BF16)
        P.op("v", lambda e: e.memset(zcol[:, :], 0.0), outs=[zcol])
        ps_m = ps(P, st, "ps_m", [PT, 512], F32)
        for dp in range(KD):
            P.dma("g", sc_xx[dp * PT:(dp + 1) * PT, 0:1], zcol[:, :], ins=[zcol], slow=True)
            P.dma("g", sc_xx[dp * PT:(dp + 1) * PT, T + 1:T + 2], zcol[:, :], ins=[zcol],
                  slow=True)
        for nch in range(NCH):
            c0 = nch * 512
            for dp in range(KD):
                for j in range(4):
                    tt = nch * 4 + j
                    P.dma("y", hTc[dp][:, j * PT:(j + 1) * PT],
                          sc_ht[tt, dp * PT:(dp + 1) * PT, :], outs=[hTc[dp]])
            for ms in range(16):
                for kt in range(KD):
                    P.op("t", lambda e, ms=ms, kt=kt: e.matmul(
                        ps_m[:, :], wint[kt][:, ms * PT:(ms + 1) * PT], hTc[kt][:, :],
                        start=(kt == 0), stop=(kt == KD - 1)),
                        ins=[wint[kt], hTc[kt]], outs=[ps_m])
                if ms < 8:
                    P.op("v", lambda e: e.tensor_copy(stage[:, :], ps_m[:, :]),
                         ins=[ps_m], outs=[stage])
                    P.dma("g", sc_xx[ms * PT:(ms + 1) * PT, 1 + c0:1 + c0 + 512],
                          stage[:, :], ins=[stage])
                else:
                    P.op("s", lambda e: e.activation(stage[:, :], ps_m[:, :], AF.Silu),
                         ins=[ps_m], outs=[stage])
                    P.dma("g", sc_zs[(ms - 8) * PT:(ms - 7) * PT, c0:c0 + 512],
                          stage[:, :], ins=[stage])
        P.barrier()
        xzst.close()
        xxc = [sb(P, st, f"xxc{i}", [PT, 514], BF16) for i in range(KD)]
        xc_f = [sb(P, st, f"xcf{i}", [PT, 512], F32) for i in range(KD)]
        xc_bf = [sb(P, st, f"xcb{i}", [PT, 512], BF16) for i in range(KD)]
        dt_f = [sb(P, st, f"dtf{i}", [PT, 512], F32) for i in range(KD)]
        zs_bf = [sb(P, st, f"zsb{i}", [PT, 512], BF16) for i in range(KD)]
        y2_bf = [sb(P, st, f"y2b{i}", [PT, 512], BF16) for i in range(KD)]
        y_acc = [sb(P, st, f"yac{i}", [PT, 512], F32) for i in range(KD)]
        sp1 = sb(P, st, "sp1", [PT, 512], F32)
        sp2 = sb(P, st, "sp2", [PT, 512], F32)
        bc_sb = sb(P, st, "bc_sb", [32, 512], F32)
        brow = sb(P, st, "brow", [1, 512], F32)
        crow = sb(P, st, "crow", [1, 512], F32)
        a_t = sb(P, st, "a_t", [PT, 512], F32)
        b_t = sb(P, st, "b_t", [PT, 512], F32)
        h_t = sb(P, st, "h_t2", [PT, 512], F32)
        hc_t = sb(P, st, "hc_t", [PT, 512], F32)
        carry = sb(P, st, "carry", [PT, PT], F32)
        oT_sb = sb(P, st, "oT_sb", [PT, 512], BF16)
        out_f = [sb(P, st, f"outf{i}", [PT, D], F32) for i in range(4)]
        hof = sb(P, st, "hof", [PT, D], F32)
        tqo = sb(P, st, "tqo", [PT, D], F32)
        tco = sb(P, st, "tco", [PT, 8], F32)
        P.op("v", lambda e: e.memset(carry[:, :], 0.0), outs=[carry])
        ps_bc = ps(P, st, "ps_bc", [32, 512], F32)
        ps_bb = ps(P, st, "ps_bb", [PT, 512], F32)
        ps_cb = ps(P, st, "ps_cb", [PT, 512], F32)
        ps_tr = ps(P, st, "ps_tr", [PT, PT], BF16)
        for tc in range(NCH):
            c0 = tc * 512
            for dp in range(KD):
                P.dma("y", xxc[dp][:, :], sc_xx[dp * PT:(dp + 1) * PT, c0:c0 + 514],
                      outs=[xxc[dp]])
                P.dma("y", zs_bf[dp][:, :], sc_zs[dp * PT:(dp + 1) * PT, c0:c0 + 512],
                      outs=[zs_bf[dp]])
            for dp in range(KD):
                w0 = conv_c[dp][:, 0:1]
                w1 = conv_c[dp][:, 1:2]
                w2 = conv_c[dp][:, 2:3]
                P.op("v", lambda e, dp=dp, w1=w1: e.tensor_scalar(
                    xc_f[dp][:, :], xxc[dp][:, 1:513], w1, None, ALU.mult),
                    ins=[xxc[dp], conv_c[dp]], outs=[xc_f[dp]])
                P.op("v", lambda e, dp=dp, w0=w0: e.scalar_tensor_tensor(
                    xc_f[dp][:, :], xxc[dp][:, 0:512], w0, xc_f[dp][:, :],
                    ALU.mult, ALU.add), ins=[xxc[dp], conv_c[dp], xc_f[dp]], outs=[xc_f[dp]])
                P.op("v", lambda e, dp=dp, w2=w2: e.scalar_tensor_tensor(
                    xc_f[dp][:, :], xxc[dp][:, 2:514], w2, xc_f[dp][:, :],
                    ALU.mult, ALU.add), ins=[xxc[dp], conv_c[dp], xc_f[dp]], outs=[xc_f[dp]])
                P.op("s", lambda e, dp=dp: e.activation(xc_f[dp][:, :], xc_f[dp][:, :],
                                                        AF.Silu), ins=[xc_f[dp]], outs=[xc_f[dp]])
                P.op("v", lambda e, dp=dp: e.tensor_copy(xc_bf[dp][:, :], xc_f[dp][:, :]),
                     ins=[xc_f[dp]], outs=[xc_bf[dp]])
            for dp in range(KD):
                for kt in range(KD):
                    P.op("t", lambda e, dp=dp, kt=kt: e.matmul(
                        ps_m[:, :], wdtt[kt][:, dp * PT:(dp + 1) * PT], xc_bf[kt][:, :],
                        start=(kt == 0), stop=(kt == KD - 1)),
                        ins=[wdtt[kt], xc_bf[kt]], outs=[ps_m])
                P.op("v", lambda e, dp=dp: e.tensor_scalar(
                    dt_f[dp][:, :], ps_m[:, :], bdt_c[:, dp:dp + 1], None, ALU.add),
                    ins=[ps_m, bdt_c], outs=[dt_f[dp]])
                softplus_tile(P, cp, dt_f[dp], dt_f[dp], sp1, sp2)
            for kt in range(KD):
                P.op("t", lambda e, kt=kt: e.matmul(
                    ps_bc[:, :], wxt[kt][:, 0:32], xc_bf[kt][:, :],
                    start=(kt == 0), stop=(kt == KD - 1)),
                    ins=[wxt[kt], xc_bf[kt]], outs=[ps_bc])
            P.op("v", lambda e: e.tensor_copy(bc_sb[:, :], ps_bc[:, :]), ins=[ps_bc],
                 outs=[bc_sb])
            for s_ in range(S):
                P.dma("y", brow[:, :], bc_sb[s_:s_ + 1, :], ins=[bc_sb], outs=[brow])
                P.dma("y", crow[:, :], bc_sb[S + s_:S + s_ + 1, :], ins=[bc_sb], outs=[crow])
                P.op("t", lambda e: e.matmul(ps_bb[:, :], cm.ones_f[0:1, :], brow[:, :],
                                             start=True, stop=True),
                     ins=[cm.ones_f, brow], outs=[ps_bb])
                P.op("t", lambda e: e.matmul(ps_cb[:, :], cm.ones_f[0:1, :], crow[:, :],
                                             start=True, stop=True),
                     ins=[cm.ones_f, crow], outs=[ps_cb])
                for dp in range(KD):
                    acol = a_sbt[dp][:, s_:s_ + 1]
                    P.op("s", lambda e, dp=dp, acol=acol: e.activation(
                        a_t[:, :], dt_f[dp][:, :], AF.Exp, scale=acol),
                        ins=[dt_f[dp], a_sbt[dp]], outs=[a_t])
                    P.op("v", lambda e, dp=dp: e.tensor_tensor(
                        b_t[:, :], dt_f[dp][:, :], ps_bb[:, :], ALU.mult),
                        ins=[dt_f[dp], ps_bb], outs=[b_t])
                    ccol = carry[:, dp * S + s_:dp * S + s_ + 1]
                    P.op("v", lambda e, ccol=ccol: e.tensor_tensor_scan(
                        h_t[:, :], a_t[:, :], b_t[:, :], ccol, ALU.mult, ALU.add),
                        ins=[a_t, b_t, carry], outs=[h_t])
                    P.op("v", lambda e, ccol=ccol: e.tensor_copy(ccol, h_t[:, 511:512]),
                         ins=[h_t], outs=[carry])
                    P.op("v", lambda e: e.tensor_tensor(hc_t[:, :], h_t[:, :], ps_cb[:, :],
                                                        ALU.mult), ins=[h_t, ps_cb], outs=[hc_t])
                    if s_ == 0:
                        P.op("v", lambda e, dp=dp: e.tensor_copy(y_acc[dp][:, :], hc_t[:, :]),
                             ins=[hc_t], outs=[y_acc[dp]])
                    else:
                        P.op("v", lambda e, dp=dp: e.tensor_add(
                            y_acc[dp][:, :], y_acc[dp][:, :], hc_t[:, :]),
                            ins=[hc_t, y_acc[dp]], outs=[y_acc[dp]])
            for dp in range(KD):
                P.op("v", lambda e, dp=dp: e.scalar_tensor_tensor(
                    y_acc[dp][:, :], xc_f[dp][:, :], dpar_c[:, dp:dp + 1], y_acc[dp][:, :],
                    ALU.mult, ALU.add), ins=[xc_f[dp], dpar_c, y_acc[dp]], outs=[y_acc[dp]])
                P.op("v", lambda e, dp=dp: e.tensor_tensor(
                    y2_bf[dp][:, :], y_acc[dp][:, :], zs_bf[dp][:, :], ALU.mult),
                    ins=[y_acc[dp], zs_bf[dp]], outs=[y2_bf[dp]])
            for j in range(4):
                tt = tc * 4 + j
                P.dma("y", out_f[j][:, :], x_d[tt * PT:(tt + 1) * PT, :], outs=[out_f[j]])
                ln_tile(P, cp, out_f[j], hof, tco, tqo)
                P.op("v", lambda e, j=j: e.tensor_add(out_f[j][:, :], out_f[j][:, :],
                                                      hof[:, :]),
                     ins=[hof, out_f[j]], outs=[out_f[j]])
            for dpo in range(KD):
                for kt in range(KD):
                    P.op("t", lambda e, dpo=dpo, kt=kt: e.matmul(
                        ps_m[:, :], woutt[kt][:, dpo * PT:(dpo + 1) * PT], y2_bf[kt][:, :],
                        start=(kt == 0), stop=(kt == KD - 1)),
                        ins=[woutt[kt], y2_bf[kt]], outs=[ps_m])
                P.op("v", lambda e: e.tensor_copy(oT_sb[:, :], ps_m[:, :]),
                     ins=[ps_m], outs=[oT_sb])
                for j in range(4):
                    P.op("t", lambda e, j=j: e.transpose(
                        ps_tr[:, :], oT_sb[:, j * PT:(j + 1) * PT], cm.ident[:, :]),
                        ins=[oT_sb, cm.ident], outs=[ps_tr])
                    osl = out_f[j][:, dpo * PT:(dpo + 1) * PT]
                    P.op("v", lambda e, osl=osl, j=j: e.scalar_tensor_tensor(
                        osl, ps_tr[:, :], 1.0, osl, ALU.mult, ALU.add),
                        ins=[ps_tr, out_f[j]], outs=[out_f[j]])
            for j in range(4):
                tt = tc * 4 + j
                P.dma("g", xo_d[tt * PT:(tt + 1) * PT, :], out_f[j][:, :], ins=[out_f[j]])
        if standalone:
            P.fence_outputs()
            P.emit()
    return nc


def l0ab_inputs(inp, pr, cst, x):
    return {"x": x, "c_ident": cst["c_ident"],
            "t_wint": pr["t_wint"], "t_wdtt": pr["t_wdtt"], "t_wxt": pr["t_wxt"],
            "t_woutt": pr["t_woutt"],
            "gamma": np.asarray(inp["L0_attn_b_gamma"]),
            "beta": np.asarray(inp["L0_attn_b_beta"]),
            "b_dt": np.asarray(inp["L0_attn_b_b_dt"]),
            "a_log": np.asarray(inp["L0_attn_b_A_log"]),
            "d_par": np.asarray(inp["L0_attn_b_D_param"]),
            "conv_w": np.ascontiguousarray(np.asarray(inp["L0_attn_b_conv_w"])[:, 0, :])}


def build_l1aa(ctx=None):
    nc, P, x_d, xo_d, uid, standalone = _ctx(ctx)
    NH, HD = 16, 64
    wqkv_d = P.gd("t_wqkv", [D, 3 * D], BF16)
    wout_d = P.gd("t_wout1a", [D, D], BF16)
    bqkv_d = P.gd("bqkvp", [3 * D], F32)
    bout_d = P.gd("b_out", [D], F32)
    cosp_d = P.gd("c_cosp", [PT, T], BF16)
    sinp_d = P.gd("c_sinp", [PT, T], BF16)
    psw_d = P.gd("c_pswap", [PT, PT], BF16)
    sc_ht = P.gd("sc_ht" + uid, [T // PT, D, PT], BF16, kind="Internal")
    sc_qte = P.gd("sc_qte" + uid, [D, T], BF16, kind="Internal")
    sc_rq = P.gd("sc_rq" + uid, [D, T], BF16, kind="Internal")
    sc_yt = P.gd("sc_yt" + uid, [D, T], BF16, kind="Internal")
    NCH, NTT = T // 512, T // PT
    with contextlib.ExitStack() as st:
        cm = Common(P, st)
        make_hT(P, st, cm, x_d, sink_dram=sc_ht)
        cosp = sb(P, st, "cosp", [PT, T], BF16)
        sinp = sb(P, st, "sinp", [PT, T], BF16)
        P.dma("y", cosp[:, :], cosp_d[:, :], outs=[cosp])
        P.dma("y", sinp[:, :], sinp_d[:, :], outs=[sinp])
        psw = sb(P, st, "psw", [PT, PT], BF16)
        P.dma("y", psw[:, :], psw_d[:, :], outs=[psw])
        bq_c = load_cols(P, st, "bq_c", bqkv_d[0:D], KD)
        bk_c = load_cols(P, st, "bk_c", bqkv_d[D:2 * D], KD)
        bv_c = load_cols(P, st, "bv_c", bqkv_d[2 * D:3 * D], KD)
        hTc = [sb(P, st, f"hTc{i}", [PT, 512], BF16) for i in range(KD)]
        ps_m = ps(P, st, "ps_m", [PT, 512], F32)
        ps_sw = ps(P, st, "ps_sw", [PT, 512], F32)
        qraw = sb(P, st, "qraw", [PT, 512], F32)
        e1t = sb(P, st, "e1t", [PT, 512], F32)
        t2t = sb(P, st, "t2t", [PT, 512], F32)
        r1t = sb(P, st, "r1t", [PT, 512], F32)
        r2t = sb(P, st, "r2t", [PT, 512], F32)

        def elu1(dst_ap, dst_h, bias_col, bias_h):
            P.op("v", lambda e: e.tensor_scalar(qraw[:, :], ps_m[:, :], bias_col,
                                                None, ALU.add),
                 ins=[ps_m, bias_h], outs=[qraw])
            P.op("v", lambda e: e.tensor_scalar_min(e1t[:, :], qraw[:, :], 0.0),
                 ins=[qraw], outs=[e1t])
            P.op("s", lambda e: e.activation(e1t[:, :], e1t[:, :], AF.Exp),
                 ins=[e1t], outs=[e1t])
            P.op("v", lambda e: e.tensor_scalar_max(t2t[:, :], qraw[:, :], 0.0),
                 ins=[qraw], outs=[t2t])
            P.op("v", lambda e: e.tensor_add(dst_ap, t2t[:, :], e1t[:, :]),
                 ins=[t2t, e1t], outs=[dst_h])

        def rope(dst_ap, dst_h, src_ap, src_h, c0):
            P.op("t", lambda e: e.matmul(ps_sw[:, :], psw[:, :], src_ap,
                                         start=True, stop=True),
                 ins=[psw, src_h], outs=[ps_sw])
            P.op("v", lambda e, c0=c0: e.tensor_tensor(r1t[:, :], src_ap,
                                                       cosp[:, c0:c0 + 512], ALU.mult),
                 ins=[src_h, cosp], outs=[r1t])
            P.op("v", lambda e, c0=c0: e.tensor_tensor(r2t[:, :], ps_sw[:, :],
                                                       sinp[:, c0:c0 + 512], ALU.mult),
                 ins=[ps_sw, sinp], outs=[r2t])
            P.op("v", lambda e: e.tensor_add(dst_ap, r1t[:, :], r2t[:, :]),
                 ins=[r1t, r2t], outs=[dst_h])

        def load_htc(nch):
            for dp in range(KD):
                for j in range(4):
                    tt = nch * 4 + j
                    P.dma("y", hTc[dp][:, j * PT:(j + 1) * PT],
                          sc_ht[tt, dp * PT:(dp + 1) * PT, :], outs=[hTc[dp]])

        qst = contextlib.ExitStack()
        wq = load_tiles(P, qst, "wq", wqkv_d[:, 0:D], D, D, BF16)
        qte_c = [sb(P, qst, f"qtec{i}", [PT, 512], BF16) for i in range(KD)]
        rq_c = [sb(P, qst, f"rqc{i}", [PT, 512], BF16) for i in range(KD)]
        for nch in range(NCH):
            c0 = nch * 512
            load_htc(nch)
            for dp in range(KD):
                for kt in range(KD):
                    P.op("t", lambda e, dp=dp, kt=kt, wq=wq: e.matmul(
                        ps_m[:, :], wq[kt][:, dp * PT:(dp + 1) * PT], hTc[kt][:, :],
                        start=(kt == 0), stop=(kt == KD - 1)),
                        ins=[wq[kt], hTc[kt]], outs=[ps_m])
                elu1(qte_c[dp][:, :], qte_c[dp], bq_c[:, dp:dp + 1], bq_c)
                rope(rq_c[dp][:, :], rq_c[dp], qte_c[dp][:, :], qte_c[dp], c0)
                P.dma("g", sc_qte[dp * PT:(dp + 1) * PT, c0:c0 + 512], qte_c[dp][:, :],
                      ins=[qte_c[dp]])
                P.dma("g", sc_rq[dp * PT:(dp + 1) * PT, c0:c0 + 512], rq_c[dp][:, :],
                      ins=[rq_c[dp]])
        P.barrier()
        qst.close()
        den_fA = [sb(P, st, f"den_f{i}", [PT, 512], F32) for i in range(2)]
        slabA = [sb(P, st, f"slab{i}", [PT, 512], BF16) for i in range(2)]
        ps_n = ps(P, st, "ps_n", [PT, 512], F32)
        ps_d = ps(P, st, "ps_d", [PT, 512], F32)
        ps_nA = [ps_n, ps_m]
        ps_dA = [ps_d, ps_sw]
        ps_y = ps(P, st, "ps_y", [HD, 512], F32)
        ps_tv = ps(P, st, "ps_tv", [PT, HD], BF16)
        for hg in range(2):
            gst = contextlib.ExitStack()
            kte = [sb(P, gst, f"kte{hg}{i}", [PT, T], BF16) for i in range(4)]
            rkT = [sb(P, gst, f"rkT{hg}{i}", [PT, T], BF16) for i in range(4)]
            vh_all = [sb(P, gst, f"vh{hg}_{i}", [PT, NTT * HD], BF16) for i in range(8)]
            qe_c = [sb(P, gst, f"qec{hg}_{i}", [PT, 512], BF16) for i in range(4)]
            rqc2 = [sb(P, gst, f"rqc2{hg}_{i}", [PT, 512], BF16) for i in range(4)]
            yt_sb = [sb(P, gst, f"yt{hg}_{i}", [PT, 512], BF16) for i in range(4)]
            vst = contextlib.ExitStack()
            wk = load_tiles(P, vst, f"wk{hg}", wqkv_d[:, D + hg * 512:D + hg * 512 + 512],
                            D, 512, BF16)
            wv = load_tiles(P, vst, f"wv{hg}", wqkv_d[:, 2 * D + hg * 512:2 * D + hg * 512 + 512],
                            D, 512, BF16)
            vT = [sb(P, vst, f"vT{hg}{i}", [PT, T], BF16) for i in range(4)]
            for nch in range(NCH):
                c0 = nch * 512
                load_htc(nch)
                for dl in range(4):
                    dg = hg * 4 + dl
                    for kt in range(KD):
                        P.op("t", lambda e, dl=dl, kt=kt, wk=wk: e.matmul(
                            ps_m[:, :], wk[kt][:, dl * PT:(dl + 1) * PT], hTc[kt][:, :],
                            start=(kt == 0), stop=(kt == KD - 1)),
                            ins=[wk[kt], hTc[kt]], outs=[ps_m])
                    elu1(kte[dl][:, c0:c0 + 512], kte[dl], bk_c[:, dg:dg + 1], bk_c)
                    for kt in range(KD):
                        P.op("t", lambda e, dl=dl, kt=kt, wv=wv: e.matmul(
                            ps_m[:, :], wv[kt][:, dl * PT:(dl + 1) * PT], hTc[kt][:, :],
                            start=(kt == 0), stop=(kt == KD - 1)),
                            ins=[wv[kt], hTc[kt]], outs=[ps_m])
                    P.op("v", lambda e, dl=dl, c0=c0, dg=dg, vT=vT: e.tensor_scalar(
                        vT[dl][:, c0:c0 + 512], ps_m[:, :], bv_c[:, dg:dg + 1],
                        None, ALU.add), ins=[ps_m, bv_c], outs=[vT[dl]])
            for dl in range(4):
                for nch in range(NCH):
                    c0 = nch * 512
                    rope(rkT[dl][:, c0:c0 + 512], rkT[dl], kte[dl][:, c0:c0 + 512],
                         kte[dl], c0)
            for hl in range(8):
                b = (hl % 2) * HD
                vh_sb = vh_all[hl]
                for tt in range(NTT):
                    P.op("t", lambda e, hl=hl, tt=tt, b=b, vT=vT: e.transpose(
                        ps_tv[:, :], vT[hl // 2][b:b + HD, tt * PT:(tt + 1) * PT],
                        cm.ident[b:b + HD, b:b + HD]),
                        ins=[vT[hl // 2], cm.ident], outs=[ps_tv])
                    P.op("v", lambda e, tt=tt, vh_sb=vh_sb: e.tensor_copy(
                        vh_sb[:, tt * HD:(tt + 1) * HD], ps_tv[:, :]),
                        ins=[ps_tv], outs=[vh_sb])
            P.barrier()
            vst.close()
            for qc in range(NCH):
                c0 = qc * 512
                for dl in range(4):
                    dq = hg * 4 + dl
                    P.dma("y", qe_c[dl][:, :],
                          sc_qte[dq * PT:(dq + 1) * PT, c0:c0 + 512], outs=[qe_c[dl]])
                    P.dma("y", rqc2[dl][:, :],
                          sc_rq[dq * PT:(dq + 1) * PT, c0:c0 + 512], outs=[rqc2[dl]])
                for hl in range(8):
                    b = (hl % 2) * HD
                    dl = hl // 2
                    vh_sb = vh_all[hl]
                    for kt in range(NTT):
                        psn, psd = ps_nA[kt % 2], ps_dA[kt % 2]
                        den_f, slab = den_fA[kt % 2], slabA[kt % 2]
                        P.op("t", lambda e, dl=dl, kt=kt, b=b, rkT=rkT,
                             rqc2=rqc2, psn=psn: e.matmul(
                            psn[:, :], rkT[dl][b:b + HD, kt * PT:(kt + 1) * PT],
                            rqc2[dl][b:b + HD, :], start=True, stop=True),
                            ins=[rkT[dl], rqc2[dl]], outs=[psn])
                        P.op("t", lambda e, dl=dl, kt=kt, b=b, kte=kte,
                             qe_c=qe_c, psd=psd: e.matmul(
                            psd[:, :], kte[dl][b:b + HD, kt * PT:(kt + 1) * PT],
                            qe_c[dl][b:b + HD, :], start=True, stop=True),
                            ins=[kte[dl], qe_c[dl]], outs=[psd])
                        P.op("v", lambda e, psd=psd, den_f=den_f: e.reciprocal(
                            den_f[:, :], psd[:, :]), ins=[psd], outs=[den_f])
                        P.op("v", lambda e, psn=psn, den_f=den_f, slab=slab:
                             e.tensor_tensor(slab[:, :], psn[:, :],
                                             den_f[:, :], ALU.mult),
                             ins=[psn, den_f], outs=[slab])
                        P.op("t", lambda e, kt=kt, vh_sb=vh_sb, slab=slab: e.matmul(
                            ps_y[:, :], vh_sb[:, kt * HD:(kt + 1) * HD], slab[:, :],
                            start=(kt == 0), stop=(kt == NTT - 1)),
                            ins=[vh_sb, slab], outs=[ps_y])
                    P.op("v", lambda e, dl=dl, b=b, yt_sb=yt_sb: e.tensor_copy(
                        yt_sb[dl][b:b + HD, :], ps_y[:, :]),
                        ins=[ps_y], outs=[yt_sb[dl]])
                for dl in range(4):
                    P.dma("g", sc_yt[hg * 512 + dl * PT:hg * 512 + (dl + 1) * PT,
                                     c0:c0 + 512], yt_sb[dl][:, :], ins=[yt_sb[dl]])
            P.barrier()
            gst.close()
        wout = load_tiles(P, st, "wout", wout_d, D, D, BF16)
        ps_b2 = ps(P, st, "ps_b2", [PT, 512], F32)
        bo_bc = bcast_row(P, st, "bo_bc", bout_d.rearrange("(a d) -> a d", a=1), D,
                          cm.ones_f, ps_b2)
        ytq = [sb(P, st, f"ytq{i}", [PT, PT], BF16) for i in range(KD)]
        x_res = sb(P, st, "x_res", [PT, D], F32)
        o_f = sb(P, st, "o_f", [PT, D], F32)
        ps_o = ps(P, st, "ps_o", [PT, 512], F32)
        for qt in range(NTT):
            for kt in range(KD):
                P.dma("y", ytq[kt][:, :], sc_yt[kt * PT:(kt + 1) * PT,
                                                qt * PT:(qt + 1) * PT], outs=[ytq[kt]])
            P.dma("y", x_res[:, :], x_d[qt * PT:(qt + 1) * PT, :], outs=[x_res])
            for n2 in range(2):
                for kt in range(KD):
                    P.op("t", lambda e, n2=n2, kt=kt: e.matmul(
                        ps_o[:, :], ytq[kt][:, :], wout[kt][:, n2 * 512:(n2 + 1) * 512],
                        start=(kt == 0), stop=(kt == KD - 1)),
                        ins=[ytq[kt], wout[kt]], outs=[ps_o])
                P.op("v", lambda e, n2=n2: e.tensor_tensor(
                    o_f[:, n2 * 512:(n2 + 1) * 512], ps_o[:, :],
                    bo_bc[:, n2 * 512:(n2 + 1) * 512], ALU.add),
                    ins=[ps_o, bo_bc], outs=[o_f])
            P.op("v", lambda e: e.tensor_add(o_f[:, :], o_f[:, :], x_res[:, :]),
                 ins=[o_f, x_res], outs=[o_f])
            P.dma("g", xo_d[qt * PT:(qt + 1) * PT, :], o_f[:, :], ins=[o_f])
        if standalone:
            P.fence_outputs()
            P.emit()
    return nc


def l1aa_inputs(inp, pr, cst, x):
    return {"x": x, "c_ident": cst["c_ident"], "c_cosp": cst["c_cosp"],
            "c_sinp": cst["c_sinp"], "c_pswap": cst["c_pswap"],
            "t_wqkv": pr["t_wqkv"], "t_wout1a": pr["t_wout1a"],
            "bqkvp": pr["_b_qkv_perm"], "b_out": np.asarray(inp["L1_attn_a_b_out"])}


def build_l1f(ctx=None):
    nc, P, x_d, xo_d, uid, standalone = _ctx(ctx)
    NHD = 8
    qwt_d = P.gd("t_qwt", [D, D], BF16)
    kwt_d = P.gd("t_kwt", [D, D], BF16)
    vwt_d = P.gd("t_vwt", [D, D], BF16)
    owt_d = P.gd("t_owt", [D, D], BF16)
    qb_d = P.gd("q_b", [D], F32)
    kb_d = P.gd("k_b", [D], F32)
    vb_d = P.gd("v_b", [D], F32)
    ob_d = P.gd("o_b", [D], F32)
    nw_d = P.gd("norm_w", [D], F32)
    nb_d = P.gd("norm_b", [D], F32)
    sc_ht = P.gd("sc_ht" + uid, [T // PT, D, PT], BF16, kind="Internal")
    sc_q = P.gd("sc_q" + uid, [D, T], BF16, kind="Internal")
    NCH, NTT = T // 512, T // PT
    with contextlib.ExitStack() as st:
        cm = Common(P, st)
        cp = cm.cp
        ps_bx = ps(P, st, "ps_bx", [PT, 512], F32)
        ob_bc = bcast_row(P, st, "ob_bc", ob_d.rearrange("(a d) -> a d", a=1), D,
                          cm.ones_f, ps_bx)
        nst = contextlib.ExitStack()
        gw_bc = bcast_row(P, nst, "gw_bc", nw_d.rearrange("(a d) -> a d", a=1), D,
                          cm.ones_f, ps_bx)
        gb_bc = bcast_row(P, nst, "gb_bc", nb_d.rearrange("(a d) -> a d", a=1), D,
                          cm.ones_f, ps_bx)
        make_hT(P, st, cm, x_d, sink_dram=sc_ht, mode="double_affine",
                gw_bc=gw_bc, gb_bc=gb_bc)
        P.barrier()
        nst.close()
        qb_c = load_cols(P, st, "qb_c", qb_d, KD)
        kb_c = load_cols(P, st, "kb_c", kb_d, KD)
        vb_row = sb(P, st, "vb_row", [1, D], F32)
        P.dma("y", vb_row[:, :], vb_d.rearrange("(a d) -> a d", a=1), outs=[vb_row])
        vb_bf = sb(P, st, "vb_bf", [1, D], BF16)
        P.op("v", lambda e: e.tensor_copy(vb_bf[:, :], vb_row[:, :]),
             ins=[vb_row], outs=[vb_bf])
        kT = [sb(P, st, f"kT{i}", [PT, T], BF16) for i in range(KD)]
        v_sb = [sb(P, st, f"v{i}", [PT, D], BF16) for i in range(NTT)]
        hTt = [sb(P, st, f"hTt{i}", [PT, PT], BF16) for i in range(KD)]
        ps_m = ps(P, st, "ps_m", [PT, 512], F32)
        for wpass in range(3):
            bst = contextlib.ExitStack()
            wname = ["kwt", "qwt", "vwt"][wpass]
            wd = [kwt_d, qwt_d, vwt_d][wpass]
            wt = load_tiles(P, bst, wname, wd, D, D, BF16)
            qstage = sb(P, bst, f"qstage{wpass}", [PT, PT], BF16)
            for tt in range(NTT):
                seg = tt // 8
                mk = 8 >> seg
                for kt in range(mk):
                    P.dma("y", hTt[kt][:, :], sc_ht[tt, kt * PT:(kt + 1) * PT, :],
                          outs=[hTt[kt]])
                if wpass == 0:
                    for dpo in range(KD):
                        for kt in range(mk):
                            P.op("t", lambda e, dpo=dpo, kt=kt, mk=mk, wt=wt: e.matmul(
                                ps_m[:, 0:PT], wt[kt][:, dpo * PT:(dpo + 1) * PT],
                                hTt[kt][:, :],
                                start=(kt == 0), stop=(kt == mk - 1)),
                                ins=[wt[kt], hTt[kt]], outs=[ps_m])
                        P.op("v", lambda e, dpo=dpo, tt=tt: e.tensor_scalar(
                            kT[dpo][:, tt * PT:(tt + 1) * PT], ps_m[:, 0:PT],
                            kb_c[:, dpo:dpo + 1], None, ALU.add),
                            ins=[ps_m, kb_c], outs=[kT[dpo]])
                elif wpass == 1:
                    for dpo in range(KD):
                        for kt in range(mk):
                            P.op("t", lambda e, dpo=dpo, kt=kt, mk=mk, wt=wt: e.matmul(
                                ps_m[:, 0:PT], wt[kt][:, dpo * PT:(dpo + 1) * PT],
                                hTt[kt][:, :],
                                start=(kt == 0), stop=(kt == mk - 1)),
                                ins=[wt[kt], hTt[kt]], outs=[ps_m])
                        P.op("v", lambda e, dpo=dpo, qstage=qstage: e.tensor_scalar(
                            qstage[:, :], ps_m[:, 0:PT],
                            qb_c[:, dpo:dpo + 1], None, ALU.add),
                            ins=[ps_m, qb_c], outs=[qstage])
                        P.dma("g", sc_q[dpo * PT:(dpo + 1) * PT, tt * PT:(tt + 1) * PT],
                              qstage[:, :], ins=[qstage])
                else:
                    for n2 in range(2):
                        for kt in range(mk):
                            P.op("t", lambda e, n2=n2, kt=kt, mk=mk, wt=wt: e.matmul(
                                ps_m[:, :], hTt[kt][:, :],
                                wt[kt][:, n2 * 512:(n2 + 1) * 512],
                                start=(kt == 0), stop=False),
                                ins=[hTt[kt], wt[kt]], outs=[ps_m])
                        P.op("t", lambda e, n2=n2: e.matmul(
                            ps_m[:, :], cm.ones_bf[0:1, :],
                            vb_bf[0:1, n2 * 512:(n2 + 1) * 512],
                            start=False, stop=True), ins=[cm.ones_bf, vb_bf], outs=[ps_m])
                        P.op("v", lambda e, n2=n2, tt=tt: e.tensor_copy(
                            v_sb[tt][:, n2 * 512:(n2 + 1) * 512], ps_m[:, :]),
                            ins=[ps_m], outs=[v_sb[tt]])
            P.barrier()
            bst.close()
        owt = load_tiles(P, st, "owt", owt_d, D, D, BF16)
        qc_sb = [sb(P, st, f"qc{i}", [PT, 512], BF16) for i in range(KD)]
        slabA = [sb(P, st, f"slab{i}", [PT, 512], BF16) for i in range(2)]
        linv = sb(P, st, "linv", [1, 512], F32)
        ytf = sb(P, st, "ytf", [PT, 512], F32)
        yt_sb = [sb(P, st, f"yts{i}", [PT, 512], BF16) for i in range(KD)]
        x_res = sb(P, st, "x_res", [PT, D], F32)
        ho_f = sb(P, st, "ho_f", [PT, D], F32)
        tqo = sb(P, st, "tqo", [PT, D], F32)
        tco = sb(P, st, "tco", [PT, 8], F32)
        ps_s = ps(P, st, "ps_s", [PT, 512], F32)
        ps_sA = [ps_s, ps_m]
        ps_l = ps(P, st, "ps_l", [1, 512], F32)
        ps_lb = ps(P, st, "ps_lb", [PT, 512], F32)
        ps_yT = ps(P, st, "ps_yT", [PT, 512], F32)
        ps_o = ps(P, st, "ps_o", [PT, 512], F32)
        c_sc = cp.col(1.0 / 32.0)
        for qc in range(NCH):
            c0 = qc * 512
            for dp in range(KD):
                P.dma("y", qc_sb[dp][:, :], sc_q[dp * PT:(dp + 1) * PT, c0:c0 + 512],
                      outs=[qc_sb[dp]])
            for h in range(NHD):
                for kt in range(NTT):
                    pss = ps_sA[kt % 2]
                    slab = slabA[kt % 2]
                    P.op("t", lambda e, h=h, kt=kt, pss=pss: e.matmul(
                        pss[:, :], kT[h][:, kt * PT:(kt + 1) * PT], qc_sb[h][:, :],
                        start=True, stop=True), ins=[kT[h], qc_sb[h]], outs=[pss])
                    P.op("s", lambda e, pss=pss, slab=slab: e.activation(
                        slab[:, :], pss[:, :], AF.Exp, scale=c_sc),
                         ins=[pss, cp.tile], outs=[slab])
                    P.op("t", lambda e, kt=kt, slab=slab: e.matmul(
                        ps_l[:, :], cm.ones_bf[:, 0:1], slab[:, :],
                        start=(kt == 0), stop=(kt == NTT - 1)),
                        ins=[cm.ones_bf, slab], outs=[ps_l])
                    P.op("t", lambda e, h=h, kt=kt, slab=slab: e.matmul(
                        ps_yT[:, :], v_sb[kt][:, h * PT:(h + 1) * PT], slab[:, :],
                        start=(kt == 0), stop=(kt == NTT - 1)),
                        ins=[v_sb[kt], slab], outs=[ps_yT])
                P.op("v", lambda e: e.reciprocal(linv[:, :], ps_l[:, :]),
                     ins=[ps_l], outs=[linv])
                P.op("t", lambda e: e.matmul(ps_lb[:, :], cm.ones_f[0:1, :], linv[:, :],
                                             start=True, stop=True),
                     ins=[cm.ones_f, linv], outs=[ps_lb])
                P.op("s", lambda e: e.copy(ytf[:, :], ps_yT[:, :]), ins=[ps_yT], outs=[ytf])
                P.op("v", lambda e, h=h: e.tensor_tensor(yt_sb[h][:, :], ytf[:, :],
                                                         ps_lb[:, :], ALU.mult),
                     ins=[ytf, ps_lb], outs=[yt_sb[h]])
            seg = qc // 2
            m = D >> seg
            mk = KD >> seg
            for qt in range(4):
                row = c0 + qt * PT
                P.dma("y", x_res[:, :], x_d[row:row + PT, :], outs=[x_res])
                ln_tile(P, cp, x_res, ho_f, tco, tqo)
                P.op("v", lambda e: e.tensor_add(ho_f[:, :], ho_f[:, :], x_res[:, :]),
                     ins=[x_res, ho_f], outs=[ho_f])
                for n2 in range((m + 511) // 512):
                    w = min(512, m - n2 * 512)
                    for kt in range(mk):
                        P.op("t", lambda e, qt=qt, n2=n2, kt=kt, w=w, mk=mk: e.matmul(
                            ps_o[:, 0:w], yt_sb[kt][:, qt * PT:(qt + 1) * PT],
                            owt[kt][:, n2 * 512:n2 * 512 + w],
                            start=(kt == 0), stop=(kt == mk - 1)),
                            ins=[yt_sb[kt], owt[kt]], outs=[ps_o])
                    osl = ho_f[:, n2 * 512:n2 * 512 + w]
                    P.op("v", lambda e, osl=osl, w=w: e.scalar_tensor_tensor(
                        osl, ps_o[:, 0:w], 1.0, osl, ALU.mult, ALU.add),
                        ins=[ps_o, ho_f], outs=[ho_f])
                    obsl = ob_bc[:, n2 * 512:n2 * 512 + w]
                    P.op("v", lambda e, osl=osl, obsl=obsl: e.tensor_add(osl, osl, obsl),
                         ins=[ob_bc, ho_f], outs=[ho_f])
                P.dma("g", xo_d[row:row + PT, :], ho_f[:, :], ins=[ho_f])
        if standalone:
            P.fence_outputs()
            P.emit()
    return nc


def l1f_inputs(inp, pr, cst, x):
    return {"x": x, "c_ident": cst["c_ident"],
            "t_qwt": pr["t_qwt"], "t_kwt": pr["t_kwt"], "t_vwt": pr["t_vwt"],
            "t_owt": pr["t_owt"],
            "q_b": np.asarray(inp["L1_ffn_q_b"]), "k_b": np.asarray(inp["L1_ffn_k_b"]),
            "v_b": np.asarray(inp["L1_ffn_v_b"]), "o_b": np.asarray(inp["L1_ffn_o_b"]),
            "norm_w": np.asarray(inp["L1_ffn_norm_w"]),
            "norm_b": np.asarray(inp["L1_ffn_norm_b"])}



def build_l1ab(ctx=None):
    nc, P, x_d, xo_d, uid, standalone = _ctx(ctx)
    NH, HD = 16, 64
    qwt_d = P.gd("t_qwt2", [D, D], BF16)
    kwt_d = P.gd("t_kwt2", [D, D], BF16)
    vwt_d = P.gd("t_vwt2", [D, D], BF16)
    owt_d = P.gd("t_owt2", [D, D], BF16)
    ob_d = P.gd("out_b", [D], F32)
    hsel_d = P.gd("c_hsel", [16, D], BF16)
    sc_ht = P.gd("sc_ht" + uid, [T // PT, D, PT], BF16, kind="Internal")
    sc_q = P.gd("sc_q" + uid, [D, T], BF16, kind="Internal")
    NCH, NTT = T // 512, T // PT
    with contextlib.ExitStack() as st:
        cm = Common(P, st)
        cp = cm.cp
        make_hT(P, st, cm, x_d, sink_dram=sc_ht)
        hsel = sb(P, st, "hsel", [16, D], BF16)
        P.dma("y", hsel[:, :], hsel_d[:, :], outs=[hsel])
        kT = [sb(P, st, f"kT{i}", [PT, T], BF16) for i in range(KD)]
        v_sb = [sb(P, st, f"v{i}", [PT, D], BF16) for i in range(NTT)]
        hTt = [sb(P, st, f"hTt{i}", [PT, PT], BF16) for i in range(KD)]
        ps_m = ps(P, st, "ps_m", [PT, 512], F32)
        bst = contextlib.ExitStack()
        qwt = load_tiles(P, bst, "qwt", qwt_d, D, D, BF16)
        kwt = load_tiles(P, bst, "kwt", kwt_d, D, D, BF16)
        vwt = load_tiles(P, bst, "vwt", vwt_d, D, D, BF16)
        qstage = sb(P, bst, "qstage", [PT, PT], BF16)
        for tt in range(NTT):
            for kt in range(KD):
                P.dma("y", hTt[kt][:, :], sc_ht[tt, kt * PT:(kt + 1) * PT, :],
                      outs=[hTt[kt]])
            for dpo in range(KD):
                for kt in range(KD):
                    P.op("t", lambda e, dpo=dpo, kt=kt: e.matmul(
                        ps_m[:, 0:PT], kwt[kt][:, dpo * PT:(dpo + 1) * PT], hTt[kt][:, :],
                        start=(kt == 0), stop=(kt == KD - 1)),
                        ins=[kwt[kt], hTt[kt]], outs=[ps_m])
                P.op("v", lambda e, dpo=dpo, tt=tt: e.tensor_copy(
                    kT[dpo][:, tt * PT:(tt + 1) * PT], ps_m[:, 0:PT]),
                    ins=[ps_m], outs=[kT[dpo]])
                for kt in range(KD):
                    P.op("t", lambda e, dpo=dpo, kt=kt: e.matmul(
                        ps_m[:, 0:PT], qwt[kt][:, dpo * PT:(dpo + 1) * PT], hTt[kt][:, :],
                        start=(kt == 0), stop=(kt == KD - 1)),
                        ins=[qwt[kt], hTt[kt]], outs=[ps_m])
                P.op("v", lambda e, dpo=dpo: e.tensor_scalar_mul(
                    qstage[:, :], ps_m[:, 0:PT], 0.125), ins=[ps_m], outs=[qstage])
                P.dma("g", sc_q[dpo * PT:(dpo + 1) * PT, tt * PT:(tt + 1) * PT],
                      qstage[:, :], ins=[qstage])
            for n2 in range(2):
                for kt in range(KD):
                    P.op("t", lambda e, n2=n2, kt=kt: e.matmul(
                        ps_m[:, :], hTt[kt][:, :], vwt[kt][:, n2 * 512:(n2 + 1) * 512],
                        start=(kt == 0), stop=(kt == KD - 1)),
                        ins=[hTt[kt], vwt[kt]], outs=[ps_m])
                P.op("v", lambda e, n2=n2, tt=tt: e.tensor_copy(
                    v_sb[tt][:, n2 * 512:(n2 + 1) * 512], ps_m[:, :]),
                    ins=[ps_m], outs=[v_sb[tt]])
        P.barrier()
        bst.close()
        qt_sb = [sb(P, st, f"qt{i}", [PT, PT], BF16) for i in range(KD)]
        mc = sb(P, st, "mc", [PT, 12], F32)
        es = sb(P, st, "es", [PT, HD], F32)
        dinv = sb(P, st, "dinv", [PT, 16], F32)
        dinv_bf = sb(P, st, "dinv_bf", [PT, 16], BF16)
        dinvT = sb(P, st, "dinvT", [16, T], BF16)
        ps_s = ps(P, st, "ps_s", [PT, 512], F32)
        ps_tr = ps(P, st, "ps_tr", [16, PT], BF16)
        for qt in range(NTT):
            for dp in range(KD):
                P.dma("y", qt_sb[dp][:, :], sc_q[dp * PT:(dp + 1) * PT,
                                                 qt * PT:(qt + 1) * PT], outs=[qt_sb[dp]])
            for h in range(NH):
                dp, b = h // 2, (h % 2) * HD
                for kc in range(8):
                    P.op("t", lambda e, dp=dp, b=b, kc=kc: e.matmul(
                        ps_s[:, :], qt_sb[dp][b:b + HD, :],
                        kT[dp][b:b + HD, kc * 512:(kc + 1) * 512],
                        start=True, stop=True), ins=[qt_sb[dp], kT[dp]], outs=[ps_s])
                    if kc < 7:
                        P.op("v", lambda e, kc=kc: e.reduce_max(
                            mc[:, kc:kc + 1], ps_s[:, :], axis=AX.X), ins=[ps_s], outs=[mc])
                    else:
                        P.op("v", lambda e: e.reduce_max(
                            mc[:, 7:8], ps_s[:, 0:448], axis=AX.X), ins=[ps_s], outs=[mc])
                        P.op("v", lambda e: e.reduce_max(
                            mc[:, 8:9], ps_s[:, 448:512], axis=AX.X), ins=[ps_s], outs=[mc])
                P.op("v", lambda e: e.reduce_max(mc[:, 9:10], mc[:, 0:8], axis=AX.X),
                     ins=[mc], outs=[mc])
                P.op("v", lambda e: e.tensor_tensor(mc[:, 10:11], mc[:, 9:10], mc[:, 8:9],
                                                    ALU.max), ins=[mc], outs=[mc])
                P.op("v", lambda e: e.tensor_scalar_mul(mc[:, 11:12], mc[:, 10:11], -1.0),
                     ins=[mc], outs=[mc])
                P.op("s", lambda e: e.activation(es[:, :], ps_s[:, 448:512], AF.Exp,
                                                 bias=mc[:, 11:12]),
                     ins=[ps_s, mc], outs=[es])
                P.op("v", lambda e: e.reduce_sum(mc[:, 11:12], es[:, :], axis=AX.X),
                     ins=[es], outs=[mc])
                P.op("s", lambda e: e.activation(mc[:, 9:10], mc[:, 9:10], AF.Exp),
                     ins=[mc], outs=[mc])
                P.op("s", lambda e: e.activation(mc[:, 10:11], mc[:, 10:11], AF.Exp),
                     ins=[mc], outs=[mc])
                P.op("v", lambda e: e.tensor_tensor(mc[:, 9:10], mc[:, 9:10], mc[:, 10:11],
                                                    ALU.add), ins=[mc], outs=[mc])
                P.op("v", lambda e: e.tensor_tensor(mc[:, 9:10], mc[:, 9:10], mc[:, 11:12],
                                                    ALU.mult), ins=[mc], outs=[mc])
                P.op("v", lambda e, h=h: e.reciprocal(dinv[:, h:h + 1], mc[:, 9:10]),
                     ins=[mc], outs=[dinv])
            P.op("v", lambda e: e.tensor_copy(dinv_bf[:, :], dinv[:, :]),
                 ins=[dinv], outs=[dinv_bf])
            P.op("t", lambda e: e.transpose(ps_tr[:, :], dinv_bf[:, :], cm.ident[:, :]),
                 ins=[dinv_bf, cm.ident], outs=[ps_tr])
            P.op("v", lambda e, qt=qt: e.tensor_copy(dinvT[:, qt * PT:(qt + 1) * PT],
                                                     ps_tr[:, :]), ins=[ps_tr], outs=[dinvT])
        qc_sb = [sb(P, st, f"qc{i}", [PT, 512], BF16) for i in range(KD)]
        slabA = [sb(P, st, f"slab{i}", [PT, 512], BF16) for i in range(2)]
        ytf = sb(P, st, "ytf", [HD, 512], F32)
        yt_sb = [sb(P, st, f"yts{i}", [PT, 512], BF16) for i in range(KD)]
        x_res = sb(P, st, "x_res", [PT, D], F32)
        o_f = sb(P, st, "o_f", [PT, D], F32)
        ps_yT = ps(P, st, "ps_yT", [HD, 512], F32)
        ps_db = ps(P, st, "ps_db", [HD, 512], F32)
        ps_o = ps(P, st, "ps_o", [PT, 512], F32)
        ps_sA2 = [ps_s, ps_m]
        owt = load_tiles(P, st, "owt", owt_d, D, D, BF16)
        ps_b2 = ps(P, st, "ps_b2", [PT, 512], F32)
        ob_bc = bcast_row(P, st, "ob_bc", ob_d.rearrange("(a d) -> a d", a=1), D,
                          cm.ones_f, ps_b2)
        for qc in range(NCH):
            c0 = qc * 512
            for dp in range(KD):
                P.dma("y", qc_sb[dp][:, :], sc_q[dp * PT:(dp + 1) * PT, c0:c0 + 512],
                      outs=[qc_sb[dp]])
            for h in range(NH):
                dp, b = h // 2, (h % 2) * HD
                for kt in range(NTT):
                    pss = ps_sA2[kt % 2]
                    slab = slabA[kt % 2]
                    P.op("t", lambda e, dp=dp, b=b, kt=kt, pss=pss: e.matmul(
                        pss[:, :], kT[dp][b:b + HD, kt * PT:(kt + 1) * PT],
                        qc_sb[dp][b:b + HD, :], start=True, stop=True),
                        ins=[kT[dp], qc_sb[dp]], outs=[pss])
                    P.op("s", lambda e, pss=pss, slab=slab: e.activation(
                        slab[:, :], pss[:, :], AF.Exp), ins=[pss], outs=[slab])
                    P.op("t", lambda e, h=h, kt=kt, slab=slab: e.matmul(
                        ps_yT[:, :], v_sb[kt][:, h * HD:(h + 1) * HD], slab[:, :],
                        start=(kt == 0), stop=(kt == NTT - 1)),
                        ins=[v_sb[kt], slab], outs=[ps_yT])
                P.op("t", lambda e, h=h, c0=c0: e.matmul(
                    ps_db[:, :], hsel[:, h * HD:(h + 1) * HD], dinvT[:, c0:c0 + 512],
                    start=True, stop=True), ins=[hsel, dinvT], outs=[ps_db])
                P.op("s", lambda e: e.copy(ytf[:, :], ps_yT[:, :]), ins=[ps_yT], outs=[ytf])
                P.op("v", lambda e, dp=dp, b=b: e.tensor_tensor(
                    yt_sb[dp][b:b + HD, :], ytf[:, :], ps_db[:, :], ALU.mult),
                    ins=[ytf, ps_db], outs=[yt_sb[dp]])
            for qt in range(4):
                row = c0 + qt * PT
                P.dma("y", x_res[:, :], x_d[row:row + PT, :], outs=[x_res])
                for n2 in range(2):
                    for kt in range(KD):
                        P.op("t", lambda e, qt=qt, n2=n2, kt=kt: e.matmul(
                            ps_o[:, :], yt_sb[kt][:, qt * PT:(qt + 1) * PT],
                            owt[kt][:, n2 * 512:(n2 + 1) * 512],
                            start=(kt == 0), stop=(kt == KD - 1)),
                            ins=[yt_sb[kt], owt[kt]], outs=[ps_o])
                    P.op("v", lambda e, n2=n2: e.tensor_tensor(
                        o_f[:, n2 * 512:(n2 + 1) * 512], ps_o[:, :],
                        ob_bc[:, n2 * 512:(n2 + 1) * 512], ALU.add),
                        ins=[ps_o, ob_bc], outs=[o_f])
                P.op("v", lambda e: e.tensor_add(o_f[:, :], o_f[:, :], x_res[:, :]),
                     ins=[o_f, x_res], outs=[o_f])
                P.dma("g", xo_d[row:row + PT, :], o_f[:, :], ins=[o_f])
        if standalone:
            P.fence_outputs()
            P.emit()
    return nc


def l1ab_inputs(inp, pr, cst, x):
    return {"x": x, "c_ident": cst["c_ident"], "c_hsel": cst["c_hsel"],
            "t_qwt2": pr["t_qwt2"], "t_kwt2": pr["t_kwt2"], "t_vwt2": pr["t_vwt2"],
            "t_owt2": pr["t_owt2"], "out_b": np.asarray(inp["L1_attn_b_out_b"])}
